# version 16
# baseline (speedup 1.0000x reference)
import numpy as np
import ml_dtypes
import concourse.bass as bass
import concourse.mybir as mybir
from concourse.bass_utils import run_bass_kernel_spmd

F32 = mybir.dt.float32
BF16 = mybir.dt.bfloat16
AF = mybir.ActivationFunctionType
ALU = mybir.AluOpType
AX = mybir.AxisListType

D = 2048
T = 2048
H = 16
INNER = 8192
ND = 16
NT = 16
NI = 64
MAGIC = float(np.float32(1.5 * 2 ** 23))
RSQ128 = float(1.0 / np.sqrt(np.float32(128.0)))
EPS = 1e-5
CLIP_W = 1.4995117

DEBUG_OUTS = set()
SERIALIZE_DMA = False
PHASE_LIMIT = None
P0_MIN = False
SERIALIZE_ALL = False
NO_ELIDE = {"vector", "gpsimd"}

COMPUTE_ENGS = {"sc", "ve", "te", "gc"}
STREAM_OF = {"sy": "sync", "sc": "scalar", "sd": "scalar",
             "ve": "vector", "te": "tensor", "gc": "gpsimd", "gd": "gpsimd"}


class Rec:
    def __init__(self):
        self.ops = []

    def op(self, eng, fn, deps=(), force=False, is_async=False, sem=None):
        i = len(self.ops)
        if SERIALIZE_ALL and i > 0:
            deps = list(deps) + [i - 1]
        if SERIALIZE_DMA and eng in ("sy", "sd", "gd"):
            prev = getattr(self, "_prev_dma_" + eng, None)
            if prev is not None:
                deps = list(deps) + [prev]
            setattr(self, "_prev_dma_" + eng, i)
        dd = []
        for d in deps:
            if d is None:
                continue
            if isinstance(d, (list, tuple)):
                dd += [x for x in d if x is not None]
            else:
                dd.append(d)
        self.ops.append({"eng": eng, "fn": fn, "deps": dd, "force": force,
                         "async": is_async, "sem": sem or eng,
                         "inc": 16 if eng in ("sy", "sd", "gd") else 1,
                         "needed": False})
        return i

    def emit(self, nc, block, stack):
        ops = self.ops
        for o in ops:
            if o["eng"] in ("sy", "sd", "gd"):
                o["needed"] = True
            if SERIALIZE_ALL:
                o["needed"] = True
            for d in o["deps"]:
                do = ops[d]
                if (not SERIALIZE_ALL and not o["force"] and not do["async"]
                        and STREAM_OF[do["eng"]] not in NO_ELIDE
                        and do["eng"] in COMPUTE_ENGS and o["eng"] in COMPUTE_ENGS
                        and STREAM_OF[do["eng"]] == STREAM_OF[o["eng"]]):
                    continue
                do["needed"] = True
        counts = {}
        for o in ops:
            if o["needed"]:
                counts[o["sem"]] = counts.get(o["sem"], 0) + o["inc"]
                o["val"] = counts[o["sem"]]
            else:
                o["val"] = None
        sems = {e: stack.enter_context(nc.semaphore(f"sem_{e}")) for e in counts}

        def emitter(stream_name):
            def inner(engobj):
                hw = {}
                for o in ops:
                    if STREAM_OF[o["eng"]] != stream_name:
                        continue
                    for d in o["deps"]:
                        do = ops[d]
                        if (not SERIALIZE_ALL and not o["force"] and not do["async"]
                                and stream_name not in NO_ELIDE
                                and do["eng"] in COMPUTE_ENGS and o["eng"] in COMPUTE_ENGS
                                and STREAM_OF[do["eng"]] == stream_name):
                            continue
                        v = do["val"]
                        if hw.get(do["sem"], 0) >= v:
                            continue
                        engobj.wait_ge(sems[do["sem"]], v)
                        hw[do["sem"]] = v
                    inst = o["fn"](engobj)
                    if o["needed"]:
                        inst.then_inc(sems[o["sem"]], o["inc"])
            return inner

        for s in sorted({STREAM_OF[o["eng"]] for o in ops}):
            getattr(block, s)(emitter(s))
        return counts


class Guard:
    def __init__(self, rec, seed=()):
        self.rec = rec
        self.writer = []
        self.readers = {}
        for t in seed:
            if t is not None:
                self.readers[("seed", len(self.readers))] = t

    def rd(self):
        return list(self.writer)

    def wr(self):
        return list(self.writer) + list(self.readers.values())

    def did_read(self, tok):
        if tok is not None:
            self.readers[self.rec.ops[tok]["eng"]] = tok

    def did_write(self, tok):
        self.writer = [tok]
        self.readers = {}


def build_program():
    nc = bass.Bass("TRN2", target_bir_lowering=False)
    r = Rec()
    G = lambda *seed: Guard(r, seed)

    din = {}
    for name, shape in [
        ("x", [T, D]),
        ("cq", [D, D]), ("sq", [D, D]), ("ck", [D, D]), ("sk", [D, D]),
        ("cv", [D, D]), ("sv", [D, D]), ("co", [D, D]), ("so", [D, D]),
        ("bq", [D]), ("bk", [D]), ("bv", [D]), ("bo", [D]),
        ("w1t", [D, INNER]), ("b1", [INNER]), ("g_ln", [INNER]), ("bt_ln", [INNER]),
        ("w2t", [INNER, D]), ("b2", [D]),
    ]:
        din[name] = nc.dram_tensor(name, shape, F32, kind="ExternalInput")
    for name, shape in [("ident", [128, 128]), ("ones_col", [128, 1]),
                        ("ones2", [2, 128])]:
        din[name] = nc.dram_tensor(name, shape, BF16, kind="ExternalInput")
    din["onesr_f"] = nc.dram_tensor("onesr_f", [1, 128], F32, kind="ExternalInput")
    din["ones_col_f"] = nc.dram_tensor("ones_col_f", [128, 1], F32, kind="ExternalInput")
    din["magicv"] = nc.dram_tensor("magicv", [128, 1], F32, kind="ExternalInput")
    din["ln127v"] = nc.dram_tensor("ln127v", [128, 1], F32, kind="ExternalInput")
    din["nmagicv"] = nc.dram_tensor("nmagicv", [128, 1], F32, kind="ExternalInput")

    out_d = nc.dram_tensor("out", [T, D], F32, kind="ExternalOutput")

    def scr(name, shape, dtype):
        kind = "ExternalOutput" if name in DEBUG_OUTS else "Internal"
        return nc.dram_tensor(name, shape, dtype, kind=kind)

    attnh_d = scr("attnh_d", [H, 128, T], BF16)
    aproj_d = scr("aproj_d", [NT, 128, D], BF16)
    s_d = scr("s_d", [NT, 128, INNER], F32)
    xq2_d = scr("xq2_d", [NT, 128, INNER], BF16)
    xq1_d = scr("xq1_d", [NT, 128, D], BF16)
    xnfm_dbg = scr("xnfm_dbg", [ND, 128, T], BF16) if "xnfm_dbg" in DEBUG_OUTS else None
    p0st_dbg = scr("p0st_dbg", [NT, 128, 32], F32) if "p0st_dbg" in DEBUG_OUTS else None
    p0bn_dbg = scr("p0bn_dbg", [NT, 128, 24], F32) if "p0bn_dbg" in DEBUG_OUTS else None
    wq_dbg = scr("wq_dbg", [128, 8], F32) if "wq_dbg" in DEBUG_OUTS else None
    mid_dbg = scr("mid_dbg", [NT, 128, 16], F32) if "mid_dbg" in DEBUG_OUTS else None

    AR_A = nc.alloc_sbuf_tensor("AR_A", [128, 32768], BF16)
    AR_B = nc.alloc_sbuf_tensor("AR_B", [128, 32768], BF16)
    AR_C = nc.alloc_sbuf_tensor("AR_C", [128, 32768], BF16)
    Bf = AR_B[:].bitcast(F32)
    Cf = AR_C[:].bitcast(F32)
    Af = AR_A[:].bitcast(F32)
    ident = nc.alloc_sbuf_tensor("identt", [128, 128], BF16)
    ones_col = nc.alloc_sbuf_tensor("ones_colt", [128, 1], BF16)
    ones_col_f = nc.alloc_sbuf_tensor("ones_col_ft", [128, 1], F32)
    ones2 = nc.alloc_sbuf_tensor("ones2t", [2, 128], BF16)
    onesr_f = nc.alloc_sbuf_tensor("onesr_ft", [1, 128], F32)
    bqkv = nc.alloc_sbuf_tensor("bqkv", [128, 48], F32)
    stp = nc.alloc_sbuf_tensor("stp", [128, 192], F32)
    wacc = nc.alloc_sbuf_tensor("wacc", [128, 4], F32)
    rowbuf = nc.alloc_sbuf_tensor("rowbuf", [1, 1024], F32)
    cmagic = nc.alloc_sbuf_tensor("cmagic", [128, 1], F32)
    nc.const_aps.register(F32, MAGIC, cmagic[:])
    cln127 = nc.alloc_sbuf_tensor("cln127", [128, 1], F32)
    nc.const_aps.register(F32, 4.844187086458591, cln127[:])
    cnmagic = nc.alloc_sbuf_tensor("cnmagic", [128, 1], F32)
    nc.const_aps.register(F32, -MAGIC, cnmagic[:])
    hilo_bo = nc.alloc_sbuf_tensor("hilo_bo", [2, D], BF16)
    invq1 = nc.alloc_sbuf_tensor("invq1", [128, 16], F32)
    invq2 = nc.alloc_sbuf_tensor("invq2", [128, 16], F32)
    scv1 = nc.alloc_sbuf_tensor("scv1", [128, 16], F32)
    scv2 = nc.alloc_sbuf_tensor("scv2", [128, 16], F32)

    ps0 = nc.alloc_psum_tensor("ps0", [128, 2048], F32)
    ps1 = nc.alloc_psum_tensor("ps1", [128, 2048], F32)

    pbk = [G() for _ in range(8)]

    def pg(t, lo, hi):
        base = 0 if t is ps0 else 4
        return [pbk[base + b] for b in range(lo // 512, (hi + 511) // 512)]

    def pgw(t, lo, hi):
        return [bb for b in pg(t, lo, hi) for bb in b.wr()]

    W_S1, W_C1, W_NC1, W_I1, W_S2, W_C2, W_NC2, W_I2, W_M1, W_M2 = range(32, 42)
    BIAS_H = 42

    def col(i):
        return stp[:, i:i + 1]

    def dma(q, dst, src, deps=(), lane=None, **kw):
        return r.op(q, lambda e, d=dst, s=src, kw=kw: e.dma_start(out=d, in_=s, **kw),
                    deps, sem=lane)

    def ts(out, in0, s1_, s2_, op0, op1, deps=(), eng="ve"):
        return r.op(eng, lambda e, o=out, i=in0, a=s1_, b=s2_, o0=op0, o1=op1:
                    e.tensor_scalar(o, i, a, b, o0, o1), deps)

    def tt(out, in0, in1, aop, deps=(), eng="ve"):
        return r.op(eng, lambda e, o=out, a=in0, b=in1, p=aop: e.tensor_tensor(o, a, b, p), deps)

    def act(out, in_, func, bias=0.0, scale=1.0, deps=()):
        return r.op("sc", lambda e, o=out, i=in_, f=func, b=bias, s=scale:
                    e.activation(o, i, f, bias=b, scale=s), deps)

    def mm(out, lhsT, rhs, start, stop, deps=()):
        return r.op("te", lambda e, o=out, l=lhsT, rr_=rhs, st=start, sp=stop:
                    e.matmul(o, l, rr_, start=st, stop=sp), deps)

    def tp(out, in_, deps=()):
        return r.op("te", lambda e, o=out, i=in_: e.transpose(o, i, ident[:]), deps)

    def red(out, in_, aop, deps=(), absv=None, eng="ve"):
        return r.op(eng, lambda e, o=out, i=in_, p=aop, a=absv:
                    e.tensor_reduce(o, i, AX.X, p, apply_absolute_value=a), deps,
                    is_async=True)

    tok_c = {}
    last_c = None
    for nm, dst in [("ident", ident), ("ones_col", ones_col), ("ones2", ones2),
                    ("onesr_f", onesr_f), ("ones_col_f", ones_col_f),
                    ("magicv", cmagic), ("ln127v", cln127), ("nmagicv", cnmagic)]:
        last_c = dma("sy", dst[:], din[nm][:], lane="Lcst")
    for j, nm in enumerate(("bq", "bk", "bv")):
        last_c = dma("sy", bqkv[:, j * 16:(j + 1) * 16],
                     din[nm][:].rearrange("(c p) -> p c", p=128),
                     allow_slow_non_contiguous=True, lane="Lcst")
    for nm in ("ident", "ones_col", "ones2", "onesr_f", "ones_col_f",
               "magicv", "ln127v"):
        tok_c[nm] = last_c
    t_magic = last_c
    tok_bqkv = [last_c]

    def make_hilo(dst, src_dram, n, row_ap, lo_ap, deps):
        t0 = dma("sy", row_ap, src_dram[:].rearrange("(a n) -> a n", a=1), deps=deps,
                 lane="Lhl")
        t1 = r.op("ve", lambda e, d=dst, rw=row_ap: e.tensor_copy(d[0:1, :], rw), [t0])
        t2 = tt(lo_ap, row_ap, dst[0:1, :], ALU.subtract, [t1])
        t3 = dma("sy", dst[1:2, :], lo_ap, deps=[t2], lane="Lhl")
        return t3

    xdv = din["x"]
    xn_t = [AR_C[:, 0:2048], AR_C[:, 2048:4096]]
    xq1_t = [AR_C[:, 4096:6144], AR_C[:, 6144:8192]]
    x_stage = [Cf[:, 4096:6144], Cf[:, 6144:8192]]
    tmp_rd = Cf[:, 8192:10240]
    xq1_sp = [AR_B[:, 0:2048], AR_B[:, 2048:4096]]
    xnf32 = Cf[:, 10240:12288]
    ps_tpA = ps0[:, 0:1024].bitcast(BF16)
    ps_tpB = ps0[:, 1024:2048].bitcast(BF16)

    g_x = [G(), G()]
    g_xn = [G(), G()]
    g_xq1t = [G(), G()]
    g_xq1sp = [G(), G()]
    g_tmp = G()
    g_xnf = G()
    ev_a_toks = []
    g_invq1 = G()
    g_invq2 = G()
    g_A = G()
    g_stats = [G(), G()]
    xq1_store = {}

    MV, _VV, MX, MN, RLN, BXN, T0, T1c, T2c, CQ, INVQ, BR2 = range(12)

    def p0_col(i, par):
        return stp[:, par * 16 + i:par * 16 + i + 1]

    for tti in range(NT):
        par = tti % 2
        bnS = 96 if par == 0 else 56
        t_x = dma("sy", x_stage[par], xdv[tti * 128:(tti + 1) * 128, :],
                  deps=g_x[par].wr(), lane="Lx%d" % par)
        g_x[par].did_write(t_x)
        t_bn = []
        for c in range(4):
            t_bn.append(r.op("ve", lambda e, c=c, par=par, bnS=bnS: e.bn_stats(
                stp[:, bnS + c * 6:bnS + c * 6 + 6],
                x_stage[par][:, c * 512:(c + 1) * 512]),
                [t_x] + g_stats[par].wr(), is_async=True))
            g_x[par].did_read(t_bn[-1])
        t_ag = r.op("ve", lambda e, par=par, bnS=bnS: e.bn_aggr(
            stp[:, par * 16 + MV:par * 16 + MV + 2], stp[:, bnS:bnS + 24]), t_bn,
            force=True, is_async=True)
        t_mx = red(p0_col(MX, par), x_stage[par], ALU.max, [t_x] + g_stats[par].wr())
        t_mn = red(p0_col(MN, par), x_stage[par], ALU.min, [t_x] + g_stats[par].wr())
        g_x[par].did_read(t_mx)
        g_x[par].did_read(t_mn)
        t_u = ts(p0_col(T0, par), p0_col(MV + 1, par), 1.0, EPS, ALU.mult, ALU.add, [t_ag])
        t_ln = act(p0_col(T0, par), p0_col(T0, par), AF.Ln, deps=[t_u])
        t_rl = act(p0_col(RLN, par), p0_col(T0, par), AF.Exp, scale=-0.5, deps=[t_ln])
        t_bx = ts(p0_col(BXN, par), p0_col(MV, par), p0_col(RLN, par), -1.0,
                  ALU.mult, ALU.mult, [t_rl])
        t_xnf = r.op("sc", lambda e, par=par: e.activation(
            xnf32, x_stage[par], AF.Identity,
            bias=p0_col(BXN, par), scale=p0_col(RLN, par)),
            [t_bx, t_x] + g_xnf.wr())
        g_xnf.did_write(t_xnf)
        g_x[par].did_read(t_xnf)
        t_xn = r.op("ve", lambda e, par=par: e.tensor_copy(xn_t[par], xnf32),
                    [t_xnf] + g_xn[par].wr())
        g_xnf.did_read(t_xn)
        g_xn[par].did_write(t_xn)
        if P0_MIN:
            g_stats[par].did_write(t_xn)
            tps_a = []
            for j in range(16):
                tps_a.append(tp(ps_tpA[:, j * 128:(j + 1) * 128],
                                xn_t[par][:, j * 128:(j + 1) * 128],
                                [t_xn, tok_c["ident"]] +
                                (pgw(ps0, 0, 1024) if j == 0 else [])))
                g_xn[par].did_read(tps_a[-1])
            for b in pg(ps0, 0, 1024):
                b.did_write(tps_a[-1])
            ev_a = r.op("ve", lambda e, tti=tti: e.tensor_copy(
                AR_A[:].rearrange("p (dc t) -> p dc t", t=T)[:, :, tti * 128:tti * 128 + 128],
                ps_tpA[:].rearrange("p (dc j) -> p dc j", j=128)),
                tps_a + g_A.wr())
            for b in pg(ps0, 0, 1024):
                b.did_read(ev_a)
            g_A.did_write(ev_a)
            xq1_store[tti] = t_xn
            continue
        t_a = tt(p0_col(T1c, par), p0_col(RLN, par), p0_col(RLN, par), ALU.mult, [t_rl])
        t_b = tt(p0_col(T1c, par), p0_col(MV + 1, par), p0_col(T1c, par), ALU.mult, [t_a])
        t_c = ts(p0_col(T1c, par), p0_col(T1c, par), 1.0, EPS, ALU.mult, ALU.add, [t_b])
        t_d = act(p0_col(T1c, par), p0_col(T1c, par), AF.Ln, deps=[t_c])
        t_r2 = act(p0_col(BR2, par), p0_col(T1c, par), AF.Exp, scale=-0.5, deps=[t_d])
        t_e = tt(p0_col(T0, par), p0_col(MX, par), p0_col(MV, par), ALU.subtract,
                 [t_mx, t_ag, t_rl])
        t_f = tt(p0_col(T2c, par), p0_col(MV, par), p0_col(MN, par), ALU.subtract,
                 [t_mn, t_ag])
        t_g = tt(p0_col(T0, par), p0_col(T0, par), p0_col(T2c, par), ALU.max, [t_e, t_f])
        t_h = tt(p0_col(T0, par), p0_col(T0, par), p0_col(RLN, par), ALU.mult, [t_g])
        t_i = tt(p0_col(T0, par), p0_col(T0, par), p0_col(BR2, par), ALU.mult,
                 [t_h, t_r2])
        t_j = ts(p0_col(T0, par), p0_col(T0, par), 1e-5, None, ALU.max, ALU.bypass, [t_i])
        t_k = act(p0_col(T1c, par), p0_col(T0, par), AF.Ln, deps=[t_j, t_r2])
        t_l = act(p0_col(T1c, par), p0_col(T1c, par), AF.Exp, scale=-1.0,
                  bias=4.844187086458591, deps=[t_k])
        t_m = tt(p0_col(CQ, par), p0_col(BR2, par), p0_col(T1c, par), ALU.mult, [t_l])
        t_n = ts(p0_col(INVQ, par), p0_col(T0, par), 1.0 / 127.0, None, ALU.mult,
                 ALU.bypass, [t_j])
        t_p1 = r.op("sc", lambda e, par=par: e.activation(
            tmp_rd, xnf32, AF.Identity, bias=MAGIC, scale=p0_col(CQ, par)),
            [t_m, t_xnf, t_magic] + g_tmp.wr())
        g_tmp.did_write(t_p1)
        g_xnf.did_read(t_p1)
        t_p2 = ts(xq1_t[par], tmp_rd, -MAGIC, None, ALU.add, ALU.bypass,
                  [t_p1] + g_xq1t[par].wr())
        g_tmp.did_read(t_p2)
        g_xq1t[par].did_write(t_p2)
        t_iv = ts(invq1[:, tti:tti + 1], p0_col(INVQ, par), 1.0, None,
                  ALU.mult, ALU.bypass, [t_n] + g_invq1.wr())
        g_invq1.writer.append(t_iv)
        g_stats[par].did_write(t_p2)
        g_stats[par].readers["iv"] = t_iv
        if p0st_dbg is not None:
            t_dbg = dma("sd", p0st_dbg[tti][:, 0:16], stp[:, par * 16:par * 16 + 16],
                        deps=[t_p2, t_n, t_m, t_bx, t_xn], lane="Ldbg")
            g_stats[par].readers[("dbg", tti)] = t_dbg
        tps_a = []
        for j in range(16):
            tps_a.append(tp(ps_tpA[:, j * 128:(j + 1) * 128],
                            xn_t[par][:, j * 128:(j + 1) * 128],
                            [t_xn, tok_c["ident"]] +
                            (pgw(ps0, 0, 1024) if j == 0 else [])))
            g_xn[par].did_read(tps_a[-1])
        for b in pg(ps0, 0, 1024):
            b.did_write(tps_a[-1])
        ev_a = r.op("ve", lambda e, tti=tti: e.tensor_copy(
            AR_A[:].rearrange("p (dc t) -> p dc t", t=T)[:, :, tti * 128:tti * 128 + 128],
            ps_tpA[:].rearrange("p (dc j) -> p dc j", j=128)),
            tps_a + g_A.wr(), is_async=True)
        for b in pg(ps0, 0, 1024):
            b.did_read(ev_a)
        g_A.writer.append(ev_a)
        ev_a_toks.append(ev_a)
        tps_b = []
        for j in range(16):
            tps_b.append(tp(ps_tpB[:, j * 128:(j + 1) * 128],
                            xq1_t[par][:, j * 128:(j + 1) * 128],
                            [t_p2] + (pgw(ps0, 1024, 2048) if j == 0 else [])))
            g_xq1t[par].did_read(tps_b[-1])
        for b in pg(ps0, 1024, 2048):
            b.did_write(tps_b[-1])
        ev_b = r.op("ve", lambda e, par=par: e.tensor_copy(
            xq1_sp[par], ps_tpB[:]), tps_b + g_xq1sp[par].wr(), is_async=True)
        for b in pg(ps0, 1024, 2048):
            b.did_read(ev_b)
        g_xq1sp[par].did_write(ev_b)
        t_spill = dma("sd", xq1_d[tti], xq1_sp[par], deps=[ev_b], lane="Lq1%d" % par)
        g_xq1sp[par].did_read(t_spill)
        xq1_store[tti] = t_spill

    if xnfm_dbg is not None:
        for dc in range(ND):
            g_A.did_read(dma("sd", xnfm_dbg[dc],
                             AR_A[:, dc * T:(dc + 1) * T], deps=g_A.rd(),
                             lane="Ldbg"))

    if PHASE_LIMIT == "p0":
        r.op("ve", lambda e: e.engine_nop(),
             [xq1_store[NT - 1]] + g_A.rd())
        import contextlib
        stack = contextlib.ExitStack()
        with nc.Block() as block:
            r.emit(nc, block, stack)
        return nc, stack

    mean_stage = Bf[:, 12288:14336]
    abs_stage = Bf[:, 14336:16384]
    CWM = 2048
    g_mst = G()
    g_mabs = G()
    t_z0 = r.op("gc", lambda e: e.memset(wacc[:], 0.0), [])
    acc_last = {0: t_z0, 1: t_z0}
    mhat_tok = {}
    nmhat_tok = {}
    mean_jobs = []
    for wi, wname in enumerate(("w1t", "w2t")):
        wd = din[wname]
        nch = wd.shape[1] // CWM
        nrb = wd.shape[0] // 128
        mcol = W_M1 if wi == 0 else W_M2
        for dc2 in range(nrb):
            for hh in range(nch):
              def _mean_job(wi=wi, wd=wd, dc2=dc2, hh=hh, mcol=mcol):
                  t_ld = dma("sy", mean_stage,
                             wd[dc2 * 128:(dc2 + 1) * 128, hh * CWM:(hh + 1) * CWM],
                             deps=g_mst.wr(), lane="Lms")
                  g_mst.did_write(t_ld)
                  t_abs = r.op("sc", lambda e: e.activation(
                      abs_stage, mean_stage, AF.Abs),
                      [t_ld] + g_mabs.wr())
                  g_mst.did_read(t_abs)
                  g_mabs.did_write(t_abs)
                  if dc2 == 0 and hh == 0:
                      t_r0 = red(wacc[:, 2 + wi:3 + wi], abs_stage, ALU.add, [t_abs])
                      g_mabs.did_read(t_r0)
                      t_m0 = mm(ps1[0:1, 0:1], ones_col_f[:], wacc[:, 2 + wi:3 + wi],
                                True, True,
                                [t_r0, tok_c["ones_col_f"]] + pgw(ps1, 0, 512))
                      for b in pg(ps1, 0, 512):
                          b.did_write(t_m0)
                      t_m1 = r.op("ve", lambda e: e.tensor_scalar(
                          rowbuf[0:1, 0:1], ps1[0:1, 0:1], 1.0 / (128 * CWM), None,
                          ALU.mult, ALU.bypass), [t_m0])
                      for b in pg(ps1, 0, 512):
                          b.did_read(t_m1)
                      t_m2 = mm(ps1[:, 0:1], onesr_f[:], rowbuf[0:1, 0:1], True, True,
                                [t_m1, tok_c["onesr_f"]] + pgw(ps1, 0, 512))
                      for b in pg(ps1, 0, 512):
                          b.did_write(t_m2)
                      t_mh = r.op("ve", lambda e, mcol=mcol: e.tensor_copy(
                          col(mcol), ps1[:, 0:1]), [t_m2], is_async=True)
                      for b in pg(ps1, 0, 512):
                          b.did_read(t_mh)
                      mhat_tok[wi] = t_mh
                      nmhat_tok[wi] = ts(col(mcol + 6), col(mcol), -1.0, None,
                                         ALU.mult, ALU.bypass, [t_mh])
                  t_rd = r.op("sc", lambda e, mcol=mcol, wi=wi: e.activation(
                      abs_stage, abs_stage, AF.Identity, bias=col(mcol + 6),
                      accum_out=wacc[:, 2 + wi:3 + wi]),
                      [t_abs, nmhat_tok[wi], acc_last[wi]])
                  g_mabs.did_read(t_rd)
                  t_ac = r.op("ve", lambda e, wi=wi: e.tensor_tensor(
                      wacc[:, wi:wi + 1], wacc[:, wi:wi + 1], wacc[:, 2 + wi:3 + wi],
                      ALU.add), [t_rd, acc_last[wi]])
                  acc_last[wi] = t_ac
              mean_jobs.append(_mean_job)

    def finish_wmean(wi, s_c, c_c, nc_c, i_c):
        mcol = W_M1 if wi == 0 else W_M2
        deps = [acc_last[wi]] + pgw(ps1, 0, 512)
        t_mm = mm(ps1[0:1, 0:1], ones_col_f[:], wacc[:, wi:wi + 1], True, True,
                  deps + [tok_c["ones_col_f"]])
        for b in pg(ps1, 0, 512):
            b.did_write(t_mm)
        t_c0 = r.op("ve", lambda e: e.tensor_scalar(
            rowbuf[0:1, 0:1], ps1[0:1, 0:1], 1.0 / (D * INNER), None,
            ALU.mult, ALU.bypass), [t_mm])
        for b in pg(ps1, 0, 512):
            b.did_read(t_c0)
        t_c1 = r.op("ve", lambda e, mcol=mcol: e.tensor_scalar(
            rowbuf[0:1, 0:1], rowbuf[0:1, 0:1], col(mcol)[0:1, :], 1e-5,
            ALU.add, ALU.max), [t_c0, mhat_tok[wi]])
        t_sp = mm(ps1[:, 0:1], onesr_f[:], rowbuf[0:1, 0:1], True, True,
                  [t_c1, tok_c["onesr_f"]] + pgw(ps1, 0, 512))
        for b in pg(ps1, 0, 512):
            b.did_write(t_sp)
        t_mc = r.op("ve", lambda e: e.tensor_copy(col(i_c), ps1[:, 0:1]), [t_sp],
                    is_async=True)
        for b in pg(ps1, 0, 512):
            b.did_read(t_mc)
        t_s0 = act(col(s_c), col(i_c), AF.Ln, deps=[t_mc])
        t_s1 = act(col(s_c), col(s_c), AF.Exp, scale=-1.0, deps=[t_s0])
        t_u = tt(stp[:, 44 + wi:45 + wi], col(i_c), col(s_c), ALU.mult, [t_s1, t_mc])
        t_v = ts(stp[:, 44 + wi:45 + wi], stp[:, 44 + wi:45 + wi], -1.0, 2.0,
                 ALU.mult, ALU.add, [t_u])
        t_s = tt(col(s_c), col(s_c), stp[:, 44 + wi:45 + wi], ALU.mult, [t_v])
        t_cc = ts(col(c_c), col(i_c), CLIP_W, None, ALU.mult, ALU.bypass, [t_mc])
        t_nc = ts(col(nc_c), col(c_c), -1.0, None, ALU.mult, ALU.bypass, [t_cc])
        return [t_s, t_cc, t_nc, t_mc]

    qh_b = AR_B[:, 4096:6144]
    kh_b = AR_B[:, 6144:8192]
    vh_b = AR_B[:, 8192:10240]
    vtm_b = AR_B[:, 10240:12288]
    sqr_b = AR_B[:, 12288:14336]
    ah_b = AR_B[:, 14336:16384]
    pt_b = [AR_B[:, 16384:17408], AR_B[:, 17408:18432]]
    nrm_pool = AR_B[:, 18432:18434]
    rown = Bf[0:1, 9220:9228]
    bcast_sb = Bf[:, 9472:10496]
    rsum_row = rowbuf[0:1, 0:1024]

    g_qh, g_kh, g_vh, g_vtm, g_sqr, g_ah = G(), G(), G(), G(), G(), G()
    g_pt = [G(), G()]
    g_nrmp, g_rown, g_rsum, g_bcsb = G(), G(), G(), G()
    g_bias_h = G()

    stage_c = AR_C[:, 0:4096]
    stage_s = AR_C[:, 4096:8192]
    wslot = [AR_C[:, 8192 + i * 4096:8192 + (i + 1) * 4096] for i in range(6)]
    seedC = g_xn[0].wr() + g_xn[1].wr() + g_xq1t[0].wr() + g_xq1t[1].wr() + \
        g_x[0].wr() + g_x[1].wr() + g_tmp.wr()
    g_stage = G(*seedC)
    g_wslot = [G(*seedC) for _ in range(6)]

    CW = {"q": ("cq", "sq"), "k": ("ck", "sk"), "v": ("cv", "sv")}
    slice_tok = {}
    sidx_box = [0]

    def prep_group(g2):
        for wn in ("q", "k", "v"):
            sidx = sidx_box[0]
            slot = sidx % 6
            cdn, sdn = CW[wn]
            t_c = dma("gd", stage_c[:].rearrange("p (dc f) -> p dc f", f=256),
                      din[cdn][:, g2 * 256:(g2 + 1) * 256]
                      .rearrange("(dc p) f -> p dc f", p=128),
                      deps=g_stage.wr(), lane="Lstc")
            t_s = dma("gd", stage_s[:].rearrange("p (dc f) -> p dc f", f=256),
                      din[sdn][:, g2 * 256:(g2 + 1) * 256]
                      .rearrange("(dc p) f -> p dc f", p=128),
                      deps=g_stage.wr(), lane="Lsts")
            g_stage.writer = [t_c, t_s]
            g_stage.readers = {}
            t_q = tt(wslot[slot], stage_c, stage_s, ALU.mult,
                     [t_c, t_s] + g_wslot[slot].wr())
            g_stage.did_read(t_q)
            g_wslot[slot].did_write(t_q)
            slice_tok[(g2, wn)] = (slot, t_q)
            sidx_box[0] += 1

    prep_group(0)
    prep_group(1)

    ps_sc = [ps0[:, 0:1024], ps0[:, 1024:2048]]
    ps_av = ps1[:, 0:1024]
    ps_sum = ps1[0:1, 1024:2048]
    ps_nrm = ps1[:, 0:16]
    ps_t128 = ps1[0:1, 32:96].bitcast(BF16)
    ps_tiny = ps1[:, 16:17]
    ps_vt = ps0[:, 0:1024].bitcast(BF16)

    attn_store_toks = {}
    mj = 0
    for h in range(H):
        g2, hl = h // 2, h % 2
        if hl == 0 and 1 <= g2 < 7:
            prep_group(g2 + 1)
        for wn, dstb, gdst, bcol in (("q", qh_b, g_qh, 0), ("k", kh_b, g_kh, 16)):
            slot, t_wq = slice_tok[(g2, wn)]
            wv = wslot[slot][:].rearrange("p (dc f) -> p dc f", f=256)
            for tq in range(4):
                par = tq % 2
                bank = pg(ps0, par * 1024, par * 1024 + 512)
                mms = []
                for dc in range(ND):
                    mms.append(mm(ps_sc[par][:, 0:512],
                                  wv[:, dc, hl * 128:(hl + 1) * 128],
                                  AR_A[:, dc * T + tq * 512:dc * T + tq * 512 + 512],
                                  dc == 0, dc == ND - 1,
                                  ([t_wq, ev_a_toks[4 * tq + 3]] +
                                   [bb for b in bank for bb in b.wr()]) if dc == 0 else []))
                    g_A.did_read(mms[-1])
                g_wslot[slot].did_read(mms[-1])
                for b in bank:
                    b.did_write(mms[-1])
                t_ev = r.op("sc", lambda e, dstb=dstb, tq=tq, par=par, h=h, bcol=bcol:
                            e.activation(dstb[:, tq * 512:(tq + 1) * 512],
                                         ps_sc[par][:, 0:512], AF.Identity,
                                         bias=bqkv[:, bcol + h:bcol + h + 1]),
                            [mms[-1]] + tok_bqkv + (gdst.wr() if tq == 0 else []))
                for b in bank:
                    b.did_read(t_ev)
                if tq == 0:
                    gdst.did_write(t_ev)
                else:
                    gdst.writer.append(t_ev)
        def pnorm(src_g, src_b, out_i):
            t_sq = tt(sqr_b, src_b, src_b, ALU.mult, src_g.rd() + g_sqr.wr())
            src_g.did_read(t_sq)
            g_sqr.did_write(t_sq)
            nm = []
            for tti in range(NT):
                nm.append(mm(ps_nrm[:, tti:tti + 1],
                             sqr_b[:, tti * 128:(tti + 1) * 128], ones_col[:],
                             True, True,
                             [t_sq, tok_c["ones_col"]] +
                             (pgw(ps1, 0, 512) if tti == 0 else [])))
                g_sqr.did_read(nm[-1])
            for b in pg(ps1, 0, 512):
                b.did_write(nm[-1])
            t_red = red(nrm_pool[:, out_i:out_i + 1], ps_nrm[:], ALU.max,
                        nm + g_nrmp.wr())
            for b in pg(ps1, 0, 512):
                b.did_read(t_red)
            g_nrmp.did_write(t_red)
            t_t = r.op("te", lambda e, out_i=out_i: e.transpose(
                ps_t128, nrm_pool[:, out_i:out_i + 1], ident[:]),
                [t_red] + pgw(ps1, 0, 512))
            g_nrmp.did_read(t_t)
            for b in pg(ps1, 0, 512):
                b.did_write(t_t)
            t_mx = red(rown[0:1, out_i:out_i + 1], ps_t128, ALU.max,
                       [t_t] + g_rown.wr())
            for b in pg(ps1, 0, 512):
                b.did_read(t_mx)
            g_rown.did_write(t_mx)
            return t_mx

        t_mq = pnorm(g_qh, qh_b, 0)
        t_mk = pnorm(g_kh, kh_b, 1)
        t_pm = tt(rown[0:1, 2:3], rown[0:1, 0:1], rown[0:1, 1:2], ALU.mult, [t_mq, t_mk])
        t_l1 = act(rown[0:1, 2:3], rown[0:1, 2:3], AF.Ln, deps=[t_pm])
        t_l2 = act(rown[0:1, 2:3], rown[0:1, 2:3], AF.Exp, scale=0.5, deps=[t_l1])
        g_rown.did_write(t_l2)
        t_sp = mm(ps_tiny, onesr_f[:], rown[0:1, 2:3], True, True,
                  [t_l2] + pgw(ps1, 0, 512))
        g_rown.did_read(t_sp)
        for b in pg(ps1, 0, 512):
            b.did_write(t_sp)
        t_bh = ts(col(BIAS_H), ps_tiny, -RSQ128, None, ALU.mult, ALU.bypass,
                  [t_sp] + g_bias_h.wr())
        for b in pg(ps1, 0, 512):
            b.did_read(t_bh)
        g_bias_h.did_write(t_bh)

        for wn, dstb, gdst, bcol in (("v", vh_b, g_vh, 32),):
            slot, t_wq = slice_tok[(g2, wn)]
            wv = wslot[slot][:].rearrange("p (dc f) -> p dc f", f=256)
            for tq in range(4):
                par = tq % 2
                bank = pg(ps0, par * 1024, par * 1024 + 512)
                mms = []
                for dc in range(ND):
                    mms.append(mm(ps_sc[par][:, 0:512],
                                  wv[:, dc, hl * 128:(hl + 1) * 128],
                                  AR_A[:, dc * T + tq * 512:dc * T + tq * 512 + 512],
                                  dc == 0, dc == ND - 1,
                                  ([t_wq, ev_a_toks[4 * tq + 3]] +
                                   [bb for b in bank for bb in b.wr()]) if dc == 0 else []))
                    g_A.did_read(mms[-1])
                g_wslot[slot].did_read(mms[-1])
                for b in bank:
                    b.did_write(mms[-1])
                t_ev = r.op("sc", lambda e, dstb=dstb, tq=tq, par=par, h=h, bcol=bcol:
                            e.activation(dstb[:, tq * 512:(tq + 1) * 512],
                                         ps_sc[par][:, 0:512], AF.Identity,
                                         bias=bqkv[:, bcol + h:bcol + h + 1]),
                            [mms[-1]] + tok_bqkv + (gdst.wr() if tq == 0 else []))
                for b in bank:
                    b.did_read(t_ev)
                if tq == 0:
                    gdst.did_write(t_ev)
                else:
                    gdst.writer.append(t_ev)
        tps = []
        for kt in range(NT):
            tps.append(tp(ps_vt[:, kt * 128:(kt + 1) * 128],
                          vh_b[:, kt * 128:(kt + 1) * 128],
                          g_vh.rd() + (pgw(ps0, 0, 1024) if kt == 0 else [])))
            g_vh.did_read(tps[-1])
        for b in pg(ps0, 0, 1024):
            b.did_write(tps[-1])
        t_vtm = r.op("ve", lambda e: e.tensor_copy(vtm_b[:], ps_vt[:]),
                     tps + g_vtm.wr(), is_async=True)
        for b in pg(ps0, 0, 1024):
            b.did_read(t_vtm)
        g_vtm.did_write(t_vtm)

        for _ in range(12):
            if mj < len(mean_jobs):
                mean_jobs[mj]()
                mj += 1
        ah_parts = []
        for qh2 in range(2):
            av_last = []
            sum_last = []
            exp_par0 = None
            for kt in range(NT):
                par = kt % 2
                bank_s = pg(ps0, par * 1024, (par + 1) * 1024)
                m1 = mm(ps_sc[par][:, 0:512], kh_b[:, kt * 128:(kt + 1) * 128],
                        qh_b[:, qh2 * 1024:qh2 * 1024 + 512], True, True,
                        g_kh.rd() + g_qh.rd() + [bb for b in bank_s for bb in b.wr()])
                m2 = mm(ps_sc[par][:, 512:1024], kh_b[:, kt * 128:(kt + 1) * 128],
                        qh_b[:, qh2 * 1024 + 512:qh2 * 1024 + 1024], True, True, [])
                g_kh.did_read(m2)
                g_qh.did_read(m2)
                for b in bank_s:
                    b.did_write(m2)
                t_exp = r.op("sc", lambda e, par=par: e.activation(
                    pt_b[par], ps_sc[par], AF.Exp, bias=col(BIAS_H), scale=RSQ128),
                    [m1, m2] + g_bias_h.rd() + g_pt[par].wr())
                g_bias_h.did_read(t_exp)
                for b in bank_s:
                    b.did_read(t_exp)
                g_pt[par].did_write(t_exp)
                if par == 0:
                    exp_par0 = t_exp
                s1_ = mm(ps_sum[:, 0:512], ones_col[:], pt_b[par][:, 0:512],
                         kt == 0, kt == NT - 1,
                         [t_exp] + (pgw(ps1, 1024, 2048) if kt == 0 else []))
                s2_ = mm(ps_sum[:, 512:1024], ones_col[:], pt_b[par][:, 512:1024],
                         kt == 0, kt == NT - 1, [])
                a1 = mm(ps_av[:, 0:512], vtm_b[:, kt * 128:(kt + 1) * 128],
                        pt_b[par][:, 0:512], kt == 0, kt == NT - 1,
                        [t_exp] + g_vtm.rd() + (pgw(ps1, 0, 1024) if kt == 0 else []))
                a2 = mm(ps_av[:, 512:1024], vtm_b[:, kt * 128:(kt + 1) * 128],
                        pt_b[par][:, 512:1024], kt == 0, kt == NT - 1, [])
                g_vtm.did_read(a2)
                g_pt[par].did_read(a2)
                if kt == NT - 1:
                    sum_last = [s1_, s2_]
                    av_last = [a1, a2]
                    for b in pg(ps1, 1024, 2048):
                        b.did_write(s2_)
                    for b in pg(ps1, 0, 1024):
                        b.did_write(a2)
            t_rc0 = act(rsum_row, ps_sum[:], AF.Ln,
                        deps=sum_last + g_rsum.wr())
            t_rc = act(rsum_row, rsum_row, AF.Exp, scale=-1.0, deps=[t_rc0])
            for b in pg(ps1, 1024, 2048):
                b.did_read(t_rc)
            g_rsum.did_write(t_rc)
            bcs = []
            for c in range(2):
                bcs.append(mm(ps_sc[0][:, c * 512:(c + 1) * 512], onesr_f[:],
                              rsum_row[0:1, c * 512:(c + 1) * 512], True, True,
                              [t_rc] + ([exp_par0] + pgw(ps0, 0, 1024)
                                        if c == 0 else [])))
                g_rsum.did_read(bcs[-1])
            for b in pg(ps0, 0, 1024):
                b.did_write(bcs[-1])
            t_bc2 = r.op("ve", lambda e: e.tensor_copy(bcast_sb, ps_sc[0]),
                         bcs + g_bcsb.wr(), is_async=True)
            for b in pg(ps0, 0, 1024):
                b.did_read(t_bc2)
            g_bcsb.did_write(t_bc2)
            t_ev = tt(ah_b[:, qh2 * 1024:(qh2 + 1) * 1024], ps_av, bcast_sb, ALU.mult,
                      av_last + [t_bc2] + (g_ah.wr() if qh2 == 0 else []))
            r.ops[t_ev]["async"] = True
            g_bcsb.did_read(t_ev)
            for b in pg(ps1, 0, 1024):
                b.did_read(t_ev)
            ah_parts.append(t_ev)
        g_ah.writer = ah_parts
        g_ah.readers = {}
        t_st = dma("sd", attnh_d[h], ah_b[:], deps=ah_parts, lane="Lah")
        g_ah.did_read(t_st)
        attn_store_toks[h] = t_st

    while mj < len(mean_jobs):
        mean_jobs[mj]()
        mj += 1
    w1_ready = finish_wmean(0, W_S1, W_C1, W_NC1, W_I1)
    w2_ready = finish_wmean(1, W_S2, W_C2, W_NC2, W_I2)
    if wq_dbg is not None:
        dma("sd", wq_dbg[:], stp[:, 32:40], deps=w1_ready + w2_ready, lane="Ldbg")

    wo_stage_c = AR_B[:, 4096:6144]
    wo_stage_s = AR_B[:, 6144:8192]
    ap_stage = [AR_B[:, 8192:10240], AR_B[:, 10240:12288]]
    g_wost = G(*(g_qh.wr() + g_kh.wr()))
    g_apst = [G(*(g_vh.wr())), G(*(g_vtm.wr()))]
    g_wo = G(*(g_A.wr()))
    g_arb = G(*(g_stage.wr() + [t for i in range(6) for t in g_wslot[i].wr()]))
    t_bo_hilo = make_hilo(hilo_bo, din["bo"], D, Cf[0:1, 0:2048],
                          AR_C[0:1, 4096:6144], g_arb.wr())
    g_arb.did_read(t_bo_hilo)

    wo_toks = []
    rb_toks = []
    g_wost_rb = g_wost
    for fc in range(ND):
        t_c = dma("gd", wo_stage_c, din["co"][fc * 128:(fc + 1) * 128, :],
                  deps=g_wost.wr(), lane="Lwoc")
        t_s = dma("gd", wo_stage_s, din["so"][fc * 128:(fc + 1) * 128, :],
                  deps=g_wost.wr(), lane="Lwos")
        g_wost.writer = [t_c, t_s]
        g_wost.readers = {}
        t_q = tt(AR_A[:, fc * 2048:(fc + 1) * 2048], wo_stage_c, wo_stage_s,
                 ALU.mult, [t_c, t_s] + (g_wo.wr() if fc == 0 else []))
        g_wost.did_read(t_q)
        if fc == 0:
            g_wo.did_write(t_q)
        else:
            g_wo.writer.append(t_q)
        wo_toks.append(t_q)
        rb_toks.append(dma("sy", AR_C[:, fc * 2048:(fc + 1) * 2048], attnh_d[fc],
                           deps=[attn_store_toks[fc], t_bo_hilo] +
                           (g_arb.wr() if fc == 0 else []), lane="Larb"))
        if fc == 0:
            g_arb.did_write(rb_toks[-1])
        else:
            g_arb.writer.append(rb_toks[-1])

    ps_full = [ps0, ps1]
    aproj_store = {}
    for tti in range(NT):
        par = tti % 2
        psf = ps_full[par]
        banks = pg(psf, 0, 2048)
        mms = []
        for fc in range(ND):
            for c in range(4):
                dep0 = []
                if c == 0:
                    dep0 = [rb_toks[-1], wo_toks[fc]]
                    if fc == 0:
                        dep0 += [bb for b in banks for bb in b.wr()]
                mms.append(mm(psf[:, c * 512:(c + 1) * 512],
                              AR_C[:, fc * 2048 + tti * 128:fc * 2048 + tti * 128 + 128],
                              AR_A[:, fc * 2048 + c * 512:fc * 2048 + c * 512 + 512],
                              fc == 0, False, dep0))
            g_arb.did_read(mms[-1])
            g_wo.did_read(mms[-1])
        for c in range(4):
            mms.append(mm(psf[:, c * 512:(c + 1) * 512], ones2[:],
                          hilo_bo[:, c * 512:(c + 1) * 512], False, True,
                          [t_bo_hilo, tok_c["ones2"]] if c == 0 else []))
        for b in banks:
            b.did_write(mms[-1])
        t_ev = r.op("sc", lambda e, par=par, psf=psf: e.activation(
            ap_stage[par], psf[:], AF.Identity), [mms[-1]] + g_apst[par].wr())
        for b in banks:
            b.did_read(t_ev)
        g_apst[par].did_write(t_ev)
        t_st2 = dma("sd", aproj_d[tti], ap_stage[par], deps=[t_ev],
                    lane="Lap%d" % par)
        g_apst[par].did_read(t_st2)
        aproj_store[tti] = t_st2

    w1_stage = [Cf[:, 0:8192], Cf[:, 8192:16384]]
    w1q = [AR_A[:, 0:8192], AR_A[:, 8192:16384]]
    Af = AR_A[:].bitcast(F32)
    s_stage = [Af[:, 8192 + i * 512:8192 + (i + 1) * 512] for i in range(4)]
    t1_stage = [Af[:, 10240 + i * 512:10240 + (i + 1) * 512] for i in range(2)]
    b1bc = [Af[:, 11264:11776], Af[:, 11776:12288]]
    g_w1st = [G(*(g_arb.wr())), G(*(g_arb.wr()))]
    g_w1q = [G(*(g_wo.wr())), G(*(g_wo.wr()))]
    g_sst = [G(*(g_wo.wr())) for _ in range(4)]
    g_t1 = [G(*(g_wo.wr())), G(*(g_wo.wr()))]
    g_b1bc = [G(*(g_wo.wr())), G(*(g_wo.wr()))]
    g_scv1 = G()
    g_Brl = G(*(g_qh.wr() + g_kh.wr() + g_vh.wr() + g_vtm.wr() + g_sqr.wr() +
                g_ah.wr() + g_pt[0].wr() + g_pt[1].wr() + g_wost.wr() +
                g_apst[0].wr() + g_apst[1].wr() + g_mst.wr() + g_rown.wr() +
                g_nrmp.wr()))
    rl_toks = {}
    for tti in range(NT):
        rl_toks[tti] = dma("sy", AR_B[:, tti * 2048:(tti + 1) * 2048], xq1_d[tti],
                           deps=[xq1_store[tti]] + (g_Brl.wr() if tti < 2 else []),
                           lane="Lrl")
        if tti == 0:
            g_Brl.did_write(rl_toks[tti])
        else:
            g_Brl.writer.append(rl_toks[tti])
    scv_toks = []
    for t_ in range(NT):
        scv_toks.append(tt(scv1[:, t_:t_ + 1], invq1[:, t_:t_ + 1], col(W_I1),
                           ALU.mult, g_invq1.wr() + w1_ready + g_scv1.wr()))
        g_scv1.writer.append(scv_toks[-1])
    g_invq1.did_read(scv_toks[-1])

    ps_fp = [ps0[:, 0:512], ps0[:, 512:1024]]
    ps_fb = [pg(ps0, 0, 512), pg(ps0, 512, 1024)]
    s_store = {}
    for sbi in range(16):
        par = sbi % 2
        t_ld = dma("sy", w1_stage[par][:].rearrange("p (dc f) -> p dc f", f=512),
                   din["w1t"][:, sbi * 512:(sbi + 1) * 512]
                   .rearrange("(dc p) f -> p dc f", p=128),
                   deps=g_w1st[par].wr(), lane="Lw1%d" % par)
        g_w1st[par].did_write(t_ld)
        t_q1 = r.op("ve", lambda e, par=par: e.tensor_scalar(
            w1_stage[par], w1_stage[par], col(W_C1), col(W_NC1), ALU.min, ALU.max),
            [t_ld] + w1_ready)
        t_q2 = r.op("ve", lambda e, par=par: e.tensor_scalar(
            w1_stage[par], w1_stage[par], col(W_S1), MAGIC, ALU.mult, ALU.add), [t_q1])
        t_q3 = r.op("ve", lambda e, par=par: e.tensor_scalar(
            w1q[par], w1_stage[par], -MAGIC, None, ALU.add, ALU.bypass),
            [t_q2] + g_w1q[par].wr())
        g_w1st[par].did_write(t_q2)
        g_w1st[par].did_read(t_q3)
        g_w1q[par].did_write(t_q3)
        bpar = sbi % 2
        t_bb = dma("sy", b1bc[bpar],
                   din["b1"][sbi * 512:(sbi + 1) * 512]
                   .rearrange("(a n) -> a n", a=1).partition_broadcast(128).squeeze(1),
                   deps=g_b1bc[bpar].wr(), lane="Lb1%d" % bpar)
        g_b1bc[bpar].did_write(t_bb)
        for tti in range(NT):
            fpar = tti % 2
            mms = []
            for dc in range(ND):
                mms.append(mm(ps_fp[fpar],
                              AR_B[:, tti * 2048 + dc * 128:tti * 2048 + dc * 128 + 128],
                              w1q[par][:, dc * 512:(dc + 1) * 512],
                              dc == 0, dc == ND - 1,
                              ([t_q3, rl_toks[NT - 1]] +
                               [bb for b in ps_fb[fpar] for bb in b.wr()])
                              if dc == 0 else []))
            g_Brl.did_read(mms[-1])
            g_w1q[par].did_read(mms[-1])
            for b in ps_fb[fpar]:
                b.did_write(mms[-1])
            t_sc = r.op("ve", lambda e, fpar=fpar, tti=tti, bpar=bpar:
                        e.scalar_tensor_tensor(t1_stage[fpar], ps_fp[fpar],
                                               scv1[:, tti:tti + 1], b1bc[bpar],
                                               ALU.mult, ALU.add),
                        [mms[-1], t_bb, scv_toks[tti]] + g_t1[fpar].wr(),
                        is_async=True)
            g_b1bc[bpar].did_read(t_sc)
            g_scv1.did_read(t_sc)
            for b in ps_fb[fpar]:
                b.did_read(t_sc)
            g_t1[fpar].did_write(t_sc)
            sslot = tti % 4
            t_ev = r.op("sc", lambda e, fpar=fpar, sslot=sslot: e.activation(
                s_stage[sslot], t1_stage[fpar], AF.Silu), [t_sc] + g_sst[sslot].wr())
            g_t1[fpar].did_read(t_ev)
            g_sst[sslot].did_write(t_ev)
            t_st3 = dma("sd", s_d[tti][:, sbi * 512:(sbi + 1) * 512], s_stage[sslot],
                        deps=[t_ev], lane="Lss%d" % sslot)
            g_sst[sslot].did_read(t_st3)
            s_store[(tti, sbi)] = t_st3

    g_bc_t = Cf[:, 0:8192]
    b_bc_t = Cf[:, 8192:16384]
    s_rbs = [Af[:, 0:8192], Af[:, 8192:16384]]
    xq2_tm = AR_B[:, 0:8192]
    xq2_fm = AR_B[:, 8192:16384]
    zbuf = Bf[:, 8192:16384]
    ps_tp2 = [ps1[:, 0:512].bitcast(BF16), ps1[:, 512:1024].bitcast(BF16)]

    g_gb = G(*(g_w1st[0].wr() + g_w1st[1].wr()))
    seed_srb = (g_w1q[0].wr() + g_w1q[1].wr() + g_sst[0].wr() + g_sst[1].wr() +
                g_sst[2].wr() + g_sst[3].wr() + g_t1[0].wr() + g_t1[1].wr() +
                g_b1bc[0].wr() + g_b1bc[1].wr())
    g_srbs = [G(*seed_srb), G(*seed_srb)]
    g_z = G(*(g_Brl.wr()))
    g_xq2tm = G(*(g_Brl.wr()))
    g_xq2fm = G(*(g_Brl.wr()))
    g_mstats = G()
    g_scv2 = G()

    t_gbc = dma("sy", g_bc_t[:], din["g_ln"][:].rearrange("(a n) -> a n", a=1)
                .partition_broadcast(128).squeeze(1), deps=g_gb.wr(), lane="Lgb")
    t_bbc = dma("sy", b_bc_t[:], din["bt_ln"][:].rearrange("(a n) -> a n", a=1)
                .partition_broadcast(128).squeeze(1), deps=g_gb.wr(), lane="Lgb")
    g_gb.writer = [t_gbc, t_bbc]
    g_gb.readers = {}

    MT = 80
    xq2_store = {}
    for grp in range(4):
        for j in range(4):
            tti = grp * 4 + j
            pb = tti % 2
            s_rb = s_rbs[pb]
            g_srb = g_srbs[pb]
            t_rb = dma("sy", s_rb, s_d[tti],
                       deps=g_srb.wr() + [s_store[(tti, sbi)] for sbi in range(16)],
                       lane="Lsrb" + str(pb))
            g_srb.did_write(t_rb)
            t_gl = r.op("sc", lambda e, s_rb=s_rb: e.activation(
                s_rb, s_rb, AF.Gelu, accum_out=col(MT)), [t_rb] +
                g_mstats.wr())
            g_srb.writer.append(t_gl)
            t_sq = r.op("sc", lambda e, s_rb=s_rb: e.activation(
                zbuf, s_rb, AF.Square, accum_out=col(MT + 1)),
                [t_gl] + g_z.wr())
            g_srb.did_read(t_sq)
            g_z.did_write(t_sq)
            t_mu = ts(col(MT), col(MT), 1.0 / INNER, None, ALU.mult, ALU.bypass, [t_gl])
            t_e2 = ts(col(MT + 1), col(MT + 1), 1.0 / INNER, None, ALU.mult,
                      ALU.bypass, [t_sq])
            t_mm2 = tt(col(MT + 2), col(MT), col(MT), ALU.mult, [t_mu])
            t_ag = tt(col(MT + 1), col(MT + 1), col(MT + 2), ALU.subtract,
                      [t_e2, t_mm2])
            t_u = ts(col(MT + 4), stp[:, MT + 1:MT + 2], 1.0, EPS, ALU.mult, ALU.add, [t_ag])
            t_l = act(col(MT + 4), col(MT + 4), AF.Ln, deps=[t_u])
            t_r = act(col(MT + 5), col(MT + 4), AF.Exp, scale=-0.5, deps=[t_l])
            t_bx = ts(col(MT + 6), stp[:, MT:MT + 1], col(MT + 5), -1.0,
                      ALU.mult, ALU.mult, [t_r])
            t_z = r.op("sc", lambda e, s_rb=s_rb: e.activation(
                zbuf, s_rb, AF.Identity, bias=col(MT + 6), scale=col(MT + 5)),
                [t_bx, t_gl] + g_z.wr())
            g_srb.did_read(t_z)
            g_z.did_write(t_z)
            t_z2 = tt(zbuf, zbuf, g_bc_t, ALU.mult, [t_z] + g_gb.rd())
            g_gb.did_read(t_z2)
            t_z3 = tt(zbuf, zbuf, b_bc_t, ALU.add, [t_z2] + g_gb.rd())
            g_gb.did_read(t_z3)
            g_z.did_write(t_z3)
            t_zq = r.op("sc", lambda e: e.activation(
                xq2_tm, zbuf, AF.Square, accum_out=col(MT + 8)),
                [t_z3] + g_xq2tm.wr())
            g_z.did_read(t_zq)
            g_xq2tm.did_write(t_zq)
            g_mstats.did_write(t_zq)
            t_am = red(col(MT + 7), zbuf, ALU.max, [t_z3], absv=True)
            t_u2 = ts(col(MT + 8), col(MT + 8), 1.0 / INNER, EPS, ALU.mult,
                      ALU.add, [t_zq])
            t_l2 = act(col(MT + 8), col(MT + 8), AF.Ln, deps=[t_u2])
            t_r2 = act(col(MT + 9), col(MT + 8), AF.Exp, scale=-0.5, deps=[t_l2])
            t_mm_ = tt(col(MT + 10), col(MT + 7), col(MT + 9), ALU.mult, [t_am, t_r2])
            t_cl = ts(col(MT + 10), col(MT + 10), 1e-5, None, ALU.max, ALU.bypass, [t_mm_])
            t_rc = act(col(MT + 11), col(MT + 10), AF.Ln, deps=[t_cl])
            t_s2 = act(col(MT + 11), col(MT + 11), AF.Exp, scale=-1.0,
                       bias=4.844187086458591, deps=[t_rc])
            t_c2 = tt(col(MT + 12), col(MT + 9), col(MT + 11), ALU.mult, [t_s2])
            t_i2 = ts(col(MT + 13), col(MT + 10), 1.0 / 127.0, None, ALU.mult,
                      ALU.bypass, [t_cl])
            if mid_dbg is not None:
                t_mdbg = dma("sd", mid_dbg[tti], stp[:, 80:96],
                             deps=[t_c2, t_i2, t_r, t_bx], lane="Ldbg")
                g_mstats.readers[("mdbg", tti)] = t_mdbg
            t_p1 = ts(zbuf, zbuf, col(MT + 12), MAGIC, ALU.mult, ALU.add,
                      [t_c2, t_am, t_zq])
            g_z.did_write(t_p1)
            t_p2 = r.op("sc", lambda e: e.activation(
                xq2_tm, zbuf, AF.Identity, bias=-MAGIC),
                [t_p1] + g_xq2tm.wr())
            g_z.did_read(t_p2)
            g_xq2tm.did_write(t_p2)
            t_iv2 = ts(invq2[:, tti:tti + 1], col(MT + 13), 1.0, None,
                       ALU.mult, ALU.bypass, [t_i2] + g_invq2.wr())
            g_invq2.writer.append(t_iv2)
            g_mstats.readers[("iv2", tti)] = t_iv2
            ev_toks = []
            for b2 in range(8):
                par2 = b2 % 2
                bank = pg(ps1, par2 * 512, (par2 + 1) * 512)
                tbs = []
                for u in range(8):
                    ic = b2 * 8 + u
                    tbs.append(tp(ps_tp2[par2][:, u * 128:(u + 1) * 128],
                                  xq2_tm[:, ic * 128:(ic + 1) * 128],
                                  [t_p2] + ([bb for b in bank for bb in b.wr()]
                                            if u == 0 else [])))
                    g_xq2tm.did_read(tbs[-1])
                for b in bank:
                    b.did_write(tbs[-1])
                t_ev = r.op("ve", lambda e, b2=b2, par2=par2: e.tensor_copy(
                    xq2_fm[:, b2 * 1024:(b2 + 1) * 1024], ps_tp2[par2]),
                    tbs + (g_xq2fm.wr() if b2 == 0 else []), is_async=True)
                for b in bank:
                    b.did_read(t_ev)
                ev_toks.append(t_ev)
            g_xq2fm.writer = ev_toks
            g_xq2fm.readers = {}
            t_st4 = dma("sd", xq2_d[tti], xq2_fm[:], deps=ev_toks, lane="Lq2")
            g_xq2fm.did_read(t_st4)
            xq2_store[tti] = t_st4

    w2q_ar = [AR_A, AR_C]
    xq2_rb = [AR_B[:, 0:8192], AR_B[:, 8192:16384]]
    ap_rb = [AR_B[:, 16384:16896], AR_B[:, 16896:17408]]
    x_rb = [Bf[:, 8704:9216], Bf[:, 9216:9728]]
    t1_st = [Bf[:, 9728:10240], Bf[:, 10240:10752]]
    out_stg = [Bf[:, 10752:11264], Bf[:, 11264:11776]]
    b2bc = Bf[:, 11776:12288]
    w2_stage = Bf[:, 12288:16384]

    g_w2st = G(*(g_z.wr() + g_xq2tm.wr()))
    g_w2q = [G(*(g_srbs[0].wr() + g_srbs[1].wr())),
             G(*(g_gb.wr()))]
    g_xq2rb = [G(*(g_xq2tm.wr())), G(*(g_xq2fm.wr()))]
    g_eio = [G(*(g_z.wr())), G(*(g_z.wr()))]
    g_b2bc = G(*(g_z.wr()))
    scv2_toks = []
    for t_ in range(NT):
        scv2_toks.append(tt(scv2[:, t_:t_ + 1], invq2[:, t_:t_ + 1], col(W_I2),
                            ALU.mult, g_invq2.wr() + w2_ready + g_scv2.wr()))
        g_scv2.writer.append(scv2_toks[-1])
    g_invq2.did_read(scv2_toks[-1])

    w2q_toks = {}

    def stage_w2(oq):
        ar = w2q_ar[oq % 2]
        gq = g_w2q[oq % 2]
        toks = []
        for bt in range(8):
            t_ld = dma("sy", w2_stage[:].rearrange("p (ic f) -> p ic f", f=512),
                       din["w2t"][bt * 1024:(bt + 1) * 1024, oq * 512:(oq + 1) * 512]
                       .rearrange("(ic p) f -> p ic f", p=128),
                       deps=g_w2st.wr(), lane="Lw2s")
            g_w2st.did_write(t_ld)
            t_q1 = r.op("ve", lambda e: e.tensor_scalar(
                w2_stage, w2_stage, col(W_C2), col(W_NC2), ALU.min, ALU.max),
                [t_ld] + w2_ready)
            t_q2 = r.op("ve", lambda e: e.tensor_scalar(
                w2_stage, w2_stage, col(W_S2), MAGIC, ALU.mult, ALU.add), [t_q1])
            t_q3 = r.op("ve", lambda e, bt=bt, ar=ar: e.tensor_scalar(
                ar[:, bt * 4096:(bt + 1) * 4096], w2_stage, -MAGIC, None,
                ALU.add, ALU.bypass), [t_q2] + (gq.wr() if bt == 0 else []))
            g_w2st.did_write(t_q2)
            g_w2st.did_read(t_q3)
            if bt == 0:
                gq.did_write(t_q3)
            else:
                gq.writer.append(t_q3)
            toks.append(t_q3)
        w2q_toks[oq] = toks

    stage_w2(0)
    out_stores = []
    for oq in range(4):
        ar = w2q_ar[oq % 2]
        gq = g_w2q[oq % 2]
        qt = w2q_toks[oq]
        t_b2 = dma("sy", b2bc, din["b2"][oq * 512:(oq + 1) * 512]
                   .rearrange("(a n) -> a n", a=1).partition_broadcast(128).squeeze(1),
                   deps=g_b2bc.wr(), lane="Lb2")
        g_b2bc.did_write(t_b2)
        for tti in range(NT):
            par = tti % 2
            t_x2 = dma("sy", xq2_rb[par], xq2_d[tti],
                       deps=g_xq2rb[par].wr() + [xq2_store[tti]],
                       lane="Lx2%d" % par)
            g_xq2rb[par].did_write(t_x2)
            t_ap = dma("sy", ap_rb[par], aproj_d[tti][:, oq * 512:(oq + 1) * 512],
                       deps=g_eio[par].wr() + [aproj_store[tti]],
                       lane="Lapr%d" % par)
            t_xr = dma("sy", x_rb[par], xdv[tti * 128:(tti + 1) * 128,
                                            oq * 512:(oq + 1) * 512],
                       deps=g_eio[par].wr(), lane="Lxr%d" % par)
            mms = []
            for ic in range(NI):
                mms.append(mm(ps_fp[par], xq2_rb[par][:, ic * 128:(ic + 1) * 128],
                              ar[:, ic * 512:(ic + 1) * 512],
                              ic == 0, ic == NI - 1,
                              ([t_x2, qt[ic // 8]] +
                               [bb for b in ps_fb[par] for bb in b.wr()])
                              if ic == 0 else ([qt[ic // 8]] if ic % 8 == 0 else [])))
            g_xq2rb[par].did_read(mms[-1])
            gq.did_read(mms[-1])
            for b in ps_fb[par]:
                b.did_write(mms[-1])
            t_t1 = r.op("ve", lambda e, par=par, tti=tti:
                        e.scalar_tensor_tensor(t1_st[par], ps_fp[par],
                                               scv2[:, tti:tti + 1], ap_rb[par],
                                               ALU.mult, ALU.add),
                        [mms[-1], t_ap, scv2_toks[tti]], is_async=True)
            g_scv2.did_read(t_t1)
            for b in ps_fb[par]:
                b.did_read(t_t1)
            t_o1 = tt(t1_st[par], t1_st[par], x_rb[par], ALU.add, [t_t1, t_xr])
            t_o = tt(out_stg[par], t1_st[par], b2bc, ALU.add, [t_o1, t_b2])
            g_b2bc.did_read(t_o)
            t_so = dma("sd", out_d[tti * 128:(tti + 1) * 128, oq * 512:(oq + 1) * 512],
                       out_stg[par], deps=[t_o], lane="Lo%d" % par)
            g_eio[par].did_write(t_so)
            out_stores.append(t_so)
        if oq < 3:
            stage_w2(oq + 1)

    r.op("ve", lambda e: e.engine_nop(), out_stores[-2:])

    import contextlib
    stack = contextlib.ExitStack()
    with nc.Block() as block:
        r.emit(nc, block, stack)
    return nc, stack


def host_inputs(inputs):
    f32 = lambda a: np.ascontiguousarray(np.asarray(a, dtype=np.float32))
    rep = lambda s: np.ascontiguousarray(np.repeat(f32(s), 32, axis=0))
    shared = {
        "cq": f32(inputs["L0_attn_wq_codes"]), "sq": rep(inputs["L0_attn_wq_scales"]),
        "ck": f32(inputs["L0_attn_wk_codes"]), "sk": rep(inputs["L0_attn_wk_scales"]),
        "cv": f32(inputs["L0_attn_wv_codes"]), "sv": rep(inputs["L0_attn_wv_scales"]),
        "co": f32(inputs["L0_attn_wo_codes"]), "so": rep(inputs["L0_attn_wo_scales"]),
        "bq": f32(inputs["L0_attn_bq"]), "bk": f32(inputs["L0_attn_bk"]),
        "bv": f32(inputs["L0_attn_bv"]), "bo": f32(inputs["L0_attn_bo"]),
        "w1t": np.ascontiguousarray(f32(inputs["L0_ffn_w1"]).T),
        "b1": f32(inputs["L0_ffn_b1"]),
        "g_ln": f32(inputs["L0_ffn_ln_g"]), "bt_ln": f32(inputs["L0_ffn_ln_b"]),
        "w2t": np.ascontiguousarray(f32(inputs["L0_ffn_w2"]).T),
        "b2": f32(inputs["L0_ffn_b2"]),
        "ident": np.eye(128).astype(ml_dtypes.bfloat16),
        "ones_col": np.ones((128, 1), dtype=ml_dtypes.bfloat16),
        "ones2": np.ones((2, 128), dtype=ml_dtypes.bfloat16),
        "onesr_f": np.ones((1, 128), dtype=np.float32),
        "magicv": np.full((128, 1), MAGIC, dtype=np.float32),
        "ln127v": np.full((128, 1), 4.844187086458591, dtype=np.float32),
        "nmagicv": np.full((128, 1), -MAGIC, dtype=np.float32),
        "ones_col_f": np.ones((128, 1), dtype=np.float32),
    }
    x = f32(inputs["L0_attn_x"])
    return x, shared


def kernel(**inputs):
    x, shared = host_inputs(inputs)
    nc, stack = build_program()
    in_maps = [dict(shared, x=np.ascontiguousarray(x[c])) for c in range(8)]
    res = run_bass_kernel_spmd(nc, in_maps, core_ids=list(range(8)))
    out = np.stack([np.asarray(res.results[c]["out"]) for c in range(8)], axis=0)
    return out.astype(np.float32)


# version 17
# speedup vs baseline: 1.1184x; 1.1184x over previous
import numpy as np
import ml_dtypes
import concourse.bass as bass
import concourse.mybir as mybir
from concourse.bass_utils import run_bass_kernel_spmd

F32 = mybir.dt.float32
BF16 = mybir.dt.bfloat16
AF = mybir.ActivationFunctionType
ALU = mybir.AluOpType
AX = mybir.AxisListType

D = 2048
T = 2048
H = 16
INNER = 8192
ND = 16
NT = 16
NI = 64
MAGIC = float(np.float32(1.5 * 2 ** 23))
RSQ128 = float(1.0 / np.sqrt(np.float32(128.0)))
EPS = 1e-5
CLIP_W = 1.4995117

DEBUG_OUTS = set()
SERIALIZE_DMA = False
PHASE_LIMIT = None
P0_MIN = False
SERIALIZE_ALL = False
NO_ELIDE = {"vector", "gpsimd"}

COMPUTE_ENGS = {"sc", "ve", "te", "gc"}
STREAM_OF = {"sy": "sync", "sc": "scalar", "sd": "scalar",
             "ve": "vector", "te": "tensor", "gc": "gpsimd", "gd": "gpsimd"}


class Rec:
    def __init__(self):
        self.ops = []

    def op(self, eng, fn, deps=(), force=False, is_async=False, sem=None):
        i = len(self.ops)
        if SERIALIZE_ALL and i > 0:
            deps = list(deps) + [i - 1]
        if SERIALIZE_DMA and eng in ("sy", "sd", "gd"):
            prev = getattr(self, "_prev_dma_" + eng, None)
            if prev is not None:
                deps = list(deps) + [prev]
            setattr(self, "_prev_dma_" + eng, i)
        dd = []
        for d in deps:
            if d is None:
                continue
            if isinstance(d, (list, tuple)):
                dd += [x for x in d if x is not None]
            else:
                dd.append(d)
        self.ops.append({"eng": eng, "fn": fn, "deps": dd, "force": force,
                         "async": is_async, "sem": sem or eng,
                         "inc": 16 if eng in ("sy", "sd", "gd") else 1,
                         "needed": False})
        return i

    def emit(self, nc, block, stack):
        ops = self.ops
        for o in ops:
            if o["eng"] in ("sy", "sd", "gd"):
                o["needed"] = True
            if SERIALIZE_ALL:
                o["needed"] = True
            for d in o["deps"]:
                do = ops[d]
                if (not SERIALIZE_ALL and not o["force"] and not do["async"]
                        and STREAM_OF[do["eng"]] not in NO_ELIDE
                        and do["eng"] in COMPUTE_ENGS and o["eng"] in COMPUTE_ENGS
                        and STREAM_OF[do["eng"]] == STREAM_OF[o["eng"]]):
                    continue
                do["needed"] = True
        counts = {}
        for o in ops:
            if o["needed"]:
                counts[o["sem"]] = counts.get(o["sem"], 0) + o["inc"]
                o["val"] = counts[o["sem"]]
            else:
                o["val"] = None
        sems = {e: stack.enter_context(nc.semaphore(f"sem_{e}")) for e in counts}

        def emitter(stream_name):
            def inner(engobj):
                hw = {}
                for o in ops:
                    if STREAM_OF[o["eng"]] != stream_name:
                        continue
                    for d in o["deps"]:
                        do = ops[d]
                        if (not SERIALIZE_ALL and not o["force"] and not do["async"]
                                and stream_name not in NO_ELIDE
                                and do["eng"] in COMPUTE_ENGS and o["eng"] in COMPUTE_ENGS
                                and STREAM_OF[do["eng"]] == stream_name):
                            continue
                        v = do["val"]
                        if hw.get(do["sem"], 0) >= v:
                            continue
                        engobj.wait_ge(sems[do["sem"]], v)
                        hw[do["sem"]] = v
                    inst = o["fn"](engobj)
                    if o["needed"]:
                        inst.then_inc(sems[o["sem"]], o["inc"])
            return inner

        for s in sorted({STREAM_OF[o["eng"]] for o in ops}):
            getattr(block, s)(emitter(s))
        return counts


class Guard:
    def __init__(self, rec, seed=()):
        self.rec = rec
        self.writer = []
        self.readers = {}
        for t in seed:
            if t is not None:
                self.readers[("seed", len(self.readers))] = t

    def rd(self):
        return list(self.writer)

    def wr(self):
        return list(self.writer) + list(self.readers.values())

    def did_read(self, tok):
        if tok is not None:
            self.readers[self.rec.ops[tok]["eng"]] = tok

    def did_write(self, tok):
        self.writer = [tok]
        self.readers = {}


def build_program():
    nc = bass.Bass("TRN2", target_bir_lowering=False)
    r = Rec()
    G = lambda *seed: Guard(r, seed)

    din = {}
    for name, shape in [
        ("x", [T, D]),
        ("cq", [D, D]), ("sq", [D, D]), ("ck", [D, D]), ("sk", [D, D]),
        ("cv", [D, D]), ("sv", [D, D]), ("co", [D, D]), ("so", [D, D]),
        ("bq", [D]), ("bk", [D]), ("bv", [D]), ("bo", [D]),
        ("w1t", [D, INNER]), ("b1", [INNER]), ("g_ln", [INNER]), ("bt_ln", [INNER]),
        ("w2t", [INNER, D]), ("b2", [D]),
    ]:
        din[name] = nc.dram_tensor(name, shape, F32, kind="ExternalInput")
    for name, shape in [("ident", [128, 128]), ("ones_col", [128, 1]),
                        ("ones2", [2, 128])]:
        din[name] = nc.dram_tensor(name, shape, BF16, kind="ExternalInput")
    din["onesr_f"] = nc.dram_tensor("onesr_f", [1, 128], F32, kind="ExternalInput")
    din["ones_col_f"] = nc.dram_tensor("ones_col_f", [128, 1], F32, kind="ExternalInput")
    din["magicv"] = nc.dram_tensor("magicv", [128, 1], F32, kind="ExternalInput")
    din["ln127v"] = nc.dram_tensor("ln127v", [128, 1], F32, kind="ExternalInput")
    din["nmagicv"] = nc.dram_tensor("nmagicv", [128, 1], F32, kind="ExternalInput")

    out_d = nc.dram_tensor("out", [T, D], F32, kind="ExternalOutput")

    def scr(name, shape, dtype):
        kind = "ExternalOutput" if name in DEBUG_OUTS else "Internal"
        return nc.dram_tensor(name, shape, dtype, kind=kind)

    attnh_d = scr("attnh_d", [H, 128, T], BF16)
    aproj_d = scr("aproj_d", [NT, 128, D], BF16)
    s_d = scr("s_d", [NT, 128, INNER], F32)
    xq2_d = scr("xq2_d", [NT, 128, INNER], BF16)
    xq1_d = scr("xq1_d", [NT, 128, D], BF16)
    xnfm_dbg = scr("xnfm_dbg", [ND, 128, T], BF16) if "xnfm_dbg" in DEBUG_OUTS else None
    p0st_dbg = scr("p0st_dbg", [NT, 128, 32], F32) if "p0st_dbg" in DEBUG_OUTS else None
    p0bn_dbg = scr("p0bn_dbg", [NT, 128, 24], F32) if "p0bn_dbg" in DEBUG_OUTS else None
    wq_dbg = scr("wq_dbg", [128, 8], F32) if "wq_dbg" in DEBUG_OUTS else None
    mid_dbg = scr("mid_dbg", [NT, 128, 16], F32) if "mid_dbg" in DEBUG_OUTS else None

    AR_A = nc.alloc_sbuf_tensor("AR_A", [128, 32768], BF16)
    AR_B = nc.alloc_sbuf_tensor("AR_B", [128, 32768], BF16)
    AR_C = nc.alloc_sbuf_tensor("AR_C", [128, 32768], BF16)
    Bf = AR_B[:].bitcast(F32)
    Cf = AR_C[:].bitcast(F32)
    Af = AR_A[:].bitcast(F32)
    ident = nc.alloc_sbuf_tensor("identt", [128, 128], BF16)
    ones_col = nc.alloc_sbuf_tensor("ones_colt", [128, 1], BF16)
    ones_col_f = nc.alloc_sbuf_tensor("ones_col_ft", [128, 1], F32)
    ones2 = nc.alloc_sbuf_tensor("ones2t", [2, 128], BF16)
    onesr_f = nc.alloc_sbuf_tensor("onesr_ft", [1, 128], F32)
    bqkv = nc.alloc_sbuf_tensor("bqkv", [128, 48], F32)
    stp = nc.alloc_sbuf_tensor("stp", [128, 192], F32)
    wacc = nc.alloc_sbuf_tensor("wacc", [128, 8], F32)
    rowbuf = nc.alloc_sbuf_tensor("rowbuf", [1, 1024], F32)
    cmagic = nc.alloc_sbuf_tensor("cmagic", [128, 1], F32)
    nc.const_aps.register(F32, MAGIC, cmagic[:])
    cln127 = nc.alloc_sbuf_tensor("cln127", [128, 1], F32)
    nc.const_aps.register(F32, 4.844187086458591, cln127[:])
    cnmagic = nc.alloc_sbuf_tensor("cnmagic", [128, 1], F32)
    nc.const_aps.register(F32, -MAGIC, cnmagic[:])
    hilo_bo = nc.alloc_sbuf_tensor("hilo_bo", [2, D], BF16)
    invq1 = nc.alloc_sbuf_tensor("invq1", [128, 16], F32)
    invq2 = nc.alloc_sbuf_tensor("invq2", [128, 16], F32)
    scv1 = nc.alloc_sbuf_tensor("scv1", [128, 16], F32)
    scv2 = nc.alloc_sbuf_tensor("scv2", [128, 16], F32)

    ps0 = nc.alloc_psum_tensor("ps0", [128, 2048], F32)
    ps1 = nc.alloc_psum_tensor("ps1", [128, 2048], F32)

    pbk = [G() for _ in range(8)]

    def pg(t, lo, hi):
        base = 0 if t is ps0 else 4
        return [pbk[base + b] for b in range(lo // 512, (hi + 511) // 512)]

    def pgw(t, lo, hi):
        return [bb for b in pg(t, lo, hi) for bb in b.wr()]

    W_S1, W_C1, W_NC1, W_I1, W_S2, W_C2, W_NC2, W_I2, W_M1, W_M2 = range(32, 42)
    BIAS_H = 42

    def col(i):
        return stp[:, i:i + 1]

    def dma(q, dst, src, deps=(), lane=None, **kw):
        return r.op(q, lambda e, d=dst, s=src, kw=kw: e.dma_start(out=d, in_=s, **kw),
                    deps, sem=lane)

    def ts(out, in0, s1_, s2_, op0, op1, deps=(), eng="ve"):
        return r.op(eng, lambda e, o=out, i=in0, a=s1_, b=s2_, o0=op0, o1=op1:
                    e.tensor_scalar(o, i, a, b, o0, o1), deps)

    def tt(out, in0, in1, aop, deps=(), eng="ve"):
        return r.op(eng, lambda e, o=out, a=in0, b=in1, p=aop: e.tensor_tensor(o, a, b, p), deps)

    def act(out, in_, func, bias=0.0, scale=1.0, deps=()):
        return r.op("sc", lambda e, o=out, i=in_, f=func, b=bias, s=scale:
                    e.activation(o, i, f, bias=b, scale=s), deps)

    def mm(out, lhsT, rhs, start, stop, deps=()):
        return r.op("te", lambda e, o=out, l=lhsT, rr_=rhs, st=start, sp=stop:
                    e.matmul(o, l, rr_, start=st, stop=sp), deps)

    def tp(out, in_, deps=()):
        return r.op("te", lambda e, o=out, i=in_: e.transpose(o, i, ident[:]), deps)

    def red(out, in_, aop, deps=(), absv=None, eng="ve"):
        return r.op(eng, lambda e, o=out, i=in_, p=aop, a=absv:
                    e.tensor_reduce(o, i, AX.X, p, apply_absolute_value=a), deps,
                    is_async=True)

    tok_c = {}
    last_c = None
    for nm, dst in [("ident", ident), ("ones_col", ones_col), ("ones2", ones2),
                    ("onesr_f", onesr_f), ("ones_col_f", ones_col_f),
                    ("magicv", cmagic), ("ln127v", cln127), ("nmagicv", cnmagic)]:
        last_c = dma("sy", dst[:], din[nm][:], lane="Lcst")
    for j, nm in enumerate(("bq", "bk", "bv")):
        last_c = dma("sy", bqkv[:, j * 16:(j + 1) * 16],
                     din[nm][:].rearrange("(c p) -> p c", p=128),
                     allow_slow_non_contiguous=True, lane="Lcst")
    for nm in ("ident", "ones_col", "ones2", "onesr_f", "ones_col_f",
               "magicv", "ln127v"):
        tok_c[nm] = last_c
    t_magic = last_c
    tok_bqkv = [last_c]

    def make_hilo(dst, src_dram, n, row_ap, lo_ap, deps):
        t0 = dma("sy", row_ap, src_dram[:].rearrange("(a n) -> a n", a=1), deps=deps,
                 lane="Lhl")
        t1 = r.op("ve", lambda e, d=dst, rw=row_ap: e.tensor_copy(d[0:1, :], rw), [t0])
        t2 = tt(lo_ap, row_ap, dst[0:1, :], ALU.subtract, [t1])
        t3 = dma("sy", dst[1:2, :], lo_ap, deps=[t2], lane="Lhl")
        return t3

    xdv = din["x"]
    xn_t = [AR_C[:, 0:2048], AR_C[:, 2048:4096]]
    xq1_t = [AR_C[:, 4096:6144], AR_C[:, 6144:8192]]
    x_stage = [Cf[:, 4096:6144], Cf[:, 6144:8192]]
    tmp_rd = Cf[:, 8192:10240]
    xq1_sp = [AR_B[:, 0:2048], AR_B[:, 2048:4096]]
    xnf32 = Cf[:, 10240:12288]
    ps_tpA = ps0[:, 0:1024].bitcast(BF16)
    ps_tpB = ps0[:, 1024:2048].bitcast(BF16)

    g_x = [G(), G()]
    g_xn = [G(), G()]
    g_xq1t = [G(), G()]
    g_xq1sp = [G(), G()]
    g_tmp = G()
    g_xnf = G()
    ev_a_toks = []
    g_invq1 = G()
    g_invq2 = G()
    g_A = G()
    g_stats = [G(), G()]
    xq1_store = {}

    MV, _VV, MX, MN, RLN, BXN, T0, T1c, T2c, CQ, INVQ, BR2 = range(12)

    def p0_col(i, par):
        return stp[:, par * 16 + i:par * 16 + i + 1]

    for tti in range(NT):
        par = tti % 2
        bnS = 96 if par == 0 else 56
        t_x = dma("sy", x_stage[par], xdv[tti * 128:(tti + 1) * 128, :],
                  deps=g_x[par].wr(), lane="Lx%d" % par)
        g_x[par].did_write(t_x)
        t_bn = []
        for c in range(4):
            t_bn.append(r.op("ve", lambda e, c=c, par=par, bnS=bnS: e.bn_stats(
                stp[:, bnS + c * 6:bnS + c * 6 + 6],
                x_stage[par][:, c * 512:(c + 1) * 512]),
                [t_x] + g_stats[par].wr(), is_async=True))
            g_x[par].did_read(t_bn[-1])
        t_ag = r.op("ve", lambda e, par=par, bnS=bnS: e.bn_aggr(
            stp[:, par * 16 + MV:par * 16 + MV + 2], stp[:, bnS:bnS + 24]), t_bn,
            force=True, is_async=True)
        t_mx = red(p0_col(MX, par), x_stage[par], ALU.max, [t_x] + g_stats[par].wr())
        t_mn = red(p0_col(MN, par), x_stage[par], ALU.min, [t_x] + g_stats[par].wr())
        g_x[par].did_read(t_mx)
        g_x[par].did_read(t_mn)
        t_u = ts(p0_col(T0, par), p0_col(MV + 1, par), 1.0, EPS, ALU.mult, ALU.add, [t_ag])
        t_ln = act(p0_col(T0, par), p0_col(T0, par), AF.Ln, deps=[t_u])
        t_rl = act(p0_col(RLN, par), p0_col(T0, par), AF.Exp, scale=-0.5, deps=[t_ln])
        t_bx = ts(p0_col(BXN, par), p0_col(MV, par), p0_col(RLN, par), -1.0,
                  ALU.mult, ALU.mult, [t_rl])
        t_xnf = r.op("sc", lambda e, par=par: e.activation(
            xnf32, x_stage[par], AF.Identity,
            bias=p0_col(BXN, par), scale=p0_col(RLN, par)),
            [t_bx, t_x] + g_xnf.wr())
        g_xnf.did_write(t_xnf)
        g_x[par].did_read(t_xnf)
        t_xn = r.op("ve", lambda e, par=par: e.tensor_copy(xn_t[par], xnf32),
                    [t_xnf] + g_xn[par].wr())
        g_xnf.did_read(t_xn)
        g_xn[par].did_write(t_xn)
        if P0_MIN:
            g_stats[par].did_write(t_xn)
            tps_a = []
            for j in range(16):
                tps_a.append(tp(ps_tpA[:, j * 128:(j + 1) * 128],
                                xn_t[par][:, j * 128:(j + 1) * 128],
                                [t_xn, tok_c["ident"]] +
                                (pgw(ps0, 0, 1024) if j == 0 else [])))
                g_xn[par].did_read(tps_a[-1])
            for b in pg(ps0, 0, 1024):
                b.did_write(tps_a[-1])
            ev_a = r.op("ve", lambda e, tti=tti: e.tensor_copy(
                AR_A[:].rearrange("p (dc t) -> p dc t", t=T)[:, :, tti * 128:tti * 128 + 128],
                ps_tpA[:].rearrange("p (dc j) -> p dc j", j=128)),
                tps_a + g_A.wr())
            for b in pg(ps0, 0, 1024):
                b.did_read(ev_a)
            g_A.did_write(ev_a)
            xq1_store[tti] = t_xn
            continue
        t_a = tt(p0_col(T1c, par), p0_col(RLN, par), p0_col(RLN, par), ALU.mult, [t_rl])
        t_b = tt(p0_col(T1c, par), p0_col(MV + 1, par), p0_col(T1c, par), ALU.mult, [t_a])
        t_c = ts(p0_col(T1c, par), p0_col(T1c, par), 1.0, EPS, ALU.mult, ALU.add, [t_b])
        t_d = act(p0_col(T1c, par), p0_col(T1c, par), AF.Ln, deps=[t_c])
        t_r2 = act(p0_col(BR2, par), p0_col(T1c, par), AF.Exp, scale=-0.5, deps=[t_d])
        t_e = tt(p0_col(T0, par), p0_col(MX, par), p0_col(MV, par), ALU.subtract,
                 [t_mx, t_ag, t_rl])
        t_f = tt(p0_col(T2c, par), p0_col(MV, par), p0_col(MN, par), ALU.subtract,
                 [t_mn, t_ag])
        t_g = tt(p0_col(T0, par), p0_col(T0, par), p0_col(T2c, par), ALU.max, [t_e, t_f])
        t_h = tt(p0_col(T0, par), p0_col(T0, par), p0_col(RLN, par), ALU.mult, [t_g])
        t_i = tt(p0_col(T0, par), p0_col(T0, par), p0_col(BR2, par), ALU.mult,
                 [t_h, t_r2])
        t_j = ts(p0_col(T0, par), p0_col(T0, par), 1e-5, None, ALU.max, ALU.bypass, [t_i])
        t_k = act(p0_col(T1c, par), p0_col(T0, par), AF.Ln, deps=[t_j, t_r2])
        t_l = act(p0_col(T1c, par), p0_col(T1c, par), AF.Exp, scale=-1.0,
                  bias=4.844187086458591, deps=[t_k])
        t_m = tt(p0_col(CQ, par), p0_col(BR2, par), p0_col(T1c, par), ALU.mult, [t_l])
        t_n = ts(p0_col(INVQ, par), p0_col(T0, par), 1.0 / 127.0, None, ALU.mult,
                 ALU.bypass, [t_j])
        t_p1 = r.op("sc", lambda e, par=par: e.activation(
            tmp_rd, xnf32, AF.Identity, bias=MAGIC, scale=p0_col(CQ, par)),
            [t_m, t_xnf, t_magic] + g_tmp.wr())
        g_tmp.did_write(t_p1)
        g_xnf.did_read(t_p1)
        t_p2 = ts(xq1_t[par], tmp_rd, -MAGIC, None, ALU.add, ALU.bypass,
                  [t_p1] + g_xq1t[par].wr())
        g_tmp.did_read(t_p2)
        g_xq1t[par].did_write(t_p2)
        t_iv = ts(invq1[:, tti:tti + 1], p0_col(INVQ, par), 1.0, None,
                  ALU.mult, ALU.bypass, [t_n] + g_invq1.wr())
        g_invq1.writer.append(t_iv)
        g_stats[par].did_write(t_p2)
        g_stats[par].readers["iv"] = t_iv
        if p0st_dbg is not None:
            t_dbg = dma("sd", p0st_dbg[tti][:, 0:16], stp[:, par * 16:par * 16 + 16],
                        deps=[t_p2, t_n, t_m, t_bx, t_xn], lane="Ldbg")
            g_stats[par].readers[("dbg", tti)] = t_dbg
        tps_a = []
        for j in range(16):
            tps_a.append(tp(ps_tpA[:, j * 128:(j + 1) * 128],
                            xn_t[par][:, j * 128:(j + 1) * 128],
                            [t_xn, tok_c["ident"]] +
                            (pgw(ps0, 0, 1024) if j == 0 else [])))
            g_xn[par].did_read(tps_a[-1])
        for b in pg(ps0, 0, 1024):
            b.did_write(tps_a[-1])
        ev_a = r.op("ve", lambda e, tti=tti: e.tensor_copy(
            AR_A[:].rearrange("p (dc t) -> p dc t", t=T)[:, :, tti * 128:tti * 128 + 128],
            ps_tpA[:].rearrange("p (dc j) -> p dc j", j=128)),
            tps_a + g_A.wr(), is_async=True)
        for b in pg(ps0, 0, 1024):
            b.did_read(ev_a)
        g_A.writer.append(ev_a)
        ev_a_toks.append(ev_a)
        tps_b = []
        for j in range(16):
            tps_b.append(tp(ps_tpB[:, j * 128:(j + 1) * 128],
                            xq1_t[par][:, j * 128:(j + 1) * 128],
                            [t_p2] + (pgw(ps0, 1024, 2048) if j == 0 else [])))
            g_xq1t[par].did_read(tps_b[-1])
        for b in pg(ps0, 1024, 2048):
            b.did_write(tps_b[-1])
        ev_b = r.op("ve", lambda e, par=par: e.tensor_copy(
            xq1_sp[par], ps_tpB[:]), tps_b + g_xq1sp[par].wr(), is_async=True)
        for b in pg(ps0, 1024, 2048):
            b.did_read(ev_b)
        g_xq1sp[par].did_write(ev_b)
        t_spill = dma("sd", xq1_d[tti], xq1_sp[par], deps=[ev_b], lane="Lq1%d" % par)
        g_xq1sp[par].did_read(t_spill)
        xq1_store[tti] = t_spill

    if xnfm_dbg is not None:
        for dc in range(ND):
            g_A.did_read(dma("sd", xnfm_dbg[dc],
                             AR_A[:, dc * T:(dc + 1) * T], deps=g_A.rd(),
                             lane="Ldbg"))

    if PHASE_LIMIT == "p0":
        r.op("ve", lambda e: e.engine_nop(),
             [xq1_store[NT - 1]] + g_A.rd())
        import contextlib
        stack = contextlib.ExitStack()
        with nc.Block() as block:
            r.emit(nc, block, stack)
        return nc, stack

    mean_stage = [Bf[:, 12288:14336], Bf[:, 14336:16384]]
    CWM = 2048
    g_mst = [G(), G()]
    t_z0 = r.op("gc", lambda e: e.memset(wacc[:], 0.0), [])
    acc_last = {0: t_z0, 1: t_z0}
    mhat_tok = {}
    nmhat_tok = {}
    mean_jobs = []
    job_ix = [0]
    for wi, wname in enumerate(("w1t", "w2t")):
        wd = din[wname]
        nch = wd.shape[1] // CWM
        nrb = wd.shape[0] // 128
        mcol = W_M1 if wi == 0 else W_M2
        for dc2 in range(nrb):
            for hh in range(nch):
              def _mean_job(wi=wi, wd=wd, dc2=dc2, hh=hh, mcol=mcol):
                  p = job_ix[0] % 2
                  job_ix[0] += 1
                  mst = mean_stage[p]
                  t_ld = dma("sy", mst,
                             wd[dc2 * 128:(dc2 + 1) * 128, hh * CWM:(hh + 1) * CWM],
                             deps=g_mst[p].wr(), lane="Lms" + str(p))
                  g_mst[p].did_write(t_ld)
                  rs = wacc[:, 4 + p:5 + p]
                  t_rd = red(rs, mst, ALU.add, [t_ld, acc_last.get(2 + p, t_z0)],
                             absv=True)
                  g_mst[p].did_read(t_rd)
                  acc_last[2 + p] = t_rd
                  if dc2 == 0 and hh == 0:
                      t_m0 = mm(ps1[0:1, 0:1], ones_col_f[:], rs,
                                True, True,
                                [t_rd, tok_c["ones_col_f"]] + pgw(ps1, 0, 512))
                      for b in pg(ps1, 0, 512):
                          b.did_write(t_m0)
                      t_m1 = r.op("ve", lambda e: e.tensor_scalar(
                          rowbuf[0:1, 0:1], ps1[0:1, 0:1], 1.0 / (128 * CWM), None,
                          ALU.mult, ALU.bypass), [t_m0])
                      for b in pg(ps1, 0, 512):
                          b.did_read(t_m1)
                      t_m2 = mm(ps1[:, 0:1], onesr_f[:], rowbuf[0:1, 0:1], True, True,
                                [t_m1, tok_c["onesr_f"]] + pgw(ps1, 0, 512))
                      for b in pg(ps1, 0, 512):
                          b.did_write(t_m2)
                      t_mh = r.op("ve", lambda e, mcol=mcol: e.tensor_copy(
                          col(mcol), ps1[:, 0:1]), [t_m2], is_async=True)
                      for b in pg(ps1, 0, 512):
                          b.did_read(t_mh)
                      mhat_tok[wi] = t_mh
                      nmhat_tok[wi] = ts(col(mcol + 6), col(mcol), -float(CWM), None,
                                         ALU.mult, ALU.bypass, [t_mh])
                  t_ct = ts(wacc[:, 6 + p:7 + p], rs, col(mcol + 6), None,
                            ALU.add, ALU.bypass, [t_rd, nmhat_tok[wi]])
                  acc_last[2 + p] = t_ct
                  t_ac = r.op("ve", lambda e, wi=wi, p=p: e.tensor_tensor(
                      wacc[:, wi:wi + 1], wacc[:, wi:wi + 1], wacc[:, 6 + p:7 + p],
                      ALU.add), [t_ct, acc_last[wi]])
                  acc_last[wi] = t_ac
              mean_jobs.append(_mean_job)

    def finish_wmean(wi, s_c, c_c, nc_c, i_c):
        mcol = W_M1 if wi == 0 else W_M2
        deps = [acc_last[wi]] + pgw(ps1, 0, 512)
        t_mm = mm(ps1[0:1, 0:1], ones_col_f[:], wacc[:, wi:wi + 1], True, True,
                  deps + [tok_c["ones_col_f"]])
        for b in pg(ps1, 0, 512):
            b.did_write(t_mm)
        t_c0 = r.op("ve", lambda e: e.tensor_scalar(
            rowbuf[0:1, 0:1], ps1[0:1, 0:1], 1.0 / (D * INNER), None,
            ALU.mult, ALU.bypass), [t_mm])
        for b in pg(ps1, 0, 512):
            b.did_read(t_c0)
        t_c1 = r.op("ve", lambda e, mcol=mcol: e.tensor_scalar(
            rowbuf[0:1, 0:1], rowbuf[0:1, 0:1], col(mcol)[0:1, :], 1e-5,
            ALU.add, ALU.max), [t_c0, mhat_tok[wi]])
        t_sp = mm(ps1[:, 0:1], onesr_f[:], rowbuf[0:1, 0:1], True, True,
                  [t_c1, tok_c["onesr_f"]] + pgw(ps1, 0, 512))
        for b in pg(ps1, 0, 512):
            b.did_write(t_sp)
        t_mc = r.op("ve", lambda e: e.tensor_copy(col(i_c), ps1[:, 0:1]), [t_sp],
                    is_async=True)
        for b in pg(ps1, 0, 512):
            b.did_read(t_mc)
        t_s0 = act(col(s_c), col(i_c), AF.Ln, deps=[t_mc])
        t_s1 = act(col(s_c), col(s_c), AF.Exp, scale=-1.0, deps=[t_s0])
        t_u = tt(stp[:, 44 + wi:45 + wi], col(i_c), col(s_c), ALU.mult, [t_s1, t_mc])
        t_v = ts(stp[:, 44 + wi:45 + wi], stp[:, 44 + wi:45 + wi], -1.0, 2.0,
                 ALU.mult, ALU.add, [t_u])
        t_s = tt(col(s_c), col(s_c), stp[:, 44 + wi:45 + wi], ALU.mult, [t_v])
        t_cc = ts(col(c_c), col(i_c), CLIP_W, None, ALU.mult, ALU.bypass, [t_mc])
        t_nc = ts(col(nc_c), col(c_c), -1.0, None, ALU.mult, ALU.bypass, [t_cc])
        return [t_s, t_cc, t_nc, t_mc]

    qh_b = AR_B[:, 4096:6144]
    kh_b = AR_B[:, 6144:8192]
    vh_b = AR_B[:, 8192:10240]
    vtm_b = AR_B[:, 10240:12288]
    sqr_b = AR_B[:, 12288:14336]
    ah_b = AR_B[:, 14336:16384]
    pt_b = [AR_B[:, 16384:17408], AR_B[:, 17408:18432]]
    nrm_pool = AR_B[:, 18432:18434]
    rown = Bf[0:1, 9220:9228]
    bcast_sb = Bf[:, 9472:10496]
    rsum_row = rowbuf[0:1, 0:1024]

    g_qh, g_kh, g_vh, g_vtm, g_sqr, g_ah = G(), G(), G(), G(), G(), G()
    g_pt = [G(), G()]
    g_nrmp, g_rown, g_rsum, g_bcsb = G(), G(), G(), G()
    g_bias_h = G()

    stage_c = AR_C[:, 0:4096]
    stage_s = AR_C[:, 4096:8192]
    wslot = [AR_C[:, 8192 + i * 4096:8192 + (i + 1) * 4096] for i in range(6)]
    seedC = g_xn[0].wr() + g_xn[1].wr() + g_xq1t[0].wr() + g_xq1t[1].wr() + \
        g_x[0].wr() + g_x[1].wr() + g_tmp.wr()
    g_stage = G(*seedC)
    g_wslot = [G(*seedC) for _ in range(6)]

    CW = {"q": ("cq", "sq"), "k": ("ck", "sk"), "v": ("cv", "sv")}
    slice_tok = {}
    sidx_box = [0]

    def prep_group(g2):
        for wn in ("q", "k", "v"):
            sidx = sidx_box[0]
            slot = sidx % 6
            cdn, sdn = CW[wn]
            t_c = dma("gd", stage_c[:].rearrange("p (dc f) -> p dc f", f=256),
                      din[cdn][:, g2 * 256:(g2 + 1) * 256]
                      .rearrange("(dc p) f -> p dc f", p=128),
                      deps=g_stage.wr(), lane="Lstc")
            t_s = dma("gd", stage_s[:].rearrange("p (dc f) -> p dc f", f=256),
                      din[sdn][:, g2 * 256:(g2 + 1) * 256]
                      .rearrange("(dc p) f -> p dc f", p=128),
                      deps=g_stage.wr(), lane="Lsts")
            g_stage.writer = [t_c, t_s]
            g_stage.readers = {}
            t_q = tt(wslot[slot], stage_c, stage_s, ALU.mult,
                     [t_c, t_s] + g_wslot[slot].wr())
            g_stage.did_read(t_q)
            g_wslot[slot].did_write(t_q)
            slice_tok[(g2, wn)] = (slot, t_q)
            sidx_box[0] += 1

    prep_group(0)
    prep_group(1)

    ps_sc = [ps0[:, 0:1024], ps0[:, 1024:2048]]
    ps_av = ps1[:, 0:1024]
    ps_sum = ps1[0:1, 1024:2048]
    ps_nrm = ps1[:, 0:16]
    ps_t128 = ps1[0:1, 32:96].bitcast(BF16)
    ps_tiny = ps1[:, 16:17]
    ps_vt = ps0[:, 0:1024].bitcast(BF16)

    attn_store_toks = {}
    mj = 0
    for h in range(H):
        g2, hl = h // 2, h % 2
        if hl == 0 and 1 <= g2 < 7:
            prep_group(g2 + 1)
        for wn, dstb, gdst, bcol in (("q", qh_b, g_qh, 0), ("k", kh_b, g_kh, 16)):
            slot, t_wq = slice_tok[(g2, wn)]
            wv = wslot[slot][:].rearrange("p (dc f) -> p dc f", f=256)
            for tq in range(4):
                par = tq % 2
                bank = pg(ps0, par * 1024, par * 1024 + 512)
                mms = []
                for dc in range(ND):
                    mms.append(mm(ps_sc[par][:, 0:512],
                                  wv[:, dc, hl * 128:(hl + 1) * 128],
                                  AR_A[:, dc * T + tq * 512:dc * T + tq * 512 + 512],
                                  dc == 0, dc == ND - 1,
                                  ([t_wq, ev_a_toks[4 * tq + 3]] +
                                   [bb for b in bank for bb in b.wr()]) if dc == 0 else []))
                    g_A.did_read(mms[-1])
                g_wslot[slot].did_read(mms[-1])
                for b in bank:
                    b.did_write(mms[-1])
                t_ev = r.op("sc", lambda e, dstb=dstb, tq=tq, par=par, h=h, bcol=bcol:
                            e.activation(dstb[:, tq * 512:(tq + 1) * 512],
                                         ps_sc[par][:, 0:512], AF.Identity,
                                         bias=bqkv[:, bcol + h:bcol + h + 1]),
                            [mms[-1]] + tok_bqkv + (gdst.wr() if tq == 0 else []))
                for b in bank:
                    b.did_read(t_ev)
                if tq == 0:
                    gdst.did_write(t_ev)
                else:
                    gdst.writer.append(t_ev)
        def pnorm(src_g, src_b, out_i):
            t_sq = tt(sqr_b, src_b, src_b, ALU.mult, src_g.rd() + g_sqr.wr())
            src_g.did_read(t_sq)
            g_sqr.did_write(t_sq)
            nm = []
            for tti in range(NT):
                nm.append(mm(ps_nrm[:, tti:tti + 1],
                             sqr_b[:, tti * 128:(tti + 1) * 128], ones_col[:],
                             True, True,
                             [t_sq, tok_c["ones_col"]] +
                             (pgw(ps1, 0, 512) if tti == 0 else [])))
                g_sqr.did_read(nm[-1])
            for b in pg(ps1, 0, 512):
                b.did_write(nm[-1])
            t_red = red(nrm_pool[:, out_i:out_i + 1], ps_nrm[:], ALU.max,
                        nm + g_nrmp.wr())
            for b in pg(ps1, 0, 512):
                b.did_read(t_red)
            g_nrmp.did_write(t_red)
            t_t = r.op("te", lambda e, out_i=out_i: e.transpose(
                ps_t128, nrm_pool[:, out_i:out_i + 1], ident[:]),
                [t_red] + pgw(ps1, 0, 512))
            g_nrmp.did_read(t_t)
            for b in pg(ps1, 0, 512):
                b.did_write(t_t)
            t_mx = red(rown[0:1, out_i:out_i + 1], ps_t128, ALU.max,
                       [t_t] + g_rown.wr())
            for b in pg(ps1, 0, 512):
                b.did_read(t_mx)
            g_rown.did_write(t_mx)
            return t_mx

        t_mq = pnorm(g_qh, qh_b, 0)
        t_mk = pnorm(g_kh, kh_b, 1)
        t_pm = tt(rown[0:1, 2:3], rown[0:1, 0:1], rown[0:1, 1:2], ALU.mult, [t_mq, t_mk])
        t_l1 = act(rown[0:1, 2:3], rown[0:1, 2:3], AF.Ln, deps=[t_pm])
        t_l2 = act(rown[0:1, 2:3], rown[0:1, 2:3], AF.Exp, scale=0.5, deps=[t_l1])
        g_rown.did_write(t_l2)
        t_sp = mm(ps_tiny, onesr_f[:], rown[0:1, 2:3], True, True,
                  [t_l2] + pgw(ps1, 0, 512))
        g_rown.did_read(t_sp)
        for b in pg(ps1, 0, 512):
            b.did_write(t_sp)
        t_bh = ts(col(BIAS_H), ps_tiny, -RSQ128, None, ALU.mult, ALU.bypass,
                  [t_sp] + g_bias_h.wr())
        for b in pg(ps1, 0, 512):
            b.did_read(t_bh)
        g_bias_h.did_write(t_bh)

        for wn, dstb, gdst, bcol in (("v", vh_b, g_vh, 32),):
            slot, t_wq = slice_tok[(g2, wn)]
            wv = wslot[slot][:].rearrange("p (dc f) -> p dc f", f=256)
            for tq in range(4):
                par = tq % 2
                bank = pg(ps0, par * 1024, par * 1024 + 512)
                mms = []
                for dc in range(ND):
                    mms.append(mm(ps_sc[par][:, 0:512],
                                  wv[:, dc, hl * 128:(hl + 1) * 128],
                                  AR_A[:, dc * T + tq * 512:dc * T + tq * 512 + 512],
                                  dc == 0, dc == ND - 1,
                                  ([t_wq, ev_a_toks[4 * tq + 3]] +
                                   [bb for b in bank for bb in b.wr()]) if dc == 0 else []))
                    g_A.did_read(mms[-1])
                g_wslot[slot].did_read(mms[-1])
                for b in bank:
                    b.did_write(mms[-1])
                t_ev = r.op("sc", lambda e, dstb=dstb, tq=tq, par=par, h=h, bcol=bcol:
                            e.activation(dstb[:, tq * 512:(tq + 1) * 512],
                                         ps_sc[par][:, 0:512], AF.Identity,
                                         bias=bqkv[:, bcol + h:bcol + h + 1]),
                            [mms[-1]] + tok_bqkv + (gdst.wr() if tq == 0 else []))
                for b in bank:
                    b.did_read(t_ev)
                if tq == 0:
                    gdst.did_write(t_ev)
                else:
                    gdst.writer.append(t_ev)
        tps = []
        for kt in range(NT):
            tps.append(tp(ps_vt[:, kt * 128:(kt + 1) * 128],
                          vh_b[:, kt * 128:(kt + 1) * 128],
                          g_vh.rd() + (pgw(ps0, 0, 1024) if kt == 0 else [])))
            g_vh.did_read(tps[-1])
        for b in pg(ps0, 0, 1024):
            b.did_write(tps[-1])
        t_vtm = r.op("ve", lambda e: e.tensor_copy(vtm_b[:], ps_vt[:]),
                     tps + g_vtm.wr(), is_async=True)
        for b in pg(ps0, 0, 1024):
            b.did_read(t_vtm)
        g_vtm.did_write(t_vtm)

        for _ in range(12):
            if mj < len(mean_jobs):
                mean_jobs[mj]()
                mj += 1
        ah_parts = []
        for qh2 in range(2):
            av_last = []
            sum_last = []
            exp_par0 = None
            for kt in range(NT):
                par = kt % 2
                bank_s = pg(ps0, par * 1024, (par + 1) * 1024)
                m1 = mm(ps_sc[par][:, 0:512], kh_b[:, kt * 128:(kt + 1) * 128],
                        qh_b[:, qh2 * 1024:qh2 * 1024 + 512], True, True,
                        g_kh.rd() + g_qh.rd() + [bb for b in bank_s for bb in b.wr()])
                m2 = mm(ps_sc[par][:, 512:1024], kh_b[:, kt * 128:(kt + 1) * 128],
                        qh_b[:, qh2 * 1024 + 512:qh2 * 1024 + 1024], True, True, [])
                g_kh.did_read(m2)
                g_qh.did_read(m2)
                for b in bank_s:
                    b.did_write(m2)
                t_exp = r.op("sc", lambda e, par=par: e.activation(
                    pt_b[par], ps_sc[par], AF.Exp, bias=col(BIAS_H), scale=RSQ128),
                    [m1, m2] + g_bias_h.rd() + g_pt[par].wr())
                g_bias_h.did_read(t_exp)
                for b in bank_s:
                    b.did_read(t_exp)
                g_pt[par].did_write(t_exp)
                if par == 0:
                    exp_par0 = t_exp
                s1_ = mm(ps_sum[:, 0:512], ones_col[:], pt_b[par][:, 0:512],
                         kt == 0, kt == NT - 1,
                         [t_exp] + (pgw(ps1, 1024, 2048) if kt == 0 else []))
                s2_ = mm(ps_sum[:, 512:1024], ones_col[:], pt_b[par][:, 512:1024],
                         kt == 0, kt == NT - 1, [])
                a1 = mm(ps_av[:, 0:512], vtm_b[:, kt * 128:(kt + 1) * 128],
                        pt_b[par][:, 0:512], kt == 0, kt == NT - 1,
                        [t_exp] + g_vtm.rd() + (pgw(ps1, 0, 1024) if kt == 0 else []))
                a2 = mm(ps_av[:, 512:1024], vtm_b[:, kt * 128:(kt + 1) * 128],
                        pt_b[par][:, 512:1024], kt == 0, kt == NT - 1, [])
                g_vtm.did_read(a2)
                g_pt[par].did_read(a2)
                if kt == NT - 1:
                    sum_last = [s1_, s2_]
                    av_last = [a1, a2]
                    for b in pg(ps1, 1024, 2048):
                        b.did_write(s2_)
                    for b in pg(ps1, 0, 1024):
                        b.did_write(a2)
            t_rc0 = act(rsum_row, ps_sum[:], AF.Ln,
                        deps=sum_last + g_rsum.wr())
            t_rc = act(rsum_row, rsum_row, AF.Exp, scale=-1.0, deps=[t_rc0])
            for b in pg(ps1, 1024, 2048):
                b.did_read(t_rc)
            g_rsum.did_write(t_rc)
            bcs = []
            for c in range(2):
                bcs.append(mm(ps_sc[0][:, c * 512:(c + 1) * 512], onesr_f[:],
                              rsum_row[0:1, c * 512:(c + 1) * 512], True, True,
                              [t_rc] + ([exp_par0] + pgw(ps0, 0, 1024)
                                        if c == 0 else [])))
                g_rsum.did_read(bcs[-1])
            for b in pg(ps0, 0, 1024):
                b.did_write(bcs[-1])
            t_bc2 = r.op("ve", lambda e: e.tensor_copy(bcast_sb, ps_sc[0]),
                         bcs + g_bcsb.wr(), is_async=True)
            for b in pg(ps0, 0, 1024):
                b.did_read(t_bc2)
            g_bcsb.did_write(t_bc2)
            t_ev = tt(ah_b[:, qh2 * 1024:(qh2 + 1) * 1024], ps_av, bcast_sb, ALU.mult,
                      av_last + [t_bc2] + (g_ah.wr() if qh2 == 0 else []))
            r.ops[t_ev]["async"] = True
            g_bcsb.did_read(t_ev)
            for b in pg(ps1, 0, 1024):
                b.did_read(t_ev)
            ah_parts.append(t_ev)
        g_ah.writer = ah_parts
        g_ah.readers = {}
        t_st = dma("sd", attnh_d[h], ah_b[:], deps=ah_parts, lane="Lah")
        g_ah.did_read(t_st)
        attn_store_toks[h] = t_st

    while mj < len(mean_jobs):
        mean_jobs[mj]()
        mj += 1
    w1_ready = finish_wmean(0, W_S1, W_C1, W_NC1, W_I1)
    w2_ready = finish_wmean(1, W_S2, W_C2, W_NC2, W_I2)
    if wq_dbg is not None:
        dma("sd", wq_dbg[:], stp[:, 32:40], deps=w1_ready + w2_ready, lane="Ldbg")

    wo_stage_c = AR_B[:, 4096:6144]
    wo_stage_s = AR_B[:, 6144:8192]
    ap_stage = [AR_B[:, 8192:10240], AR_B[:, 10240:12288]]
    g_wost = G(*(g_qh.wr() + g_kh.wr()))
    g_apst = [G(*(g_vh.wr())), G(*(g_vtm.wr()))]
    g_wo = G(*(g_A.wr()))
    g_arb = G(*(g_stage.wr() + [t for i in range(6) for t in g_wslot[i].wr()]))
    t_bo_hilo = make_hilo(hilo_bo, din["bo"], D, Cf[0:1, 0:2048],
                          AR_C[0:1, 4096:6144], g_arb.wr())
    g_arb.did_read(t_bo_hilo)

    wo_toks = []
    rb_toks = []
    g_wost_rb = g_wost
    for fc in range(ND):
        t_c = dma("gd", wo_stage_c, din["co"][fc * 128:(fc + 1) * 128, :],
                  deps=g_wost.wr(), lane="Lwoc")
        t_s = dma("gd", wo_stage_s, din["so"][fc * 128:(fc + 1) * 128, :],
                  deps=g_wost.wr(), lane="Lwos")
        g_wost.writer = [t_c, t_s]
        g_wost.readers = {}
        t_q = tt(AR_A[:, fc * 2048:(fc + 1) * 2048], wo_stage_c, wo_stage_s,
                 ALU.mult, [t_c, t_s] + (g_wo.wr() if fc == 0 else []))
        g_wost.did_read(t_q)
        if fc == 0:
            g_wo.did_write(t_q)
        else:
            g_wo.writer.append(t_q)
        wo_toks.append(t_q)
        rb_toks.append(dma("sy", AR_C[:, fc * 2048:(fc + 1) * 2048], attnh_d[fc],
                           deps=[attn_store_toks[fc], t_bo_hilo] +
                           (g_arb.wr() if fc == 0 else []), lane="Larb"))
        if fc == 0:
            g_arb.did_write(rb_toks[-1])
        else:
            g_arb.writer.append(rb_toks[-1])

    ps_full = [ps0, ps1]
    aproj_store = {}
    for tti in range(NT):
        par = tti % 2
        psf = ps_full[par]
        banks = pg(psf, 0, 2048)
        mms = []
        for fc in range(ND):
            for c in range(4):
                dep0 = []
                if c == 0:
                    dep0 = [rb_toks[-1], wo_toks[fc]]
                    if fc == 0:
                        dep0 += [bb for b in banks for bb in b.wr()]
                mms.append(mm(psf[:, c * 512:(c + 1) * 512],
                              AR_C[:, fc * 2048 + tti * 128:fc * 2048 + tti * 128 + 128],
                              AR_A[:, fc * 2048 + c * 512:fc * 2048 + c * 512 + 512],
                              fc == 0, False, dep0))
            g_arb.did_read(mms[-1])
            g_wo.did_read(mms[-1])
        for c in range(4):
            mms.append(mm(psf[:, c * 512:(c + 1) * 512], ones2[:],
                          hilo_bo[:, c * 512:(c + 1) * 512], False, True,
                          [t_bo_hilo, tok_c["ones2"]] if c == 0 else []))
        for b in banks:
            b.did_write(mms[-1])
        t_ev = r.op("sc", lambda e, par=par, psf=psf: e.activation(
            ap_stage[par], psf[:], AF.Identity), [mms[-1]] + g_apst[par].wr())
        for b in banks:
            b.did_read(t_ev)
        g_apst[par].did_write(t_ev)
        t_st2 = dma("sd", aproj_d[tti], ap_stage[par], deps=[t_ev],
                    lane="Lap%d" % par)
        g_apst[par].did_read(t_st2)
        aproj_store[tti] = t_st2

    w1_stage = [Cf[:, 0:8192], Cf[:, 8192:16384]]
    w1q = [AR_A[:, 0:8192], AR_A[:, 8192:16384]]
    Af = AR_A[:].bitcast(F32)
    s_stage = [Af[:, 8192 + i * 512:8192 + (i + 1) * 512] for i in range(4)]
    t1_stage = [Af[:, 10240 + i * 512:10240 + (i + 1) * 512] for i in range(2)]
    b1bc = [Af[:, 11264:11776], Af[:, 11776:12288]]
    g_w1st = [G(*(g_arb.wr())), G(*(g_arb.wr()))]
    g_w1q = [G(*(g_wo.wr())), G(*(g_wo.wr()))]
    g_sst = [G(*(g_wo.wr())) for _ in range(4)]
    g_t1 = [G(*(g_wo.wr())), G(*(g_wo.wr()))]
    g_b1bc = [G(*(g_wo.wr())), G(*(g_wo.wr()))]
    g_scv1 = G()
    g_Brl = G(*(g_qh.wr() + g_kh.wr() + g_vh.wr() + g_vtm.wr() + g_sqr.wr() +
                g_ah.wr() + g_pt[0].wr() + g_pt[1].wr() + g_wost.wr() +
                g_apst[0].wr() + g_apst[1].wr() + g_mst[0].wr() + g_mst[1].wr() + g_rown.wr() +
                g_nrmp.wr()))
    rl_toks = {}
    for tti in range(NT):
        rl_toks[tti] = dma("sy", AR_B[:, tti * 2048:(tti + 1) * 2048], xq1_d[tti],
                           deps=[xq1_store[tti]] + (g_Brl.wr() if tti < 2 else []),
                           lane="Lrl")
        if tti == 0:
            g_Brl.did_write(rl_toks[tti])
        else:
            g_Brl.writer.append(rl_toks[tti])
    scv_toks = []
    for t_ in range(NT):
        scv_toks.append(tt(scv1[:, t_:t_ + 1], invq1[:, t_:t_ + 1], col(W_I1),
                           ALU.mult, g_invq1.wr() + w1_ready + g_scv1.wr()))
        g_scv1.writer.append(scv_toks[-1])
    g_invq1.did_read(scv_toks[-1])

    ps_fp = [ps0[:, 0:512], ps0[:, 512:1024]]
    ps_fb = [pg(ps0, 0, 512), pg(ps0, 512, 1024)]
    s_store = {}
    for sbi in range(16):
        par = sbi % 2
        t_ld = dma("sy", w1_stage[par][:].rearrange("p (dc f) -> p dc f", f=512),
                   din["w1t"][:, sbi * 512:(sbi + 1) * 512]
                   .rearrange("(dc p) f -> p dc f", p=128),
                   deps=g_w1st[par].wr(), lane="Lw1%d" % par)
        g_w1st[par].did_write(t_ld)
        t_q1 = r.op("ve", lambda e, par=par: e.tensor_scalar(
            w1_stage[par], w1_stage[par], col(W_C1), col(W_NC1), ALU.min, ALU.max),
            [t_ld] + w1_ready)
        t_q2 = r.op("ve", lambda e, par=par: e.tensor_scalar(
            w1_stage[par], w1_stage[par], col(W_S1), MAGIC, ALU.mult, ALU.add), [t_q1])
        t_q3 = r.op("ve", lambda e, par=par: e.tensor_scalar(
            w1q[par], w1_stage[par], -MAGIC, None, ALU.add, ALU.bypass),
            [t_q2] + g_w1q[par].wr())
        g_w1st[par].did_write(t_q2)
        g_w1st[par].did_read(t_q3)
        g_w1q[par].did_write(t_q3)
        bpar = sbi % 2
        t_bb = dma("sy", b1bc[bpar],
                   din["b1"][sbi * 512:(sbi + 1) * 512]
                   .rearrange("(a n) -> a n", a=1).partition_broadcast(128).squeeze(1),
                   deps=g_b1bc[bpar].wr(), lane="Lb1%d" % bpar)
        g_b1bc[bpar].did_write(t_bb)
        for tti in range(NT):
            fpar = tti % 2
            mms = []
            for dc in range(ND):
                mms.append(mm(ps_fp[fpar],
                              AR_B[:, tti * 2048 + dc * 128:tti * 2048 + dc * 128 + 128],
                              w1q[par][:, dc * 512:(dc + 1) * 512],
                              dc == 0, dc == ND - 1,
                              ([t_q3, rl_toks[NT - 1]] +
                               [bb for b in ps_fb[fpar] for bb in b.wr()])
                              if dc == 0 else []))
            g_Brl.did_read(mms[-1])
            g_w1q[par].did_read(mms[-1])
            for b in ps_fb[fpar]:
                b.did_write(mms[-1])
            t_sc = r.op("ve", lambda e, fpar=fpar, tti=tti, bpar=bpar:
                        e.scalar_tensor_tensor(t1_stage[fpar], ps_fp[fpar],
                                               scv1[:, tti:tti + 1], b1bc[bpar],
                                               ALU.mult, ALU.add),
                        [mms[-1], t_bb, scv_toks[tti]] + g_t1[fpar].wr(),
                        is_async=True)
            g_b1bc[bpar].did_read(t_sc)
            g_scv1.did_read(t_sc)
            for b in ps_fb[fpar]:
                b.did_read(t_sc)
            g_t1[fpar].did_write(t_sc)
            sslot = tti % 4
            t_ev = r.op("sc", lambda e, fpar=fpar, sslot=sslot: e.activation(
                s_stage[sslot], t1_stage[fpar], AF.Silu), [t_sc] + g_sst[sslot].wr())
            g_t1[fpar].did_read(t_ev)
            g_sst[sslot].did_write(t_ev)
            t_st3 = dma("sd", s_d[tti][:, sbi * 512:(sbi + 1) * 512], s_stage[sslot],
                        deps=[t_ev], lane="Lss%d" % sslot)
            g_sst[sslot].did_read(t_st3)
            s_store[(tti, sbi)] = t_st3

    g_bc_t = Cf[:, 0:8192]
    b_bc_t = Cf[:, 8192:16384]
    s_rbs = [Af[:, 0:8192], Af[:, 8192:16384]]
    xq2_tm = AR_B[:, 0:8192]
    xq2_fm = AR_B[:, 8192:16384]
    zbuf = Bf[:, 8192:16384]
    ps_tp2 = [ps1[:, 0:512].bitcast(BF16), ps1[:, 512:1024].bitcast(BF16)]

    g_gb = G(*(g_w1st[0].wr() + g_w1st[1].wr()))
    seed_srb = (g_w1q[0].wr() + g_w1q[1].wr() + g_sst[0].wr() + g_sst[1].wr() +
                g_sst[2].wr() + g_sst[3].wr() + g_t1[0].wr() + g_t1[1].wr() +
                g_b1bc[0].wr() + g_b1bc[1].wr())
    g_srbs = [G(*seed_srb), G(*seed_srb)]
    g_z = G(*(g_Brl.wr()))
    g_xq2tm = G(*(g_Brl.wr()))
    g_xq2fm = G(*(g_Brl.wr()))
    g_mstats = G()
    g_scv2 = G()

    t_gbc = dma("sy", g_bc_t[:], din["g_ln"][:].rearrange("(a n) -> a n", a=1)
                .partition_broadcast(128).squeeze(1), deps=g_gb.wr(), lane="Lgb")
    t_bbc = dma("sy", b_bc_t[:], din["bt_ln"][:].rearrange("(a n) -> a n", a=1)
                .partition_broadcast(128).squeeze(1), deps=g_gb.wr(), lane="Lgb")
    g_gb.writer = [t_gbc, t_bbc]
    g_gb.readers = {}

    MT = 80
    xq2_store = {}
    for grp in range(4):
        for j in range(4):
            tti = grp * 4 + j
            pb = tti % 2
            s_rb = s_rbs[pb]
            g_srb = g_srbs[pb]
            t_rb = dma("sy", s_rb, s_d[tti],
                       deps=g_srb.wr() + [s_store[(tti, sbi)] for sbi in range(16)],
                       lane="Lsrb" + str(pb))
            g_srb.did_write(t_rb)
            t_gl = r.op("sc", lambda e, s_rb=s_rb: e.activation(
                s_rb, s_rb, AF.Gelu, accum_out=col(MT)), [t_rb] +
                g_mstats.wr())
            g_srb.writer.append(t_gl)
            t_sq = r.op("sc", lambda e, s_rb=s_rb: e.activation(
                zbuf, s_rb, AF.Square, accum_out=col(MT + 1)),
                [t_gl] + g_z.wr())
            g_srb.did_read(t_sq)
            g_z.did_write(t_sq)
            t_mu = ts(col(MT), col(MT), 1.0 / INNER, None, ALU.mult, ALU.bypass, [t_gl])
            t_e2 = ts(col(MT + 1), col(MT + 1), 1.0 / INNER, None, ALU.mult,
                      ALU.bypass, [t_sq])
            t_mm2 = tt(col(MT + 2), col(MT), col(MT), ALU.mult, [t_mu])
            t_ag = tt(col(MT + 1), col(MT + 1), col(MT + 2), ALU.subtract,
                      [t_e2, t_mm2])
            t_u = ts(col(MT + 4), stp[:, MT + 1:MT + 2], 1.0, EPS, ALU.mult, ALU.add, [t_ag])
            t_l = act(col(MT + 4), col(MT + 4), AF.Ln, deps=[t_u])
            t_r = act(col(MT + 5), col(MT + 4), AF.Exp, scale=-0.5, deps=[t_l])
            t_bx = ts(col(MT + 6), stp[:, MT:MT + 1], col(MT + 5), -1.0,
                      ALU.mult, ALU.mult, [t_r])
            t_z = r.op("sc", lambda e, s_rb=s_rb: e.activation(
                zbuf, s_rb, AF.Identity, bias=col(MT + 6), scale=col(MT + 5)),
                [t_bx, t_gl] + g_z.wr())
            g_srb.did_read(t_z)
            g_z.did_write(t_z)
            t_z2 = tt(zbuf, zbuf, g_bc_t, ALU.mult, [t_z] + g_gb.rd())
            g_gb.did_read(t_z2)
            t_z3 = tt(zbuf, zbuf, b_bc_t, ALU.add, [t_z2] + g_gb.rd())
            g_gb.did_read(t_z3)
            g_z.did_write(t_z3)
            t_zq = r.op("sc", lambda e: e.activation(
                xq2_tm, zbuf, AF.Square, accum_out=col(MT + 8)),
                [t_z3] + g_xq2tm.wr())
            g_z.did_read(t_zq)
            g_xq2tm.did_write(t_zq)
            g_mstats.did_write(t_zq)
            t_am = red(col(MT + 7), zbuf, ALU.max, [t_z3], absv=True)
            t_u2 = ts(col(MT + 8), col(MT + 8), 1.0 / INNER, EPS, ALU.mult,
                      ALU.add, [t_zq])
            t_l2 = act(col(MT + 8), col(MT + 8), AF.Ln, deps=[t_u2])
            t_r2 = act(col(MT + 9), col(MT + 8), AF.Exp, scale=-0.5, deps=[t_l2])
            t_mm_ = tt(col(MT + 10), col(MT + 7), col(MT + 9), ALU.mult, [t_am, t_r2])
            t_cl = ts(col(MT + 10), col(MT + 10), 1e-5, None, ALU.max, ALU.bypass, [t_mm_])
            t_rc = act(col(MT + 11), col(MT + 10), AF.Ln, deps=[t_cl])
            t_s2 = act(col(MT + 11), col(MT + 11), AF.Exp, scale=-1.0,
                       bias=4.844187086458591, deps=[t_rc])
            t_c2 = tt(col(MT + 12), col(MT + 9), col(MT + 11), ALU.mult, [t_s2])
            t_i2 = ts(col(MT + 13), col(MT + 10), 1.0 / 127.0, None, ALU.mult,
                      ALU.bypass, [t_cl])
            if mid_dbg is not None:
                t_mdbg = dma("sd", mid_dbg[tti], stp[:, 80:96],
                             deps=[t_c2, t_i2, t_r, t_bx], lane="Ldbg")
                g_mstats.readers[("mdbg", tti)] = t_mdbg
            t_p1 = ts(zbuf, zbuf, col(MT + 12), MAGIC, ALU.mult, ALU.add,
                      [t_c2, t_am, t_zq])
            g_z.did_write(t_p1)
            t_p2 = r.op("sc", lambda e: e.activation(
                xq2_tm, zbuf, AF.Identity, bias=-MAGIC),
                [t_p1] + g_xq2tm.wr())
            g_z.did_read(t_p2)
            g_xq2tm.did_write(t_p2)
            t_iv2 = ts(invq2[:, tti:tti + 1], col(MT + 13), 1.0, None,
                       ALU.mult, ALU.bypass, [t_i2] + g_invq2.wr())
            g_invq2.writer.append(t_iv2)
            g_mstats.readers[("iv2", tti)] = t_iv2
            ev_toks = []
            for b2 in range(8):
                par2 = b2 % 2
                bank = pg(ps1, par2 * 512, (par2 + 1) * 512)
                tbs = []
                for u in range(8):
                    ic = b2 * 8 + u
                    tbs.append(tp(ps_tp2[par2][:, u * 128:(u + 1) * 128],
                                  xq2_tm[:, ic * 128:(ic + 1) * 128],
                                  [t_p2] + ([bb for b in bank for bb in b.wr()]
                                            if u == 0 else [])))
                    g_xq2tm.did_read(tbs[-1])
                for b in bank:
                    b.did_write(tbs[-1])
                t_ev = r.op("ve", lambda e, b2=b2, par2=par2: e.tensor_copy(
                    xq2_fm[:, b2 * 1024:(b2 + 1) * 1024], ps_tp2[par2]),
                    tbs + (g_xq2fm.wr() if b2 == 0 else []), is_async=True)
                for b in bank:
                    b.did_read(t_ev)
                ev_toks.append(t_ev)
            g_xq2fm.writer = ev_toks
            g_xq2fm.readers = {}
            t_st4 = dma("sd", xq2_d[tti], xq2_fm[:], deps=ev_toks, lane="Lq2")
            g_xq2fm.did_read(t_st4)
            xq2_store[tti] = t_st4

    w2q_ar = [AR_A, AR_C]
    xq2_rb = [AR_B[:, 0:8192], AR_B[:, 8192:16384]]
    ap_rb = [AR_B[:, 16384:16896], AR_B[:, 16896:17408]]
    x_rb = [Bf[:, 8704:9216], Bf[:, 9216:9728]]
    t1_st = [Bf[:, 9728:10240], Bf[:, 10240:10752]]
    out_stg = [Bf[:, 10752:11264], Bf[:, 11264:11776]]
    b2bc = Bf[:, 11776:12288]
    w2_stage = Bf[:, 12288:16384]

    g_w2st = G(*(g_z.wr() + g_xq2tm.wr()))
    g_w2q = [G(*(g_srbs[0].wr() + g_srbs[1].wr())),
             G(*(g_gb.wr()))]
    g_xq2rb = [G(*(g_xq2tm.wr())), G(*(g_xq2fm.wr()))]
    g_eio = [G(*(g_z.wr())), G(*(g_z.wr()))]
    g_b2bc = G(*(g_z.wr()))
    scv2_toks = []
    for t_ in range(NT):
        scv2_toks.append(tt(scv2[:, t_:t_ + 1], invq2[:, t_:t_ + 1], col(W_I2),
                            ALU.mult, g_invq2.wr() + w2_ready + g_scv2.wr()))
        g_scv2.writer.append(scv2_toks[-1])
    g_invq2.did_read(scv2_toks[-1])

    w2q_toks = {}

    def stage_w2(oq):
        ar = w2q_ar[oq % 2]
        gq = g_w2q[oq % 2]
        toks = []
        for bt in range(8):
            t_ld = dma("sy", w2_stage[:].rearrange("p (ic f) -> p ic f", f=512),
                       din["w2t"][bt * 1024:(bt + 1) * 1024, oq * 512:(oq + 1) * 512]
                       .rearrange("(ic p) f -> p ic f", p=128),
                       deps=g_w2st.wr(), lane="Lw2s")
            g_w2st.did_write(t_ld)
            t_q1 = r.op("ve", lambda e: e.tensor_scalar(
                w2_stage, w2_stage, col(W_C2), col(W_NC2), ALU.min, ALU.max),
                [t_ld] + w2_ready)
            t_q2 = r.op("ve", lambda e: e.tensor_scalar(
                w2_stage, w2_stage, col(W_S2), MAGIC, ALU.mult, ALU.add), [t_q1])
            t_q3 = r.op("ve", lambda e, bt=bt, ar=ar: e.tensor_scalar(
                ar[:, bt * 4096:(bt + 1) * 4096], w2_stage, -MAGIC, None,
                ALU.add, ALU.bypass), [t_q2] + (gq.wr() if bt == 0 else []))
            g_w2st.did_write(t_q2)
            g_w2st.did_read(t_q3)
            if bt == 0:
                gq.did_write(t_q3)
            else:
                gq.writer.append(t_q3)
            toks.append(t_q3)
        w2q_toks[oq] = toks

    stage_w2(0)
    out_stores = []
    for oq in range(4):
        ar = w2q_ar[oq % 2]
        gq = g_w2q[oq % 2]
        qt = w2q_toks[oq]
        t_b2 = dma("sy", b2bc, din["b2"][oq * 512:(oq + 1) * 512]
                   .rearrange("(a n) -> a n", a=1).partition_broadcast(128).squeeze(1),
                   deps=g_b2bc.wr(), lane="Lb2")
        g_b2bc.did_write(t_b2)
        for tti in range(NT):
            par = tti % 2
            t_x2 = dma("sy", xq2_rb[par], xq2_d[tti],
                       deps=g_xq2rb[par].wr() + [xq2_store[tti]],
                       lane="Lx2%d" % par)
            g_xq2rb[par].did_write(t_x2)
            t_ap = dma("sy", ap_rb[par], aproj_d[tti][:, oq * 512:(oq + 1) * 512],
                       deps=g_eio[par].wr() + [aproj_store[tti]],
                       lane="Lapr%d" % par)
            t_xr = dma("sy", x_rb[par], xdv[tti * 128:(tti + 1) * 128,
                                            oq * 512:(oq + 1) * 512],
                       deps=g_eio[par].wr(), lane="Lxr%d" % par)
            mms = []
            for ic in range(NI):
                mms.append(mm(ps_fp[par], xq2_rb[par][:, ic * 128:(ic + 1) * 128],
                              ar[:, ic * 512:(ic + 1) * 512],
                              ic == 0, ic == NI - 1,
                              ([t_x2, qt[ic // 8]] +
                               [bb for b in ps_fb[par] for bb in b.wr()])
                              if ic == 0 else ([qt[ic // 8]] if ic % 8 == 0 else [])))
            g_xq2rb[par].did_read(mms[-1])
            gq.did_read(mms[-1])
            for b in ps_fb[par]:
                b.did_write(mms[-1])
            t_t1 = r.op("ve", lambda e, par=par, tti=tti:
                        e.scalar_tensor_tensor(t1_st[par], ps_fp[par],
                                               scv2[:, tti:tti + 1], ap_rb[par],
                                               ALU.mult, ALU.add),
                        [mms[-1], t_ap, scv2_toks[tti]], is_async=True)
            g_scv2.did_read(t_t1)
            for b in ps_fb[par]:
                b.did_read(t_t1)
            t_o1 = tt(t1_st[par], t1_st[par], x_rb[par], ALU.add, [t_t1, t_xr])
            t_o = tt(out_stg[par], t1_st[par], b2bc, ALU.add, [t_o1, t_b2])
            g_b2bc.did_read(t_o)
            t_so = dma("sd", out_d[tti * 128:(tti + 1) * 128, oq * 512:(oq + 1) * 512],
                       out_stg[par], deps=[t_o], lane="Lo%d" % par)
            g_eio[par].did_write(t_so)
            out_stores.append(t_so)
        if oq < 3:
            stage_w2(oq + 1)

    r.op("ve", lambda e: e.engine_nop(), out_stores[-2:])

    import contextlib
    stack = contextlib.ExitStack()
    with nc.Block() as block:
        r.emit(nc, block, stack)
    return nc, stack


def host_inputs(inputs):
    f32 = lambda a: np.ascontiguousarray(np.asarray(a, dtype=np.float32))
    rep = lambda s: np.ascontiguousarray(np.repeat(f32(s), 32, axis=0))
    shared = {
        "cq": f32(inputs["L0_attn_wq_codes"]), "sq": rep(inputs["L0_attn_wq_scales"]),
        "ck": f32(inputs["L0_attn_wk_codes"]), "sk": rep(inputs["L0_attn_wk_scales"]),
        "cv": f32(inputs["L0_attn_wv_codes"]), "sv": rep(inputs["L0_attn_wv_scales"]),
        "co": f32(inputs["L0_attn_wo_codes"]), "so": rep(inputs["L0_attn_wo_scales"]),
        "bq": f32(inputs["L0_attn_bq"]), "bk": f32(inputs["L0_attn_bk"]),
        "bv": f32(inputs["L0_attn_bv"]), "bo": f32(inputs["L0_attn_bo"]),
        "w1t": np.ascontiguousarray(f32(inputs["L0_ffn_w1"]).T),
        "b1": f32(inputs["L0_ffn_b1"]),
        "g_ln": f32(inputs["L0_ffn_ln_g"]), "bt_ln": f32(inputs["L0_ffn_ln_b"]),
        "w2t": np.ascontiguousarray(f32(inputs["L0_ffn_w2"]).T),
        "b2": f32(inputs["L0_ffn_b2"]),
        "ident": np.eye(128).astype(ml_dtypes.bfloat16),
        "ones_col": np.ones((128, 1), dtype=ml_dtypes.bfloat16),
        "ones2": np.ones((2, 128), dtype=ml_dtypes.bfloat16),
        "onesr_f": np.ones((1, 128), dtype=np.float32),
        "magicv": np.full((128, 1), MAGIC, dtype=np.float32),
        "ln127v": np.full((128, 1), 4.844187086458591, dtype=np.float32),
        "nmagicv": np.full((128, 1), -MAGIC, dtype=np.float32),
        "ones_col_f": np.ones((128, 1), dtype=np.float32),
    }
    x = f32(inputs["L0_attn_x"])
    return x, shared


def kernel(**inputs):
    x, shared = host_inputs(inputs)
    nc, stack = build_program()
    in_maps = [dict(shared, x=np.ascontiguousarray(x[c])) for c in range(8)]
    res = run_bass_kernel_spmd(nc, in_maps, core_ids=list(range(8)))
    out = np.stack([np.asarray(res.results[c]["out"]) for c in range(8)], axis=0)
    return out.astype(np.float32)


# version 18
# speedup vs baseline: 1.1336x; 1.0136x over previous
import numpy as np
import ml_dtypes
import concourse.bass as bass
import concourse.mybir as mybir
from concourse.bass_utils import run_bass_kernel_spmd

F32 = mybir.dt.float32
BF16 = mybir.dt.bfloat16
AF = mybir.ActivationFunctionType
ALU = mybir.AluOpType
AX = mybir.AxisListType

D = 2048
T = 2048
H = 16
INNER = 8192
ND = 16
NT = 16
NI = 64
MAGIC = float(np.float32(1.5 * 2 ** 23))
RSQ128 = float(1.0 / np.sqrt(np.float32(128.0)))
EPS = 1e-5
CLIP_W = 1.4995117

DEBUG_OUTS = set()
SERIALIZE_DMA = False
PHASE_LIMIT = None
P0_MIN = False
SERIALIZE_ALL = False
NO_ELIDE = {"vector", "gpsimd"}

COMPUTE_ENGS = {"sc", "ve", "te", "gc"}
STREAM_OF = {"sy": "sync", "sc": "scalar", "sd": "scalar",
             "ve": "vector", "te": "tensor", "gc": "gpsimd", "gd": "gpsimd"}


class Rec:
    def __init__(self):
        self.ops = []

    def op(self, eng, fn, deps=(), force=False, is_async=False, sem=None):
        i = len(self.ops)
        if SERIALIZE_ALL and i > 0:
            deps = list(deps) + [i - 1]
        if SERIALIZE_DMA and eng in ("sy", "sd", "gd"):
            prev = getattr(self, "_prev_dma_" + eng, None)
            if prev is not None:
                deps = list(deps) + [prev]
            setattr(self, "_prev_dma_" + eng, i)
        dd = []
        for d in deps:
            if d is None:
                continue
            if isinstance(d, (list, tuple)):
                dd += [x for x in d if x is not None]
            else:
                dd.append(d)
        self.ops.append({"eng": eng, "fn": fn, "deps": dd, "force": force,
                         "async": is_async, "sem": sem or eng,
                         "inc": 16 if eng in ("sy", "sd", "gd") else 1,
                         "needed": False})
        return i

    def emit(self, nc, block, stack):
        ops = self.ops
        for o in ops:
            if o["eng"] in ("sy", "sd", "gd"):
                o["needed"] = True
            if SERIALIZE_ALL:
                o["needed"] = True
            for d in o["deps"]:
                do = ops[d]
                if (not SERIALIZE_ALL and not o["force"] and not do["async"]
                        and STREAM_OF[do["eng"]] not in NO_ELIDE
                        and do["eng"] in COMPUTE_ENGS and o["eng"] in COMPUTE_ENGS
                        and STREAM_OF[do["eng"]] == STREAM_OF[o["eng"]]):
                    continue
                do["needed"] = True
        counts = {}
        for o in ops:
            if o["needed"]:
                counts[o["sem"]] = counts.get(o["sem"], 0) + o["inc"]
                o["val"] = counts[o["sem"]]
            else:
                o["val"] = None
        sems = {e: stack.enter_context(nc.semaphore(f"sem_{e}")) for e in counts}

        def emitter(stream_name):
            def inner(engobj):
                hw = {}
                for o in ops:
                    if STREAM_OF[o["eng"]] != stream_name:
                        continue
                    for d in o["deps"]:
                        do = ops[d]
                        if (not SERIALIZE_ALL and not o["force"] and not do["async"]
                                and stream_name not in NO_ELIDE
                                and do["eng"] in COMPUTE_ENGS and o["eng"] in COMPUTE_ENGS
                                and STREAM_OF[do["eng"]] == stream_name):
                            continue
                        v = do["val"]
                        if hw.get(do["sem"], 0) >= v:
                            continue
                        engobj.wait_ge(sems[do["sem"]], v)
                        hw[do["sem"]] = v
                    inst = o["fn"](engobj)
                    if o["needed"]:
                        inst.then_inc(sems[o["sem"]], o["inc"])
            return inner

        for s in sorted({STREAM_OF[o["eng"]] for o in ops}):
            getattr(block, s)(emitter(s))
        return counts


class Guard:
    def __init__(self, rec, seed=()):
        self.rec = rec
        self.writer = []
        self.readers = {}
        for t in seed:
            if t is not None:
                self.readers[("seed", len(self.readers))] = t

    def rd(self):
        return list(self.writer)

    def wr(self):
        return list(self.writer) + list(self.readers.values())

    def did_read(self, tok):
        if tok is not None:
            self.readers[self.rec.ops[tok]["eng"]] = tok

    def did_write(self, tok):
        self.writer = [tok]
        self.readers = {}


def build_program():
    nc = bass.Bass("TRN2", target_bir_lowering=False)
    r = Rec()
    G = lambda *seed: Guard(r, seed)

    din = {}
    for name, shape in [
        ("x", [T, D]),
        ("cq", [D, D]), ("sq", [D, D]), ("ck", [D, D]), ("sk", [D, D]),
        ("cv", [D, D]), ("sv", [D, D]), ("co", [D, D]), ("so", [D, D]),
        ("bq", [D]), ("bk", [D]), ("bv", [D]), ("bo", [D]),
        ("w1t", [D, INNER]), ("b1", [INNER]), ("g_ln", [INNER]), ("bt_ln", [INNER]),
        ("w2t", [INNER, D]), ("b2", [D]),
    ]:
        din[name] = nc.dram_tensor(name, shape, F32, kind="ExternalInput")
    for name, shape in [("ident", [128, 128]), ("ones_col", [128, 1]),
                        ("ones2", [2, 128])]:
        din[name] = nc.dram_tensor(name, shape, BF16, kind="ExternalInput")
    din["onesr_f"] = nc.dram_tensor("onesr_f", [1, 128], F32, kind="ExternalInput")
    din["ones_col_f"] = nc.dram_tensor("ones_col_f", [128, 1], F32, kind="ExternalInput")
    din["magicv"] = nc.dram_tensor("magicv", [128, 1], F32, kind="ExternalInput")
    din["ln127v"] = nc.dram_tensor("ln127v", [128, 1], F32, kind="ExternalInput")
    din["nmagicv"] = nc.dram_tensor("nmagicv", [128, 1], F32, kind="ExternalInput")

    out_d = nc.dram_tensor("out", [T, D], F32, kind="ExternalOutput")

    def scr(name, shape, dtype):
        kind = "ExternalOutput" if name in DEBUG_OUTS else "Internal"
        return nc.dram_tensor(name, shape, dtype, kind=kind)

    attnh_d = scr("attnh_d", [H, 128, T], BF16)
    aproj_d = scr("aproj_d", [NT, 128, D], BF16)
    s_d = scr("s_d", [NT, 128, INNER], F32)
    xq2_d = scr("xq2_d", [NT, 128, INNER], BF16)
    xq1_d = scr("xq1_d", [NT, 128, D], BF16)
    xnfm_dbg = scr("xnfm_dbg", [ND, 128, T], BF16) if "xnfm_dbg" in DEBUG_OUTS else None
    p0st_dbg = scr("p0st_dbg", [NT, 128, 32], F32) if "p0st_dbg" in DEBUG_OUTS else None
    p0bn_dbg = scr("p0bn_dbg", [NT, 128, 24], F32) if "p0bn_dbg" in DEBUG_OUTS else None
    wq_dbg = scr("wq_dbg", [128, 8], F32) if "wq_dbg" in DEBUG_OUTS else None
    mid_dbg = scr("mid_dbg", [NT, 128, 16], F32) if "mid_dbg" in DEBUG_OUTS else None

    AR_A = nc.alloc_sbuf_tensor("AR_A", [128, 32768], BF16)
    AR_B = nc.alloc_sbuf_tensor("AR_B", [128, 32768], BF16)
    AR_C = nc.alloc_sbuf_tensor("AR_C", [128, 32768], BF16)
    Bf = AR_B[:].bitcast(F32)
    Cf = AR_C[:].bitcast(F32)
    Af = AR_A[:].bitcast(F32)
    ident = nc.alloc_sbuf_tensor("identt", [128, 128], BF16)
    ones_col = nc.alloc_sbuf_tensor("ones_colt", [128, 1], BF16)
    ones_col_f = nc.alloc_sbuf_tensor("ones_col_ft", [128, 1], F32)
    ones2 = nc.alloc_sbuf_tensor("ones2t", [2, 128], BF16)
    onesr_f = nc.alloc_sbuf_tensor("onesr_ft", [1, 128], F32)
    bqkv = nc.alloc_sbuf_tensor("bqkv", [128, 48], F32)
    stp = nc.alloc_sbuf_tensor("stp", [128, 192], F32)
    wacc = nc.alloc_sbuf_tensor("wacc", [128, 8], F32)
    rowbuf = nc.alloc_sbuf_tensor("rowbuf", [1, 1024], F32)
    cmagic = nc.alloc_sbuf_tensor("cmagic", [128, 1], F32)
    nc.const_aps.register(F32, MAGIC, cmagic[:])
    cln127 = nc.alloc_sbuf_tensor("cln127", [128, 1], F32)
    nc.const_aps.register(F32, 4.844187086458591, cln127[:])
    cnmagic = nc.alloc_sbuf_tensor("cnmagic", [128, 1], F32)
    nc.const_aps.register(F32, -MAGIC, cnmagic[:])
    hilo_bo = nc.alloc_sbuf_tensor("hilo_bo", [2, D], BF16)
    invq1 = nc.alloc_sbuf_tensor("invq1", [128, 16], F32)
    invq2 = nc.alloc_sbuf_tensor("invq2", [128, 16], F32)
    scv1 = nc.alloc_sbuf_tensor("scv1", [128, 16], F32)
    scv2 = nc.alloc_sbuf_tensor("scv2", [128, 16], F32)

    ps0 = nc.alloc_psum_tensor("ps0", [128, 2048], F32)
    ps1 = nc.alloc_psum_tensor("ps1", [128, 2048], F32)

    pbk = [G() for _ in range(8)]

    def pg(t, lo, hi):
        base = 0 if t is ps0 else 4
        return [pbk[base + b] for b in range(lo // 512, (hi + 511) // 512)]

    def pgw(t, lo, hi):
        return [bb for b in pg(t, lo, hi) for bb in b.wr()]

    W_S1, W_C1, W_NC1, W_I1, W_S2, W_C2, W_NC2, W_I2, W_M1, W_M2 = range(32, 42)
    BIAS_H = 42

    def col(i):
        return stp[:, i:i + 1]

    def dma(q, dst, src, deps=(), lane=None, **kw):
        return r.op(q, lambda e, d=dst, s=src, kw=kw: e.dma_start(out=d, in_=s, **kw),
                    deps, sem=lane)

    def ts(out, in0, s1_, s2_, op0, op1, deps=(), eng="ve"):
        return r.op(eng, lambda e, o=out, i=in0, a=s1_, b=s2_, o0=op0, o1=op1:
                    e.tensor_scalar(o, i, a, b, o0, o1), deps)

    def tt(out, in0, in1, aop, deps=(), eng="ve"):
        return r.op(eng, lambda e, o=out, a=in0, b=in1, p=aop: e.tensor_tensor(o, a, b, p), deps)

    def act(out, in_, func, bias=0.0, scale=1.0, deps=()):
        return r.op("sc", lambda e, o=out, i=in_, f=func, b=bias, s=scale:
                    e.activation(o, i, f, bias=b, scale=s), deps)

    def mm(out, lhsT, rhs, start, stop, deps=()):
        return r.op("te", lambda e, o=out, l=lhsT, rr_=rhs, st=start, sp=stop:
                    e.matmul(o, l, rr_, start=st, stop=sp), deps)

    def tp(out, in_, deps=()):
        return r.op("te", lambda e, o=out, i=in_: e.transpose(o, i, ident[:]), deps)

    def red(out, in_, aop, deps=(), absv=None, eng="ve"):
        return r.op(eng, lambda e, o=out, i=in_, p=aop, a=absv:
                    e.tensor_reduce(o, i, AX.X, p, apply_absolute_value=a), deps,
                    is_async=True)

    tok_c = {}
    last_c = None
    for nm, dst in [("ident", ident), ("ones_col", ones_col), ("ones2", ones2),
                    ("onesr_f", onesr_f), ("ones_col_f", ones_col_f),
                    ("magicv", cmagic), ("ln127v", cln127), ("nmagicv", cnmagic)]:
        last_c = dma("sy", dst[:], din[nm][:], lane="Lcst")
    for j, nm in enumerate(("bq", "bk", "bv")):
        last_c = dma("sy", bqkv[:, j * 16:(j + 1) * 16],
                     din[nm][:].rearrange("(c p) -> p c", p=128),
                     allow_slow_non_contiguous=True, lane="Lcst")
    for nm in ("ident", "ones_col", "ones2", "onesr_f", "ones_col_f",
               "magicv", "ln127v"):
        tok_c[nm] = last_c
    t_magic = last_c
    tok_bqkv = [last_c]

    def make_hilo(dst, src_dram, n, row_ap, lo_ap, deps):
        t0 = dma("sy", row_ap, src_dram[:].rearrange("(a n) -> a n", a=1), deps=deps,
                 lane="Lhl")
        t1 = r.op("ve", lambda e, d=dst, rw=row_ap: e.tensor_copy(d[0:1, :], rw), [t0])
        t2 = tt(lo_ap, row_ap, dst[0:1, :], ALU.subtract, [t1])
        t3 = dma("sy", dst[1:2, :], lo_ap, deps=[t2], lane="Lhl")
        return t3

    xdv = din["x"]
    xn_t = [AR_C[:, 0:2048], AR_C[:, 2048:4096]]
    xq1_t = [AR_C[:, 4096:6144], AR_C[:, 6144:8192]]
    x_stage = [Cf[:, 4096:6144], Cf[:, 6144:8192]]
    tmp_rd = Cf[:, 8192:10240]
    xq1_sp = [AR_B[:, 0:2048], AR_B[:, 2048:4096]]
    xnf32 = Cf[:, 10240:12288]
    ps_tpA = ps0[:, 0:1024].bitcast(BF16)
    ps_tpB = ps0[:, 1024:2048].bitcast(BF16)

    g_x = [G(), G()]
    g_xn = [G(), G()]
    g_xq1t = [G(), G()]
    g_xq1sp = [G(), G()]
    g_tmp = G()
    g_xnf = G()
    ev_a_toks = []
    g_invq1 = G()
    g_invq2 = G()
    g_A = G()
    g_stats = [G(), G()]
    xq1_store = {}

    MV, _VV, MX, MN, RLN, BXN, T0, T1c, T2c, CQ, INVQ, BR2 = range(12)

    def p0_col(i, par):
        return stp[:, par * 16 + i:par * 16 + i + 1]

    for tti in range(NT):
        par = tti % 2
        bnS = 96 if par == 0 else 56
        t_x = dma("sy", x_stage[par], xdv[tti * 128:(tti + 1) * 128, :],
                  deps=g_x[par].wr(), lane="Lx%d" % par)
        g_x[par].did_write(t_x)
        t_bn = []
        for c in range(4):
            t_bn.append(r.op("ve", lambda e, c=c, par=par, bnS=bnS: e.bn_stats(
                stp[:, bnS + c * 6:bnS + c * 6 + 6],
                x_stage[par][:, c * 512:(c + 1) * 512]),
                [t_x] + g_stats[par].wr(), is_async=True))
            g_x[par].did_read(t_bn[-1])
        t_ag = r.op("ve", lambda e, par=par, bnS=bnS: e.bn_aggr(
            stp[:, par * 16 + MV:par * 16 + MV + 2], stp[:, bnS:bnS + 24]), t_bn,
            force=True, is_async=True)
        t_mx = red(p0_col(MX, par), x_stage[par], ALU.max, [t_x] + g_stats[par].wr())
        t_mn = red(p0_col(MN, par), x_stage[par], ALU.min, [t_x] + g_stats[par].wr())
        g_x[par].did_read(t_mx)
        g_x[par].did_read(t_mn)
        t_u = ts(p0_col(T0, par), p0_col(MV + 1, par), 1.0, EPS, ALU.mult, ALU.add, [t_ag])
        t_ln = act(p0_col(T0, par), p0_col(T0, par), AF.Ln, deps=[t_u])
        t_rl = act(p0_col(RLN, par), p0_col(T0, par), AF.Exp, scale=-0.5, deps=[t_ln])
        t_bx = ts(p0_col(BXN, par), p0_col(MV, par), p0_col(RLN, par), -1.0,
                  ALU.mult, ALU.mult, [t_rl])
        t_xnf = r.op("sc", lambda e, par=par: e.activation(
            xnf32, x_stage[par], AF.Identity,
            bias=p0_col(BXN, par), scale=p0_col(RLN, par)),
            [t_bx, t_x] + g_xnf.wr())
        g_xnf.did_write(t_xnf)
        g_x[par].did_read(t_xnf)
        t_xn = r.op("ve", lambda e, par=par: e.tensor_copy(xn_t[par], xnf32),
                    [t_xnf] + g_xn[par].wr())
        g_xnf.did_read(t_xn)
        g_xn[par].did_write(t_xn)
        if P0_MIN:
            g_stats[par].did_write(t_xn)
            tps_a = []
            for j in range(16):
                tps_a.append(tp(ps_tpA[:, j * 128:(j + 1) * 128],
                                xn_t[par][:, j * 128:(j + 1) * 128],
                                [t_xn, tok_c["ident"]] +
                                (pgw(ps0, 0, 1024) if j == 0 else [])))
                g_xn[par].did_read(tps_a[-1])
            for b in pg(ps0, 0, 1024):
                b.did_write(tps_a[-1])
            ev_a = r.op("ve", lambda e, tti=tti: e.tensor_copy(
                AR_A[:].rearrange("p (dc t) -> p dc t", t=T)[:, :, tti * 128:tti * 128 + 128],
                ps_tpA[:].rearrange("p (dc j) -> p dc j", j=128)),
                tps_a + g_A.wr())
            for b in pg(ps0, 0, 1024):
                b.did_read(ev_a)
            g_A.did_write(ev_a)
            xq1_store[tti] = t_xn
            continue
        t_a = tt(p0_col(T1c, par), p0_col(RLN, par), p0_col(RLN, par), ALU.mult, [t_rl])
        t_b = tt(p0_col(T1c, par), p0_col(MV + 1, par), p0_col(T1c, par), ALU.mult, [t_a])
        t_c = ts(p0_col(T1c, par), p0_col(T1c, par), 1.0, EPS, ALU.mult, ALU.add, [t_b])
        t_d = act(p0_col(T1c, par), p0_col(T1c, par), AF.Ln, deps=[t_c])
        t_r2 = act(p0_col(BR2, par), p0_col(T1c, par), AF.Exp, scale=-0.5, deps=[t_d])
        t_e = tt(p0_col(T0, par), p0_col(MX, par), p0_col(MV, par), ALU.subtract,
                 [t_mx, t_ag, t_rl])
        t_f = tt(p0_col(T2c, par), p0_col(MV, par), p0_col(MN, par), ALU.subtract,
                 [t_mn, t_ag])
        t_g = tt(p0_col(T0, par), p0_col(T0, par), p0_col(T2c, par), ALU.max, [t_e, t_f])
        t_h = tt(p0_col(T0, par), p0_col(T0, par), p0_col(RLN, par), ALU.mult, [t_g])
        t_i = tt(p0_col(T0, par), p0_col(T0, par), p0_col(BR2, par), ALU.mult,
                 [t_h, t_r2])
        t_j = ts(p0_col(T0, par), p0_col(T0, par), 1e-5, None, ALU.max, ALU.bypass, [t_i])
        t_k = act(p0_col(T1c, par), p0_col(T0, par), AF.Ln, deps=[t_j, t_r2])
        t_l = act(p0_col(T1c, par), p0_col(T1c, par), AF.Exp, scale=-1.0,
                  bias=4.844187086458591, deps=[t_k])
        t_m = tt(p0_col(CQ, par), p0_col(BR2, par), p0_col(T1c, par), ALU.mult, [t_l])
        t_n = ts(p0_col(INVQ, par), p0_col(T0, par), 1.0 / 127.0, None, ALU.mult,
                 ALU.bypass, [t_j])
        t_p1 = r.op("sc", lambda e, par=par: e.activation(
            tmp_rd, xnf32, AF.Identity, bias=MAGIC, scale=p0_col(CQ, par)),
            [t_m, t_xnf, t_magic] + g_tmp.wr())
        g_tmp.did_write(t_p1)
        g_xnf.did_read(t_p1)
        t_p2 = ts(xq1_t[par], tmp_rd, -MAGIC, None, ALU.add, ALU.bypass,
                  [t_p1] + g_xq1t[par].wr())
        g_tmp.did_read(t_p2)
        g_xq1t[par].did_write(t_p2)
        t_iv = ts(invq1[:, tti:tti + 1], p0_col(INVQ, par), 1.0, None,
                  ALU.mult, ALU.bypass, [t_n] + g_invq1.wr())
        g_invq1.writer.append(t_iv)
        g_stats[par].did_write(t_p2)
        g_stats[par].readers["iv"] = t_iv
        if p0st_dbg is not None:
            t_dbg = dma("sd", p0st_dbg[tti][:, 0:16], stp[:, par * 16:par * 16 + 16],
                        deps=[t_p2, t_n, t_m, t_bx, t_xn], lane="Ldbg")
            g_stats[par].readers[("dbg", tti)] = t_dbg
        tps_a = []
        for j in range(16):
            tps_a.append(tp(ps_tpA[:, j * 128:(j + 1) * 128],
                            xn_t[par][:, j * 128:(j + 1) * 128],
                            [t_xn, tok_c["ident"]] +
                            (pgw(ps0, 0, 1024) if j == 0 else [])))
            g_xn[par].did_read(tps_a[-1])
        for b in pg(ps0, 0, 1024):
            b.did_write(tps_a[-1])
        ev_a = r.op("ve", lambda e, tti=tti: e.tensor_copy(
            AR_A[:].rearrange("p (dc t) -> p dc t", t=T)[:, :, tti * 128:tti * 128 + 128],
            ps_tpA[:].rearrange("p (dc j) -> p dc j", j=128)),
            tps_a + g_A.wr(), is_async=True)
        for b in pg(ps0, 0, 1024):
            b.did_read(ev_a)
        g_A.writer.append(ev_a)
        ev_a_toks.append(ev_a)
        tps_b = []
        for j in range(16):
            tps_b.append(tp(ps_tpB[:, j * 128:(j + 1) * 128],
                            xq1_t[par][:, j * 128:(j + 1) * 128],
                            [t_p2] + (pgw(ps0, 1024, 2048) if j == 0 else [])))
            g_xq1t[par].did_read(tps_b[-1])
        for b in pg(ps0, 1024, 2048):
            b.did_write(tps_b[-1])
        ev_b = r.op("ve", lambda e, par=par: e.tensor_copy(
            xq1_sp[par], ps_tpB[:]), tps_b + g_xq1sp[par].wr(), is_async=True)
        for b in pg(ps0, 1024, 2048):
            b.did_read(ev_b)
        g_xq1sp[par].did_write(ev_b)
        t_spill = dma("sd", xq1_d[tti], xq1_sp[par], deps=[ev_b], lane="Lq1%d" % par)
        g_xq1sp[par].did_read(t_spill)
        xq1_store[tti] = t_spill

    if xnfm_dbg is not None:
        for dc in range(ND):
            g_A.did_read(dma("sd", xnfm_dbg[dc],
                             AR_A[:, dc * T:(dc + 1) * T], deps=g_A.rd(),
                             lane="Ldbg"))

    if PHASE_LIMIT == "p0":
        r.op("ve", lambda e: e.engine_nop(),
             [xq1_store[NT - 1]] + g_A.rd())
        import contextlib
        stack = contextlib.ExitStack()
        with nc.Block() as block:
            r.emit(nc, block, stack)
        return nc, stack

    mean_stage = [Bf[:, 12288:14336], Bf[:, 14336:16384]]
    CWM = 2048
    g_mst = [G(), G()]
    t_z0 = r.op("gc", lambda e: e.memset(wacc[:], 0.0), [])
    acc_last = {0: t_z0, 1: t_z0}
    mhat_tok = {}
    nmhat_tok = {}
    mean_jobs = []
    job_ix = [0]
    for wi, wname in enumerate(("w1t", "w2t")):
        wd = din[wname]
        nch = wd.shape[1] // CWM
        nrb = wd.shape[0] // 128
        mcol = W_M1 if wi == 0 else W_M2
        for dc2 in range(nrb):
            for hh in range(nch):
              def _mean_job(wi=wi, wd=wd, dc2=dc2, hh=hh, mcol=mcol):
                  p = job_ix[0] % 2
                  job_ix[0] += 1
                  mst = mean_stage[p]
                  t_ld = dma("sy", mst,
                             wd[dc2 * 128:(dc2 + 1) * 128, hh * CWM:(hh + 1) * CWM],
                             deps=g_mst[p].wr(), lane="Lms" + str(p))
                  g_mst[p].did_write(t_ld)
                  rs = wacc[:, 4 + p:5 + p]
                  t_rd = red(rs, mst, ALU.add, [t_ld, acc_last.get(2 + p, t_z0)],
                             absv=True)
                  g_mst[p].did_read(t_rd)
                  acc_last[2 + p] = t_rd
                  if dc2 == 0 and hh == 0:
                      t_m0 = mm(ps1[0:1, 0:1], ones_col_f[:], rs,
                                True, True,
                                [t_rd, tok_c["ones_col_f"]] + pgw(ps1, 0, 512))
                      for b in pg(ps1, 0, 512):
                          b.did_write(t_m0)
                      t_m1 = r.op("ve", lambda e: e.tensor_scalar(
                          rowbuf[0:1, 0:1], ps1[0:1, 0:1], 1.0 / (128 * CWM), None,
                          ALU.mult, ALU.bypass), [t_m0])
                      for b in pg(ps1, 0, 512):
                          b.did_read(t_m1)
                      t_m2 = mm(ps1[:, 0:1], onesr_f[:], rowbuf[0:1, 0:1], True, True,
                                [t_m1, tok_c["onesr_f"]] + pgw(ps1, 0, 512))
                      for b in pg(ps1, 0, 512):
                          b.did_write(t_m2)
                      t_mh = r.op("ve", lambda e, mcol=mcol: e.tensor_copy(
                          col(mcol), ps1[:, 0:1]), [t_m2], is_async=True)
                      for b in pg(ps1, 0, 512):
                          b.did_read(t_mh)
                      mhat_tok[wi] = t_mh
                      nmhat_tok[wi] = ts(col(mcol + 6), col(mcol), -float(CWM), None,
                                         ALU.mult, ALU.bypass, [t_mh])
                  t_ct = ts(wacc[:, 6 + p:7 + p], rs, col(mcol + 6), None,
                            ALU.add, ALU.bypass, [t_rd, nmhat_tok[wi]])
                  acc_last[2 + p] = t_ct
                  t_ac = r.op("ve", lambda e, wi=wi, p=p: e.tensor_tensor(
                      wacc[:, wi:wi + 1], wacc[:, wi:wi + 1], wacc[:, 6 + p:7 + p],
                      ALU.add), [t_ct, acc_last[wi]])
                  acc_last[wi] = t_ac
              mean_jobs.append(_mean_job)

    def finish_wmean(wi, s_c, c_c, nc_c, i_c):
        mcol = W_M1 if wi == 0 else W_M2
        deps = [acc_last[wi]] + pgw(ps1, 0, 512)
        t_mm = mm(ps1[0:1, 0:1], ones_col_f[:], wacc[:, wi:wi + 1], True, True,
                  deps + [tok_c["ones_col_f"]])
        for b in pg(ps1, 0, 512):
            b.did_write(t_mm)
        t_c0 = r.op("ve", lambda e: e.tensor_scalar(
            rowbuf[0:1, 0:1], ps1[0:1, 0:1], 1.0 / (D * INNER), None,
            ALU.mult, ALU.bypass), [t_mm])
        for b in pg(ps1, 0, 512):
            b.did_read(t_c0)
        t_c1 = r.op("ve", lambda e, mcol=mcol: e.tensor_scalar(
            rowbuf[0:1, 0:1], rowbuf[0:1, 0:1], col(mcol)[0:1, :], 1e-5,
            ALU.add, ALU.max), [t_c0, mhat_tok[wi]])
        t_sp = mm(ps1[:, 0:1], onesr_f[:], rowbuf[0:1, 0:1], True, True,
                  [t_c1, tok_c["onesr_f"]] + pgw(ps1, 0, 512))
        for b in pg(ps1, 0, 512):
            b.did_write(t_sp)
        t_mc = r.op("ve", lambda e: e.tensor_copy(col(i_c), ps1[:, 0:1]), [t_sp],
                    is_async=True)
        for b in pg(ps1, 0, 512):
            b.did_read(t_mc)
        t_s0 = act(col(s_c), col(i_c), AF.Ln, deps=[t_mc])
        t_s1 = act(col(s_c), col(s_c), AF.Exp, scale=-1.0, deps=[t_s0])
        t_u = tt(stp[:, 44 + wi:45 + wi], col(i_c), col(s_c), ALU.mult, [t_s1, t_mc])
        t_v = ts(stp[:, 44 + wi:45 + wi], stp[:, 44 + wi:45 + wi], -1.0, 2.0,
                 ALU.mult, ALU.add, [t_u])
        t_s = tt(col(s_c), col(s_c), stp[:, 44 + wi:45 + wi], ALU.mult, [t_v])
        t_cc = ts(col(c_c), col(i_c), CLIP_W, None, ALU.mult, ALU.bypass, [t_mc])
        t_nc = ts(col(nc_c), col(c_c), -1.0, None, ALU.mult, ALU.bypass, [t_cc])
        return [t_s, t_cc, t_nc, t_mc]

    qh_b = AR_B[:, 4096:6144]
    kh_b = AR_B[:, 6144:8192]
    vh_b = AR_B[:, 8192:10240]
    vtm_b = AR_B[:, 10240:12288]
    sqr_b = AR_B[:, 12288:14336]
    ah_b = AR_B[:, 14336:16384]
    pt_b = [AR_B[:, 16384:17408], AR_B[:, 17408:18432]]
    nrm_pool = AR_B[:, 18432:18434]
    rown = Bf[0:1, 9220:9228]
    bcast_sb = Bf[:, 9472:10496]
    rsum_row = rowbuf[0:1, 0:1024]

    g_qh, g_kh, g_vh, g_vtm, g_sqr, g_ah = G(), G(), G(), G(), G(), G()
    g_pt = [G(), G()]
    g_nrmp, g_rown, g_rsum, g_bcsb = G(), G(), G(), G()
    g_bias_h = G()

    stage_c = AR_C[:, 0:4096]
    stage_s = AR_C[:, 4096:8192]
    wslot = [AR_C[:, 8192 + i * 4096:8192 + (i + 1) * 4096] for i in range(6)]
    seedC = g_xn[0].wr() + g_xn[1].wr() + g_xq1t[0].wr() + g_xq1t[1].wr() + \
        g_x[0].wr() + g_x[1].wr() + g_tmp.wr()
    g_stage = G(*seedC)
    g_wslot = [G(*seedC) for _ in range(6)]

    CW = {"q": ("cq", "sq"), "k": ("ck", "sk"), "v": ("cv", "sv")}
    slice_tok = {}
    sidx_box = [0]

    def prep_group(g2):
        for wn in ("q", "k", "v"):
            sidx = sidx_box[0]
            slot = sidx % 6
            cdn, sdn = CW[wn]
            t_c = dma("gd", stage_c[:].rearrange("p (dc f) -> p dc f", f=256),
                      din[cdn][:, g2 * 256:(g2 + 1) * 256]
                      .rearrange("(dc p) f -> p dc f", p=128),
                      deps=g_stage.wr(), lane="Lstc")
            t_s = dma("gd", stage_s[:].rearrange("p (dc f) -> p dc f", f=256),
                      din[sdn][:, g2 * 256:(g2 + 1) * 256]
                      .rearrange("(dc p) f -> p dc f", p=128),
                      deps=g_stage.wr(), lane="Lsts")
            g_stage.writer = [t_c, t_s]
            g_stage.readers = {}
            t_q = tt(wslot[slot], stage_c, stage_s, ALU.mult,
                     [t_c, t_s] + g_wslot[slot].wr())
            g_stage.did_read(t_q)
            g_wslot[slot].did_write(t_q)
            slice_tok[(g2, wn)] = (slot, t_q)
            sidx_box[0] += 1

    prep_group(0)
    prep_group(1)

    ps_sc = [ps0[:, 0:1024], ps0[:, 1024:2048]]
    ps_av = ps1[:, 0:1024]
    ps_sum = ps1[0:1, 1024:2048]
    ps_nrm = ps1[:, 0:16]
    ps_t128 = ps1[0:1, 32:96].bitcast(BF16)
    ps_tiny = ps1[:, 16:17]
    ps_vt = ps0[:, 0:1024].bitcast(BF16)

    attn_store_toks = {}
    mj = 0
    for h in range(H):
        g2, hl = h // 2, h % 2
        if hl == 0 and 1 <= g2 < 7:
            prep_group(g2 + 1)
        for wn, dstb, gdst, bcol in (("q", qh_b, g_qh, 0), ("k", kh_b, g_kh, 16)):
            slot, t_wq = slice_tok[(g2, wn)]
            wv = wslot[slot][:].rearrange("p (dc f) -> p dc f", f=256)
            for tq in range(4):
                par = tq % 2
                bank = pg(ps0, par * 1024, par * 1024 + 512)
                mms = []
                for dc in range(ND):
                    mms.append(mm(ps_sc[par][:, 0:512],
                                  wv[:, dc, hl * 128:(hl + 1) * 128],
                                  AR_A[:, dc * T + tq * 512:dc * T + tq * 512 + 512],
                                  dc == 0, dc == ND - 1,
                                  ([t_wq, ev_a_toks[4 * tq + 3]] +
                                   [bb for b in bank for bb in b.wr()]) if dc == 0 else []))
                    g_A.did_read(mms[-1])
                g_wslot[slot].did_read(mms[-1])
                for b in bank:
                    b.did_write(mms[-1])
                t_ev = r.op("sc", lambda e, dstb=dstb, tq=tq, par=par, h=h, bcol=bcol:
                            e.activation(dstb[:, tq * 512:(tq + 1) * 512],
                                         ps_sc[par][:, 0:512], AF.Identity,
                                         bias=bqkv[:, bcol + h:bcol + h + 1]),
                            [mms[-1]] + tok_bqkv + (gdst.wr() if tq == 0 else []))
                for b in bank:
                    b.did_read(t_ev)
                if tq == 0:
                    gdst.did_write(t_ev)
                else:
                    gdst.writer.append(t_ev)
        def pnorm(src_g, src_b, out_i):
            t_sq = tt(sqr_b, src_b, src_b, ALU.mult, src_g.rd() + g_sqr.wr())
            src_g.did_read(t_sq)
            g_sqr.did_write(t_sq)
            nm = []
            for tti in range(NT):
                nm.append(mm(ps_nrm[:, tti:tti + 1],
                             sqr_b[:, tti * 128:(tti + 1) * 128], ones_col[:],
                             True, True,
                             [t_sq, tok_c["ones_col"]] +
                             (pgw(ps1, 0, 512) if tti == 0 else [])))
                g_sqr.did_read(nm[-1])
            for b in pg(ps1, 0, 512):
                b.did_write(nm[-1])
            t_red = red(nrm_pool[:, out_i:out_i + 1], ps_nrm[:], ALU.max,
                        nm + g_nrmp.wr())
            for b in pg(ps1, 0, 512):
                b.did_read(t_red)
            g_nrmp.did_write(t_red)
            t_t = r.op("te", lambda e, out_i=out_i: e.transpose(
                ps_t128, nrm_pool[:, out_i:out_i + 1], ident[:]),
                [t_red] + pgw(ps1, 0, 512))
            g_nrmp.did_read(t_t)
            for b in pg(ps1, 0, 512):
                b.did_write(t_t)
            t_mx = red(rown[0:1, out_i:out_i + 1], ps_t128, ALU.max,
                       [t_t] + g_rown.wr())
            for b in pg(ps1, 0, 512):
                b.did_read(t_mx)
            g_rown.did_write(t_mx)
            return t_mx

        t_mq = pnorm(g_qh, qh_b, 0)
        t_mk = pnorm(g_kh, kh_b, 1)
        t_pm = tt(rown[0:1, 2:3], rown[0:1, 0:1], rown[0:1, 1:2], ALU.mult, [t_mq, t_mk])
        t_l1 = act(rown[0:1, 2:3], rown[0:1, 2:3], AF.Ln, deps=[t_pm])
        t_l2 = act(rown[0:1, 2:3], rown[0:1, 2:3], AF.Exp, scale=0.5, deps=[t_l1])
        g_rown.did_write(t_l2)
        t_sp = mm(ps_tiny, onesr_f[:], rown[0:1, 2:3], True, True,
                  [t_l2] + pgw(ps1, 0, 512))
        g_rown.did_read(t_sp)
        for b in pg(ps1, 0, 512):
            b.did_write(t_sp)
        t_bh = ts(col(BIAS_H), ps_tiny, -RSQ128, None, ALU.mult, ALU.bypass,
                  [t_sp] + g_bias_h.wr())
        for b in pg(ps1, 0, 512):
            b.did_read(t_bh)
        g_bias_h.did_write(t_bh)

        for wn, dstb, gdst, bcol in (("v", vh_b, g_vh, 32),):
            slot, t_wq = slice_tok[(g2, wn)]
            wv = wslot[slot][:].rearrange("p (dc f) -> p dc f", f=256)
            for tq in range(4):
                par = tq % 2
                bank = pg(ps0, par * 1024, par * 1024 + 512)
                mms = []
                for dc in range(ND):
                    mms.append(mm(ps_sc[par][:, 0:512],
                                  wv[:, dc, hl * 128:(hl + 1) * 128],
                                  AR_A[:, dc * T + tq * 512:dc * T + tq * 512 + 512],
                                  dc == 0, dc == ND - 1,
                                  ([t_wq, ev_a_toks[4 * tq + 3]] +
                                   [bb for b in bank for bb in b.wr()]) if dc == 0 else []))
                    g_A.did_read(mms[-1])
                g_wslot[slot].did_read(mms[-1])
                for b in bank:
                    b.did_write(mms[-1])
                t_ev = r.op("sc", lambda e, dstb=dstb, tq=tq, par=par, h=h, bcol=bcol:
                            e.activation(dstb[:, tq * 512:(tq + 1) * 512],
                                         ps_sc[par][:, 0:512], AF.Identity,
                                         bias=bqkv[:, bcol + h:bcol + h + 1]),
                            [mms[-1]] + tok_bqkv + (gdst.wr() if tq == 0 else []))
                for b in bank:
                    b.did_read(t_ev)
                if tq == 0:
                    gdst.did_write(t_ev)
                else:
                    gdst.writer.append(t_ev)
        tps = []
        for kt in range(NT):
            tps.append(tp(ps_vt[:, kt * 128:(kt + 1) * 128],
                          vh_b[:, kt * 128:(kt + 1) * 128],
                          g_vh.rd() + (pgw(ps0, 0, 1024) if kt == 0 else [])))
            g_vh.did_read(tps[-1])
        for b in pg(ps0, 0, 1024):
            b.did_write(tps[-1])
        t_vtm = r.op("ve", lambda e: e.tensor_copy(vtm_b[:], ps_vt[:]),
                     tps + g_vtm.wr(), is_async=True)
        for b in pg(ps0, 0, 1024):
            b.did_read(t_vtm)
        g_vtm.did_write(t_vtm)

        for _ in range(12):
            if mj < len(mean_jobs):
                mean_jobs[mj]()
                mj += 1
        ah_parts = []
        for qh2 in range(2):
            av_last = []
            sum_last = []
            exp_par0 = None
            for kt in range(NT):
                par = kt % 2
                bank_s = pg(ps0, par * 1024, (par + 1) * 1024)
                m1 = mm(ps_sc[par][:, 0:512], kh_b[:, kt * 128:(kt + 1) * 128],
                        qh_b[:, qh2 * 1024:qh2 * 1024 + 512], True, True,
                        g_kh.rd() + g_qh.rd() + [bb for b in bank_s for bb in b.wr()])
                m2 = mm(ps_sc[par][:, 512:1024], kh_b[:, kt * 128:(kt + 1) * 128],
                        qh_b[:, qh2 * 1024 + 512:qh2 * 1024 + 1024], True, True, [])
                g_kh.did_read(m2)
                g_qh.did_read(m2)
                for b in bank_s:
                    b.did_write(m2)
                t_exp = r.op("sc", lambda e, par=par: e.activation(
                    pt_b[par], ps_sc[par], AF.Exp, bias=col(BIAS_H), scale=RSQ128),
                    [m1, m2] + g_bias_h.rd() + g_pt[par].wr())
                g_bias_h.did_read(t_exp)
                for b in bank_s:
                    b.did_read(t_exp)
                g_pt[par].did_write(t_exp)
                if par == 0:
                    exp_par0 = t_exp
                s1_ = mm(ps_sum[:, 0:512], ones_col[:], pt_b[par][:, 0:512],
                         kt == 0, kt == NT - 1,
                         [t_exp] + (pgw(ps1, 1024, 2048) if kt == 0 else []))
                s2_ = mm(ps_sum[:, 512:1024], ones_col[:], pt_b[par][:, 512:1024],
                         kt == 0, kt == NT - 1, [])
                a1 = mm(ps_av[:, 0:512], vtm_b[:, kt * 128:(kt + 1) * 128],
                        pt_b[par][:, 0:512], kt == 0, kt == NT - 1,
                        [t_exp] + g_vtm.rd() + (pgw(ps1, 0, 1024) if kt == 0 else []))
                a2 = mm(ps_av[:, 512:1024], vtm_b[:, kt * 128:(kt + 1) * 128],
                        pt_b[par][:, 512:1024], kt == 0, kt == NT - 1, [])
                g_vtm.did_read(a2)
                g_pt[par].did_read(a2)
                if kt == NT - 1:
                    sum_last = [s1_, s2_]
                    av_last = [a1, a2]
                    for b in pg(ps1, 1024, 2048):
                        b.did_write(s2_)
                    for b in pg(ps1, 0, 1024):
                        b.did_write(a2)
            t_rc0 = act(rsum_row, ps_sum[:], AF.Ln,
                        deps=sum_last + g_rsum.wr())
            t_rc = act(rsum_row, rsum_row, AF.Exp, scale=-1.0, deps=[t_rc0])
            for b in pg(ps1, 1024, 2048):
                b.did_read(t_rc)
            g_rsum.did_write(t_rc)
            bcs = []
            for c in range(2):
                bcs.append(mm(ps_sc[0][:, c * 512:(c + 1) * 512], onesr_f[:],
                              rsum_row[0:1, c * 512:(c + 1) * 512], True, True,
                              [t_rc] + ([exp_par0] + pgw(ps0, 0, 1024)
                                        if c == 0 else [])))
                g_rsum.did_read(bcs[-1])
            for b in pg(ps0, 0, 1024):
                b.did_write(bcs[-1])
            t_bc2 = r.op("ve", lambda e: e.tensor_copy(bcast_sb, ps_sc[0]),
                         bcs + g_bcsb.wr(), is_async=True)
            for b in pg(ps0, 0, 1024):
                b.did_read(t_bc2)
            g_bcsb.did_write(t_bc2)
            t_ev = tt(ah_b[:, qh2 * 1024:(qh2 + 1) * 1024], ps_av, bcast_sb, ALU.mult,
                      av_last + [t_bc2] + (g_ah.wr() if qh2 == 0 else []))
            r.ops[t_ev]["async"] = True
            g_bcsb.did_read(t_ev)
            for b in pg(ps1, 0, 1024):
                b.did_read(t_ev)
            ah_parts.append(t_ev)
        g_ah.writer = ah_parts
        g_ah.readers = {}
        t_st = dma("sd", attnh_d[h], ah_b[:], deps=ah_parts, lane="Lah")
        g_ah.did_read(t_st)
        attn_store_toks[h] = t_st

    while mj < len(mean_jobs):
        mean_jobs[mj]()
        mj += 1
    w1_ready = finish_wmean(0, W_S1, W_C1, W_NC1, W_I1)
    w2_ready = finish_wmean(1, W_S2, W_C2, W_NC2, W_I2)
    if wq_dbg is not None:
        dma("sd", wq_dbg[:], stp[:, 32:40], deps=w1_ready + w2_ready, lane="Ldbg")

    wo_stage_c = AR_B[:, 4096:6144]
    wo_stage_s = AR_B[:, 6144:8192]
    ap_stage = [AR_B[:, 8192:10240], AR_B[:, 10240:12288]]
    g_wost = G(*(g_qh.wr() + g_kh.wr()))
    g_apst = [G(*(g_vh.wr())), G(*(g_vtm.wr()))]
    g_wo = G(*(g_A.wr()))
    g_arb = G(*(g_stage.wr() + [t for i in range(6) for t in g_wslot[i].wr()]))
    t_bo_hilo = make_hilo(hilo_bo, din["bo"], D, Cf[0:1, 0:2048],
                          AR_C[0:1, 4096:6144], g_arb.wr())
    g_arb.did_read(t_bo_hilo)

    wo_toks = []
    rb_toks = []
    g_wost_rb = g_wost
    for fc in range(ND):
        t_c = dma("gd", wo_stage_c, din["co"][fc * 128:(fc + 1) * 128, :],
                  deps=g_wost.wr(), lane="Lwoc")
        t_s = dma("gd", wo_stage_s, din["so"][fc * 128:(fc + 1) * 128, :],
                  deps=g_wost.wr(), lane="Lwos")
        g_wost.writer = [t_c, t_s]
        g_wost.readers = {}
        t_q = tt(AR_A[:, fc * 2048:(fc + 1) * 2048], wo_stage_c, wo_stage_s,
                 ALU.mult, [t_c, t_s] + (g_wo.wr() if fc == 0 else []))
        g_wost.did_read(t_q)
        if fc == 0:
            g_wo.did_write(t_q)
        else:
            g_wo.writer.append(t_q)
        wo_toks.append(t_q)
        rb_toks.append(dma("sy", AR_C[:, fc * 2048:(fc + 1) * 2048], attnh_d[fc],
                           deps=[attn_store_toks[fc], t_bo_hilo] +
                           (g_arb.wr() if fc == 0 else []), lane="Larb"))
        if fc == 0:
            g_arb.did_write(rb_toks[-1])
        else:
            g_arb.writer.append(rb_toks[-1])

    ps_full = [ps0, ps1]
    aproj_store = {}
    for tti in range(NT):
        par = tti % 2
        psf = ps_full[par]
        banks = pg(psf, 0, 2048)
        mms = []
        for fc in range(ND):
            for c in range(4):
                dep0 = []
                if c == 0:
                    dep0 = [rb_toks[-1], wo_toks[fc]]
                    if fc == 0:
                        dep0 += [bb for b in banks for bb in b.wr()]
                mms.append(mm(psf[:, c * 512:(c + 1) * 512],
                              AR_C[:, fc * 2048 + tti * 128:fc * 2048 + tti * 128 + 128],
                              AR_A[:, fc * 2048 + c * 512:fc * 2048 + c * 512 + 512],
                              fc == 0, False, dep0))
            g_arb.did_read(mms[-1])
            g_wo.did_read(mms[-1])
        for c in range(4):
            mms.append(mm(psf[:, c * 512:(c + 1) * 512], ones2[:],
                          hilo_bo[:, c * 512:(c + 1) * 512], False, True,
                          [t_bo_hilo, tok_c["ones2"]] if c == 0 else []))
        for b in banks:
            b.did_write(mms[-1])
        t_ev = r.op("sc", lambda e, par=par, psf=psf: e.activation(
            ap_stage[par], psf[:], AF.Identity), [mms[-1]] + g_apst[par].wr())
        for b in banks:
            b.did_read(t_ev)
        g_apst[par].did_write(t_ev)
        t_st2 = dma("sd", aproj_d[tti], ap_stage[par], deps=[t_ev],
                    lane="Lap%d" % par)
        g_apst[par].did_read(t_st2)
        aproj_store[tti] = t_st2

    w1_stage = [Cf[:, 0:8192], Cf[:, 8192:16384]]
    w1q = [AR_A[:, 0:8192], AR_A[:, 8192:16384]]
    Af = AR_A[:].bitcast(F32)
    s_stage = [Af[:, 8192 + i * 512:8192 + (i + 1) * 512] for i in range(4)]
    t1_stage = [Af[:, 10240 + i * 512:10240 + (i + 1) * 512] for i in range(2)]
    b1bc = [Af[:, 11264:11776], Af[:, 11776:12288]]
    g_w1st = [G(*(g_arb.wr())), G(*(g_arb.wr()))]
    g_w1q = [G(*(g_wo.wr())), G(*(g_wo.wr()))]
    g_sst = [G(*(g_wo.wr())) for _ in range(4)]
    g_t1 = [G(*(g_wo.wr())), G(*(g_wo.wr()))]
    g_b1bc = [G(*(g_wo.wr())), G(*(g_wo.wr()))]
    g_scv1 = G()
    g_Brl = G(*(g_qh.wr() + g_kh.wr() + g_vh.wr() + g_vtm.wr() + g_sqr.wr() +
                g_ah.wr() + g_pt[0].wr() + g_pt[1].wr() + g_wost.wr() +
                g_apst[0].wr() + g_apst[1].wr() + g_mst[0].wr() + g_mst[1].wr() + g_rown.wr() +
                g_nrmp.wr()))
    rl_toks = {}
    for tti in range(NT):
        rl_toks[tti] = dma("sy", AR_B[:, tti * 2048:(tti + 1) * 2048], xq1_d[tti],
                           deps=[xq1_store[tti]] + (g_Brl.wr() if tti < 2 else []),
                           lane="Lrl")
        if tti == 0:
            g_Brl.did_write(rl_toks[tti])
        else:
            g_Brl.writer.append(rl_toks[tti])
    scv_toks = []
    for t_ in range(NT):
        scv_toks.append(tt(scv1[:, t_:t_ + 1], invq1[:, t_:t_ + 1], col(W_I1),
                           ALU.mult, g_invq1.wr() + w1_ready + g_scv1.wr()))
        g_scv1.writer.append(scv_toks[-1])
    g_invq1.did_read(scv_toks[-1])

    ps_fp = [ps0[:, 0:512], ps0[:, 512:1024]]
    ps_fb = [pg(ps0, 0, 512), pg(ps0, 512, 1024)]
    s_store = {}
    for sbi in range(16):
        par = sbi % 2
        t_ld = dma("sy", w1_stage[par][:].rearrange("p (dc f) -> p dc f", f=512),
                   din["w1t"][:, sbi * 512:(sbi + 1) * 512]
                   .rearrange("(dc p) f -> p dc f", p=128),
                   deps=g_w1st[par].wr(), lane="Lw1%d" % par)
        g_w1st[par].did_write(t_ld)
        t_q1 = r.op("ve", lambda e, par=par: e.tensor_scalar(
            w1_stage[par], w1_stage[par], col(W_C1), col(W_NC1), ALU.min, ALU.max),
            [t_ld] + w1_ready)
        t_q2 = r.op("ve", lambda e, par=par: e.tensor_scalar(
            w1_stage[par], w1_stage[par], col(W_S1), MAGIC, ALU.mult, ALU.add), [t_q1])
        t_q3 = r.op("ve", lambda e, par=par: e.tensor_scalar(
            w1q[par], w1_stage[par], -MAGIC, None, ALU.add, ALU.bypass),
            [t_q2] + g_w1q[par].wr())
        g_w1st[par].did_write(t_q2)
        g_w1st[par].did_read(t_q3)
        g_w1q[par].did_write(t_q3)
        bpar = sbi % 2
        t_bb = dma("sy", b1bc[bpar],
                   din["b1"][sbi * 512:(sbi + 1) * 512]
                   .rearrange("(a n) -> a n", a=1).partition_broadcast(128).squeeze(1),
                   deps=g_b1bc[bpar].wr(), lane="Lb1%d" % bpar)
        g_b1bc[bpar].did_write(t_bb)
        for tti in range(NT):
            fpar = tti % 2
            mms = []
            for dc in range(ND):
                mms.append(mm(ps_fp[fpar],
                              AR_B[:, tti * 2048 + dc * 128:tti * 2048 + dc * 128 + 128],
                              w1q[par][:, dc * 512:(dc + 1) * 512],
                              dc == 0, dc == ND - 1,
                              ([t_q3, rl_toks[NT - 1]] +
                               [bb for b in ps_fb[fpar] for bb in b.wr()])
                              if dc == 0 else []))
            g_Brl.did_read(mms[-1])
            g_w1q[par].did_read(mms[-1])
            for b in ps_fb[fpar]:
                b.did_write(mms[-1])
            t_sc = r.op("ve", lambda e, fpar=fpar, tti=tti, bpar=bpar:
                        e.scalar_tensor_tensor(t1_stage[fpar], ps_fp[fpar],
                                               scv1[:, tti:tti + 1], b1bc[bpar],
                                               ALU.mult, ALU.add),
                        [mms[-1], t_bb, scv_toks[tti]] + g_t1[fpar].wr(),
                        is_async=True)
            g_b1bc[bpar].did_read(t_sc)
            g_scv1.did_read(t_sc)
            for b in ps_fb[fpar]:
                b.did_read(t_sc)
            g_t1[fpar].did_write(t_sc)
            sslot = tti % 4
            t_ev = r.op("sc", lambda e, fpar=fpar, sslot=sslot: e.activation(
                s_stage[sslot], t1_stage[fpar], AF.Silu), [t_sc] + g_sst[sslot].wr())
            g_t1[fpar].did_read(t_ev)
            g_sst[sslot].did_write(t_ev)
            t_st3 = dma("sd", s_d[tti][:, sbi * 512:(sbi + 1) * 512], s_stage[sslot],
                        deps=[t_ev], lane="Lss%d" % sslot)
            g_sst[sslot].did_read(t_st3)
            s_store[(tti, sbi)] = t_st3

    g_bc_t = Cf[:, 0:8192]
    b_bc_t = Cf[:, 8192:16384]
    s_rbs = [Af[:, 0:8192], Af[:, 8192:16384]]
    xq2_tm = AR_B[:, 0:8192]
    xq2_fm = AR_B[:, 8192:16384]
    zbuf = Bf[:, 8192:16384]
    ps_tp2 = [ps1[:, 0:512].bitcast(BF16), ps1[:, 512:1024].bitcast(BF16)]

    g_gb = G(*(g_w1st[0].wr() + g_w1st[1].wr()))
    seed_srb = (g_w1q[0].wr() + g_w1q[1].wr() + g_sst[0].wr() + g_sst[1].wr() +
                g_sst[2].wr() + g_sst[3].wr() + g_t1[0].wr() + g_t1[1].wr() +
                g_b1bc[0].wr() + g_b1bc[1].wr())
    g_srbs = [G(*seed_srb), G(*seed_srb)]
    g_z = G(*(g_Brl.wr()))
    g_xq2tm = G(*(g_Brl.wr()))
    g_xq2fm = G(*(g_Brl.wr()))
    g_mstats = G()
    g_scv2 = G()

    t_gbc = dma("sy", g_bc_t[:], din["g_ln"][:].rearrange("(a n) -> a n", a=1)
                .partition_broadcast(128).squeeze(1), deps=g_gb.wr(), lane="Lgb")
    t_bbc = dma("sy", b_bc_t[:], din["bt_ln"][:].rearrange("(a n) -> a n", a=1)
                .partition_broadcast(128).squeeze(1), deps=g_gb.wr(), lane="Lgb")
    g_gb.writer = [t_gbc, t_bbc]
    g_gb.readers = {}

    MT = 80
    QC = 96
    HW = INNER // 2
    xq2_store = {}
    zb_h = [zbuf[:, 0:HW], zbuf[:, HW:INNER]]
    gb_h = [g_bc_t[:, 0:HW], g_bc_t[:, HW:INNER]]
    bb_h = [b_bc_t[:, 0:HW], b_bc_t[:, HW:INNER]]
    prev = {}
    for grp in range(4):
        for j in range(4):
            tti = grp * 4 + j
            pb = tti % 2
            s_rb = s_rbs[pb]
            g_srb = g_srbs[pb]
            t_gl, t_sq, t_zv, t_z3, t_zq, t_am, t_p1, t_p2 = (
                [None, None] for _ in range(8))
            t_rbh = [None, None]
            for hf in range(2):
                t_rbh[hf] = dma(
                    "sy", s_rb[:, hf * HW:(hf + 1) * HW],
                    s_d[tti][:, hf * HW:(hf + 1) * HW],
                    deps=(g_srb.wr() if hf == 0 else []) +
                    [s_store[(tti, sbi)] for sbi in range(hf * 8, hf * 8 + 8)],
                    lane="Lsrb%d%d" % (pb, hf))
                if hf == 0:
                    g_srb.writer = [t_rbh[0]]
                    g_srb.readers = {}
                else:
                    g_srb.writer.append(t_rbh[1])
            for hf in range(2):
                t_gl[hf] = r.op("sc", lambda e, s=s_rb, hf=hf: e.activation(
                    s[:, hf * HW:(hf + 1) * HW], s[:, hf * HW:(hf + 1) * HW],
                    AF.Gelu, accum_out=col(QC + hf)),
                    [t_rbh[hf]] + ([prev.get("mu0")]
                                   if prev.get("mu0") else []))
                g_srb.writer.append(t_gl[hf])
            for hf in range(2):
                t_sq[hf] = r.op("sc", lambda e, s=s_rb, hf=hf: e.activation(
                    xq2_tm[:, hf * HW:(hf + 1) * HW], s[:, hf * HW:(hf + 1) * HW],
                    AF.Square, accum_out=col(QC + 2 + hf)),
                    [t_gl[hf]] + ([prev.get("e20")] if prev.get("e20") else []) +
                    prev.get("tpr%d" % hf, []) + g_xq2tm.wr())
                g_srb.did_read(t_sq[hf])
                g_xq2tm.did_write(t_sq[hf])
            t_mu0 = tt(col(MT), col(QC), col(QC + 1), ALU.add, [t_gl[0], t_gl[1]])
            t_e20 = tt(col(MT + 1), col(QC + 2), col(QC + 3), ALU.add,
                       [t_sq[0], t_sq[1]])
            prev["mu0"] = t_mu0
            prev["e20"] = t_e20
            t_mu = ts(col(MT), col(MT), 1.0 / INNER, None, ALU.mult, ALU.bypass,
                      [t_mu0])
            t_e2 = ts(col(MT + 1), col(MT + 1), 1.0 / INNER, None, ALU.mult,
                      ALU.bypass, [t_e20])
            t_mm2 = tt(col(MT + 2), col(MT), col(MT), ALU.mult, [t_mu])
            t_ag = tt(col(MT + 1), col(MT + 1), col(MT + 2), ALU.subtract,
                      [t_e2, t_mm2])
            t_u = ts(col(MT + 4), stp[:, MT + 1:MT + 2], 1.0, EPS, ALU.mult,
                     ALU.add, [t_ag])
            t_l = act(col(MT + 4), col(MT + 4), AF.Ln, deps=[t_u])
            t_r = act(col(MT + 5), col(MT + 4), AF.Exp, scale=-0.5, deps=[t_l])
            t_bx = ts(col(MT + 6), stp[:, MT:MT + 1], col(MT + 5), -1.0,
                      ALU.mult, ALU.mult, [t_r])
            for hf in range(2):
                t_zv[hf] = r.op("sc", lambda e, s=s_rb, hf=hf: e.activation(
                    zb_h[hf], s[:, hf * HW:(hf + 1) * HW], AF.Identity,
                    bias=col(MT + 6), scale=col(MT + 5)),
                    [t_bx, t_gl[hf]] + prev.get("zfree%d" % hf, g_z.wr()))
                g_srb.did_read(t_zv[hf])
                t_z2 = tt(zb_h[hf], zb_h[hf], gb_h[hf], ALU.mult,
                          [t_zv[hf]] + g_gb.rd())
                g_gb.did_read(t_z2)
                t_z3[hf] = tt(zb_h[hf], zb_h[hf], bb_h[hf], ALU.add,
                              [t_z2] + g_gb.rd())
                g_gb.did_read(t_z3[hf])
            for hf in range(2):
                t_zq[hf] = r.op("sc", lambda e, hf=hf: e.activation(
                    xq2_tm[:, hf * HW:(hf + 1) * HW], zb_h[hf], AF.Square,
                    accum_out=col(QC + 4 + hf)),
                    [t_z3[hf]] + ([prev.get("zqs")] if prev.get("zqs") else []))
                t_am[hf] = red(col(QC + 6 + hf), zb_h[hf], ALU.max,
                               [t_z3[hf]] + ([prev.get("am2")]
                                             if prev.get("am2") else []),
                               absv=True)
            t_zqs = tt(col(MT + 8), col(QC + 4), col(QC + 5), ALU.add,
                       [t_zq[0], t_zq[1]])
            t_am2 = tt(col(MT + 7), col(QC + 6), col(QC + 7), ALU.max,
                       [t_am[0], t_am[1]])
            prev["zqs"] = t_zqs
            prev["am2"] = t_am2
            g_mstats.writer = [t_zqs, t_am2]
            g_mstats.readers = {}
            t_u2 = ts(col(MT + 8), col(MT + 8), 1.0 / INNER, EPS, ALU.mult,
                      ALU.add, [t_zqs])
            t_l2 = act(col(MT + 8), col(MT + 8), AF.Ln, deps=[t_u2])
            t_r2 = act(col(MT + 9), col(MT + 8), AF.Exp, scale=-0.5, deps=[t_l2])
            t_mm_ = tt(col(MT + 10), col(MT + 7), col(MT + 9), ALU.mult,
                       [t_am2, t_r2])
            t_cl = ts(col(MT + 10), col(MT + 10), 1e-5, None, ALU.max,
                      ALU.bypass, [t_mm_])
            t_rc = act(col(MT + 11), col(MT + 10), AF.Ln, deps=[t_cl])
            t_s2 = act(col(MT + 11), col(MT + 11), AF.Exp, scale=-1.0,
                       bias=4.844187086458591, deps=[t_rc])
            t_c2 = tt(col(MT + 12), col(MT + 9), col(MT + 11), ALU.mult, [t_s2])
            t_i2 = ts(col(MT + 13), col(MT + 10), 1.0 / 127.0, None, ALU.mult,
                      ALU.bypass, [t_cl])
            if mid_dbg is not None:
                t_mdbg = dma("sd", mid_dbg[tti], stp[:, 80:96],
                             deps=[t_c2, t_i2, t_r, t_bx], lane="Ldbg")
                g_mstats.readers[("mdbg", tti)] = t_mdbg
            for hf in range(2):
                t_p1[hf] = ts(zb_h[hf], zb_h[hf], col(MT + 12), MAGIC, ALU.mult,
                              ALU.add, [t_c2, t_am[hf], t_zq[hf]])
                t_p2[hf] = r.op("sc", lambda e, hf=hf: e.activation(
                    xq2_tm[:, hf * HW:(hf + 1) * HW], zb_h[hf], AF.Identity,
                    bias=-MAGIC), [t_p1[hf]] + g_xq2tm.wr())
                g_xq2tm.did_write(t_p2[hf])
                prev["zfree%d" % hf] = [t_p2[hf]]
            t_iv2 = ts(invq2[:, tti:tti + 1], col(MT + 13), 1.0, None,
                       ALU.mult, ALU.bypass, [t_i2] + g_invq2.wr())
            g_invq2.writer.append(t_iv2)
            g_mstats.readers[("iv2", tti)] = t_iv2
            prev["iv2"] = t_iv2
            ev_toks = []
            tpr = [[], []]
            for b2 in range(8):
                par2 = b2 % 2
                hf = b2 // 4
                bank = pg(ps1, par2 * 512, (par2 + 1) * 512)
                tbs = []
                for u in range(8):
                    ic = b2 * 8 + u
                    tbs.append(tp(ps_tp2[par2][:, u * 128:(u + 1) * 128],
                                  xq2_tm[:, ic * 128:(ic + 1) * 128],
                                  [t_p2[hf]] + ([bb for b in bank for bb in b.wr()]
                                                if u == 0 else [])))
                    g_xq2tm.did_read(tbs[-1])
                for b in bank:
                    b.did_write(tbs[-1])
                tpr[hf].append(tbs[-1])
                t_ev = r.op("ve", lambda e, b2=b2, par2=par2: e.tensor_copy(
                    xq2_fm[:, b2 * 1024:(b2 + 1) * 1024], ps_tp2[par2]),
                    tbs + (g_xq2fm.wr() if b2 == 0 else []), is_async=True)
                for b in bank:
                    b.did_read(t_ev)
                ev_toks.append(t_ev)
            prev["tpr0"] = tpr[0]
            prev["tpr1"] = tpr[1]
            g_xq2fm.writer = ev_toks
            g_xq2fm.readers = {}
            t_st4 = dma("sd", xq2_d[tti], xq2_fm[:], deps=ev_toks, lane="Lq2")
            g_xq2fm.did_read(t_st4)
            xq2_store[tti] = t_st4

    g_z.writer = prev["zfree0"] + prev["zfree1"]
    g_z.readers = {}

    w2q_ar = [AR_A, AR_C]
    xq2_rb = [AR_B[:, 0:8192], AR_B[:, 8192:16384]]
    ap_rb = [AR_B[:, 16384:16896], AR_B[:, 16896:17408]]
    x_rb = [Bf[:, 8704:9216], Bf[:, 9216:9728]]
    t1_st = [Bf[:, 9728:10240], Bf[:, 10240:10752]]
    out_stg = [Bf[:, 10752:11264], Bf[:, 11264:11776]]
    b2bc = Bf[:, 11776:12288]
    w2_stage = Bf[:, 12288:16384]

    g_w2st = G(*(g_z.wr() + g_xq2tm.wr()))
    g_w2q = [G(*(g_srbs[0].wr() + g_srbs[1].wr())),
             G(*(g_gb.wr()))]
    g_xq2rb = [G(*(g_xq2tm.wr())), G(*(g_xq2fm.wr()))]
    g_eio = [G(*(g_z.wr())), G(*(g_z.wr()))]
    g_b2bc = G(*(g_z.wr()))
    scv2_toks = []
    for t_ in range(NT):
        scv2_toks.append(tt(scv2[:, t_:t_ + 1], invq2[:, t_:t_ + 1], col(W_I2),
                            ALU.mult, g_invq2.wr() + w2_ready + g_scv2.wr()))
        g_scv2.writer.append(scv2_toks[-1])
    g_invq2.did_read(scv2_toks[-1])

    w2q_toks = {}

    def stage_w2(oq):
        ar = w2q_ar[oq % 2]
        gq = g_w2q[oq % 2]
        toks = []
        for bt in range(8):
            t_ld = dma("sy", w2_stage[:].rearrange("p (ic f) -> p ic f", f=512),
                       din["w2t"][bt * 1024:(bt + 1) * 1024, oq * 512:(oq + 1) * 512]
                       .rearrange("(ic p) f -> p ic f", p=128),
                       deps=g_w2st.wr(), lane="Lw2s")
            g_w2st.did_write(t_ld)
            t_q1 = r.op("ve", lambda e: e.tensor_scalar(
                w2_stage, w2_stage, col(W_C2), col(W_NC2), ALU.min, ALU.max),
                [t_ld] + w2_ready)
            t_q2 = r.op("ve", lambda e: e.tensor_scalar(
                w2_stage, w2_stage, col(W_S2), MAGIC, ALU.mult, ALU.add), [t_q1])
            t_q3 = r.op("ve", lambda e, bt=bt, ar=ar: e.tensor_scalar(
                ar[:, bt * 4096:(bt + 1) * 4096], w2_stage, -MAGIC, None,
                ALU.add, ALU.bypass), [t_q2] + (gq.wr() if bt == 0 else []))
            g_w2st.did_write(t_q2)
            g_w2st.did_read(t_q3)
            if bt == 0:
                gq.did_write(t_q3)
            else:
                gq.writer.append(t_q3)
            toks.append(t_q3)
        w2q_toks[oq] = toks

    stage_w2(0)
    out_stores = []
    for oq in range(4):
        ar = w2q_ar[oq % 2]
        gq = g_w2q[oq % 2]
        qt = w2q_toks[oq]
        t_b2 = dma("sy", b2bc, din["b2"][oq * 512:(oq + 1) * 512]
                   .rearrange("(a n) -> a n", a=1).partition_broadcast(128).squeeze(1),
                   deps=g_b2bc.wr(), lane="Lb2")
        g_b2bc.did_write(t_b2)
        for tti in range(NT):
            par = tti % 2
            t_x2 = dma("sy", xq2_rb[par], xq2_d[tti],
                       deps=g_xq2rb[par].wr() + [xq2_store[tti]],
                       lane="Lx2%d" % par)
            g_xq2rb[par].did_write(t_x2)
            t_ap = dma("sy", ap_rb[par], aproj_d[tti][:, oq * 512:(oq + 1) * 512],
                       deps=g_eio[par].wr() + [aproj_store[tti]],
                       lane="Lapr%d" % par)
            t_xr = dma("sy", x_rb[par], xdv[tti * 128:(tti + 1) * 128,
                                            oq * 512:(oq + 1) * 512],
                       deps=g_eio[par].wr(), lane="Lxr%d" % par)
            mms = []
            for ic in range(NI):
                mms.append(mm(ps_fp[par], xq2_rb[par][:, ic * 128:(ic + 1) * 128],
                              ar[:, ic * 512:(ic + 1) * 512],
                              ic == 0, ic == NI - 1,
                              ([t_x2, qt[ic // 8]] +
                               [bb for b in ps_fb[par] for bb in b.wr()])
                              if ic == 0 else ([qt[ic // 8]] if ic % 8 == 0 else [])))
            g_xq2rb[par].did_read(mms[-1])
            gq.did_read(mms[-1])
            for b in ps_fb[par]:
                b.did_write(mms[-1])
            t_t1 = r.op("ve", lambda e, par=par, tti=tti:
                        e.scalar_tensor_tensor(t1_st[par], ps_fp[par],
                                               scv2[:, tti:tti + 1], ap_rb[par],
                                               ALU.mult, ALU.add),
                        [mms[-1], t_ap, scv2_toks[tti]], is_async=True)
            g_scv2.did_read(t_t1)
            for b in ps_fb[par]:
                b.did_read(t_t1)
            t_o1 = tt(t1_st[par], t1_st[par], x_rb[par], ALU.add, [t_t1, t_xr])
            t_o = tt(out_stg[par], t1_st[par], b2bc, ALU.add, [t_o1, t_b2])
            g_b2bc.did_read(t_o)
            t_so = dma("sd", out_d[tti * 128:(tti + 1) * 128, oq * 512:(oq + 1) * 512],
                       out_stg[par], deps=[t_o], lane="Lo%d" % par)
            g_eio[par].did_write(t_so)
            out_stores.append(t_so)
        if oq < 3:
            stage_w2(oq + 1)

    r.op("ve", lambda e: e.engine_nop(), out_stores[-2:])

    import contextlib
    stack = contextlib.ExitStack()
    with nc.Block() as block:
        r.emit(nc, block, stack)
    return nc, stack


def host_inputs(inputs):
    f32 = lambda a: np.ascontiguousarray(np.asarray(a, dtype=np.float32))
    rep = lambda s: np.ascontiguousarray(np.repeat(f32(s), 32, axis=0))
    shared = {
        "cq": f32(inputs["L0_attn_wq_codes"]), "sq": rep(inputs["L0_attn_wq_scales"]),
        "ck": f32(inputs["L0_attn_wk_codes"]), "sk": rep(inputs["L0_attn_wk_scales"]),
        "cv": f32(inputs["L0_attn_wv_codes"]), "sv": rep(inputs["L0_attn_wv_scales"]),
        "co": f32(inputs["L0_attn_wo_codes"]), "so": rep(inputs["L0_attn_wo_scales"]),
        "bq": f32(inputs["L0_attn_bq"]), "bk": f32(inputs["L0_attn_bk"]),
        "bv": f32(inputs["L0_attn_bv"]), "bo": f32(inputs["L0_attn_bo"]),
        "w1t": np.ascontiguousarray(f32(inputs["L0_ffn_w1"]).T),
        "b1": f32(inputs["L0_ffn_b1"]),
        "g_ln": f32(inputs["L0_ffn_ln_g"]), "bt_ln": f32(inputs["L0_ffn_ln_b"]),
        "w2t": np.ascontiguousarray(f32(inputs["L0_ffn_w2"]).T),
        "b2": f32(inputs["L0_ffn_b2"]),
        "ident": np.eye(128).astype(ml_dtypes.bfloat16),
        "ones_col": np.ones((128, 1), dtype=ml_dtypes.bfloat16),
        "ones2": np.ones((2, 128), dtype=ml_dtypes.bfloat16),
        "onesr_f": np.ones((1, 128), dtype=np.float32),
        "magicv": np.full((128, 1), MAGIC, dtype=np.float32),
        "ln127v": np.full((128, 1), 4.844187086458591, dtype=np.float32),
        "nmagicv": np.full((128, 1), -MAGIC, dtype=np.float32),
        "ones_col_f": np.ones((128, 1), dtype=np.float32),
    }
    x = f32(inputs["L0_attn_x"])
    return x, shared


def kernel(**inputs):
    x, shared = host_inputs(inputs)
    nc, stack = build_program()
    in_maps = [dict(shared, x=np.ascontiguousarray(x[c])) for c in range(8)]
    res = run_bass_kernel_spmd(nc, in_maps, core_ids=list(range(8)))
    out = np.stack([np.asarray(res.results[c]["out"]) for c in range(8)], axis=0)
    return out.astype(np.float32)


# version 19 (final)
# speedup vs baseline: 1.1749x; 1.0364x over previous
import numpy as np
import ml_dtypes
import concourse.bass as bass
import concourse.mybir as mybir
from concourse.bass_utils import run_bass_kernel_spmd

F32 = mybir.dt.float32
BF16 = mybir.dt.bfloat16
AF = mybir.ActivationFunctionType
ALU = mybir.AluOpType
AX = mybir.AxisListType

D = 2048
T = 2048
H = 16
INNER = 8192
ND = 16
NT = 16
NI = 64
MAGIC = float(np.float32(1.5 * 2 ** 23))
RSQ128 = float(1.0 / np.sqrt(np.float32(128.0)))
EPS = 1e-5
CLIP_W = 1.4995117

DEBUG_OUTS = set()
SERIALIZE_DMA = False
PHASE_LIMIT = None
P0_MIN = False
SERIALIZE_ALL = False
NO_ELIDE = {"vector", "gpsimd"}

COMPUTE_ENGS = {"sc", "ve", "te", "gc"}
STREAM_OF = {"sy": "sync", "sc": "scalar", "sd": "scalar",
             "ve": "vector", "te": "tensor", "gc": "gpsimd", "gd": "gpsimd"}


class Rec:
    def __init__(self):
        self.ops = []

    def op(self, eng, fn, deps=(), force=False, is_async=False, sem=None):
        i = len(self.ops)
        if SERIALIZE_ALL and i > 0:
            deps = list(deps) + [i - 1]
        if SERIALIZE_DMA and eng in ("sy", "sd", "gd"):
            prev = getattr(self, "_prev_dma_" + eng, None)
            if prev is not None:
                deps = list(deps) + [prev]
            setattr(self, "_prev_dma_" + eng, i)
        dd = []
        for d in deps:
            if d is None:
                continue
            if isinstance(d, (list, tuple)):
                dd += [x for x in d if x is not None]
            else:
                dd.append(d)
        self.ops.append({"eng": eng, "fn": fn, "deps": dd, "force": force,
                         "async": is_async, "sem": sem or eng,
                         "inc": 16 if eng in ("sy", "sd", "gd") else 1,
                         "needed": False})
        return i

    def emit(self, nc, block, stack):
        ops = self.ops
        for o in ops:
            if o["eng"] in ("sy", "sd", "gd"):
                o["needed"] = True
            if SERIALIZE_ALL:
                o["needed"] = True
            for d in o["deps"]:
                do = ops[d]
                if (not SERIALIZE_ALL and not o["force"] and not do["async"]
                        and STREAM_OF[do["eng"]] not in NO_ELIDE
                        and do["eng"] in COMPUTE_ENGS and o["eng"] in COMPUTE_ENGS
                        and STREAM_OF[do["eng"]] == STREAM_OF[o["eng"]]):
                    continue
                do["needed"] = True
        counts = {}
        for o in ops:
            if o["needed"]:
                counts[o["sem"]] = counts.get(o["sem"], 0) + o["inc"]
                o["val"] = counts[o["sem"]]
            else:
                o["val"] = None
        sems = {e: stack.enter_context(nc.semaphore(f"sem_{e}")) for e in counts}

        def emitter(stream_name):
            def inner(engobj):
                hw = {}
                for o in ops:
                    if STREAM_OF[o["eng"]] != stream_name:
                        continue
                    for d in o["deps"]:
                        do = ops[d]
                        if (not SERIALIZE_ALL and not o["force"] and not do["async"]
                                and stream_name not in NO_ELIDE
                                and do["eng"] in COMPUTE_ENGS and o["eng"] in COMPUTE_ENGS
                                and STREAM_OF[do["eng"]] == stream_name):
                            continue
                        v = do["val"]
                        if hw.get(do["sem"], 0) >= v:
                            continue
                        engobj.wait_ge(sems[do["sem"]], v)
                        hw[do["sem"]] = v
                    inst = o["fn"](engobj)
                    if o["needed"]:
                        inst.then_inc(sems[o["sem"]], o["inc"])
            return inner

        for s in sorted({STREAM_OF[o["eng"]] for o in ops}):
            getattr(block, s)(emitter(s))
        return counts


class Guard:
    def __init__(self, rec, seed=()):
        self.rec = rec
        self.writer = []
        self.readers = {}
        for t in seed:
            if t is not None:
                self.readers[("seed", len(self.readers))] = t

    def rd(self):
        return list(self.writer)

    def wr(self):
        return list(self.writer) + list(self.readers.values())

    def did_read(self, tok):
        if tok is not None:
            self.readers[self.rec.ops[tok]["eng"]] = tok

    def did_write(self, tok):
        self.writer = [tok]
        self.readers = {}


def build_program():
    nc = bass.Bass("TRN2", target_bir_lowering=False)
    r = Rec()
    G = lambda *seed: Guard(r, seed)

    din = {}
    for name, shape in [
        ("x", [T, D]),
        ("cq", [D, D]), ("sq", [D, D]), ("ck", [D, D]), ("sk", [D, D]),
        ("cv", [D, D]), ("sv", [D, D]), ("co", [D, D]), ("so", [D, D]),
        ("bq", [D]), ("bk", [D]), ("bv", [D]), ("bo", [D]),
        ("w1t", [D, INNER]), ("b1", [INNER]), ("g_ln", [INNER]), ("bt_ln", [INNER]),
        ("w2t", [INNER, D]), ("b2", [D]),
    ]:
        din[name] = nc.dram_tensor(name, shape, F32, kind="ExternalInput")
    for name, shape in [("ident", [128, 128]), ("ones_col", [128, 1]),
                        ("ones2", [2, 128])]:
        din[name] = nc.dram_tensor(name, shape, BF16, kind="ExternalInput")
    din["onesr_f"] = nc.dram_tensor("onesr_f", [1, 128], F32, kind="ExternalInput")
    din["ones_col_f"] = nc.dram_tensor("ones_col_f", [128, 1], F32, kind="ExternalInput")
    din["magicv"] = nc.dram_tensor("magicv", [128, 1], F32, kind="ExternalInput")
    din["ln127v"] = nc.dram_tensor("ln127v", [128, 1], F32, kind="ExternalInput")
    din["nmagicv"] = nc.dram_tensor("nmagicv", [128, 1], F32, kind="ExternalInput")

    out_d = nc.dram_tensor("out", [T, D], F32, kind="ExternalOutput")

    def scr(name, shape, dtype):
        kind = "ExternalOutput" if name in DEBUG_OUTS else "Internal"
        return nc.dram_tensor(name, shape, dtype, kind=kind)

    attnh_d = scr("attnh_d", [H, 128, T], BF16)
    aproj_d = scr("aproj_d", [NT, 128, D], BF16)
    s_d = scr("s_d", [NT, 128, INNER], F32)
    xq2_d = scr("xq2_d", [NT, 128, INNER], BF16)
    xq1_d = scr("xq1_d", [NT, 128, D], BF16)
    xnfm_dbg = scr("xnfm_dbg", [ND, 128, T], BF16) if "xnfm_dbg" in DEBUG_OUTS else None
    p0st_dbg = scr("p0st_dbg", [NT, 128, 32], F32) if "p0st_dbg" in DEBUG_OUTS else None
    p0bn_dbg = scr("p0bn_dbg", [NT, 128, 24], F32) if "p0bn_dbg" in DEBUG_OUTS else None
    wq_dbg = scr("wq_dbg", [128, 8], F32) if "wq_dbg" in DEBUG_OUTS else None
    mid_dbg = scr("mid_dbg", [NT, 128, 16], F32) if "mid_dbg" in DEBUG_OUTS else None

    AR_A = nc.alloc_sbuf_tensor("AR_A", [128, 32768], BF16)
    AR_B = nc.alloc_sbuf_tensor("AR_B", [128, 32768], BF16)
    AR_C = nc.alloc_sbuf_tensor("AR_C", [128, 32768], BF16)
    Bf = AR_B[:].bitcast(F32)
    Cf = AR_C[:].bitcast(F32)
    Af = AR_A[:].bitcast(F32)
    ident = nc.alloc_sbuf_tensor("identt", [128, 128], BF16)
    ones_col = nc.alloc_sbuf_tensor("ones_colt", [128, 1], BF16)
    ones_col_f = nc.alloc_sbuf_tensor("ones_col_ft", [128, 1], F32)
    ones2 = nc.alloc_sbuf_tensor("ones2t", [2, 128], BF16)
    onesr_f = nc.alloc_sbuf_tensor("onesr_ft", [1, 128], F32)
    bqkv = nc.alloc_sbuf_tensor("bqkv", [128, 48], F32)
    stp = nc.alloc_sbuf_tensor("stp", [128, 192], F32)
    wacc = nc.alloc_sbuf_tensor("wacc", [128, 8], F32)
    rowbuf = nc.alloc_sbuf_tensor("rowbuf", [1, 1024], F32)
    cmagic = nc.alloc_sbuf_tensor("cmagic", [128, 1], F32)
    nc.const_aps.register(F32, MAGIC, cmagic[:])
    cln127 = nc.alloc_sbuf_tensor("cln127", [128, 1], F32)
    nc.const_aps.register(F32, 4.844187086458591, cln127[:])
    cnmagic = nc.alloc_sbuf_tensor("cnmagic", [128, 1], F32)
    nc.const_aps.register(F32, -MAGIC, cnmagic[:])
    hilo_bo = nc.alloc_sbuf_tensor("hilo_bo", [2, D], BF16)
    invq1 = nc.alloc_sbuf_tensor("invq1", [128, 16], F32)
    invq2 = nc.alloc_sbuf_tensor("invq2", [128, 16], F32)
    scv1 = nc.alloc_sbuf_tensor("scv1", [128, 16], F32)
    scv2 = nc.alloc_sbuf_tensor("scv2", [128, 16], F32)

    ps0 = nc.alloc_psum_tensor("ps0", [128, 2048], F32)
    ps1 = nc.alloc_psum_tensor("ps1", [128, 2048], F32)

    pbk = [G() for _ in range(8)]

    def pg(t, lo, hi):
        base = 0 if t is ps0 else 4
        return [pbk[base + b] for b in range(lo // 512, (hi + 511) // 512)]

    def pgw(t, lo, hi):
        return [bb for b in pg(t, lo, hi) for bb in b.wr()]

    W_S1, W_C1, W_NC1, W_I1, W_S2, W_C2, W_NC2, W_I2, W_M1, W_M2 = range(32, 42)
    BIAS_H = 42

    def col(i):
        return stp[:, i:i + 1]

    def dma(q, dst, src, deps=(), lane=None, **kw):
        return r.op(q, lambda e, d=dst, s=src, kw=kw: e.dma_start(out=d, in_=s, **kw),
                    deps, sem=lane)

    def ts(out, in0, s1_, s2_, op0, op1, deps=(), eng="ve"):
        return r.op(eng, lambda e, o=out, i=in0, a=s1_, b=s2_, o0=op0, o1=op1:
                    e.tensor_scalar(o, i, a, b, o0, o1), deps)

    def tt(out, in0, in1, aop, deps=(), eng="ve"):
        return r.op(eng, lambda e, o=out, a=in0, b=in1, p=aop: e.tensor_tensor(o, a, b, p), deps)

    def act(out, in_, func, bias=0.0, scale=1.0, deps=()):
        return r.op("sc", lambda e, o=out, i=in_, f=func, b=bias, s=scale:
                    e.activation(o, i, f, bias=b, scale=s), deps)

    def mm(out, lhsT, rhs, start, stop, deps=()):
        return r.op("te", lambda e, o=out, l=lhsT, rr_=rhs, st=start, sp=stop:
                    e.matmul(o, l, rr_, start=st, stop=sp), deps)

    def tp(out, in_, deps=()):
        return r.op("te", lambda e, o=out, i=in_: e.transpose(o, i, ident[:]), deps)

    def red(out, in_, aop, deps=(), absv=None, eng="ve"):
        return r.op(eng, lambda e, o=out, i=in_, p=aop, a=absv:
                    e.tensor_reduce(o, i, AX.X, p, apply_absolute_value=a), deps,
                    is_async=True)

    tok_c = {}
    last_c = None
    for nm, dst in [("ident", ident), ("ones_col", ones_col), ("ones2", ones2),
                    ("onesr_f", onesr_f), ("ones_col_f", ones_col_f),
                    ("magicv", cmagic), ("ln127v", cln127), ("nmagicv", cnmagic)]:
        last_c = dma("sy", dst[:], din[nm][:], lane="Lcst")
    for j, nm in enumerate(("bq", "bk", "bv")):
        last_c = dma("sy", bqkv[:, j * 16:(j + 1) * 16],
                     din[nm][:].rearrange("(c p) -> p c", p=128),
                     allow_slow_non_contiguous=True, lane="Lcst")
    for nm in ("ident", "ones_col", "ones2", "onesr_f", "ones_col_f",
               "magicv", "ln127v"):
        tok_c[nm] = last_c
    t_magic = last_c
    tok_bqkv = [last_c]

    def make_hilo(dst, src_dram, n, row_ap, lo_ap, deps):
        t0 = dma("sy", row_ap, src_dram[:].rearrange("(a n) -> a n", a=1), deps=deps,
                 lane="Lhl")
        t1 = r.op("ve", lambda e, d=dst, rw=row_ap: e.tensor_copy(d[0:1, :], rw), [t0])
        t2 = tt(lo_ap, row_ap, dst[0:1, :], ALU.subtract, [t1])
        t3 = dma("sy", dst[1:2, :], lo_ap, deps=[t2], lane="Lhl")
        return t3

    xdv = din["x"]
    xn_t = [AR_C[:, 0:2048], AR_C[:, 2048:4096]]
    xq1_t = [AR_C[:, 4096:6144], AR_C[:, 6144:8192]]
    x_stage = [Cf[:, 4096:6144], Cf[:, 6144:8192]]
    tmp_rd = Cf[:, 8192:10240]
    xq1_sp = [AR_B[:, 0:2048], AR_B[:, 2048:4096]]
    xnf32 = Cf[:, 10240:12288]
    ps_tpA = ps0[:, 0:1024].bitcast(BF16)
    ps_tpB = ps0[:, 1024:2048].bitcast(BF16)

    g_x = [G(), G()]
    g_xn = [G(), G()]
    g_xq1t = [G(), G()]
    g_xq1sp = [G(), G()]
    g_tmp = G()
    g_xnf = G()
    ev_a_toks = []
    g_invq1 = G()
    g_invq2 = G()
    g_A = G()
    g_stats = [G(), G()]
    xq1_store = {}

    MV, _VV, MX, MN, RLN, BXN, T0, T1c, T2c, CQ, INVQ, BR2 = range(12)

    def p0_col(i, par):
        return stp[:, par * 16 + i:par * 16 + i + 1]

    for tti in range(NT):
        par = tti % 2
        bnS = 96 if par == 0 else 56
        t_x = dma("sy", x_stage[par], xdv[tti * 128:(tti + 1) * 128, :],
                  deps=g_x[par].wr(), lane="Lx%d" % par)
        g_x[par].did_write(t_x)
        t_bn = []
        for c in range(4):
            t_bn.append(r.op("ve", lambda e, c=c, par=par, bnS=bnS: e.bn_stats(
                stp[:, bnS + c * 6:bnS + c * 6 + 6],
                x_stage[par][:, c * 512:(c + 1) * 512]),
                [t_x] + g_stats[par].wr(), is_async=True))
            g_x[par].did_read(t_bn[-1])
        t_ag = r.op("ve", lambda e, par=par, bnS=bnS: e.bn_aggr(
            stp[:, par * 16 + MV:par * 16 + MV + 2], stp[:, bnS:bnS + 24]), t_bn,
            force=True, is_async=True)
        t_mx = red(p0_col(MX, par), x_stage[par], ALU.max, [t_x] + g_stats[par].wr())
        t_mn = red(p0_col(MN, par), x_stage[par], ALU.min, [t_x] + g_stats[par].wr())
        g_x[par].did_read(t_mx)
        g_x[par].did_read(t_mn)
        t_u = ts(p0_col(T0, par), p0_col(MV + 1, par), 1.0, EPS, ALU.mult, ALU.add, [t_ag])
        t_ln = act(p0_col(T0, par), p0_col(T0, par), AF.Ln, deps=[t_u])
        t_rl = act(p0_col(RLN, par), p0_col(T0, par), AF.Exp, scale=-0.5, deps=[t_ln])
        t_bx = ts(p0_col(BXN, par), p0_col(MV, par), p0_col(RLN, par), -1.0,
                  ALU.mult, ALU.mult, [t_rl])
        t_xnf = r.op("sc", lambda e, par=par: e.activation(
            xnf32, x_stage[par], AF.Identity,
            bias=p0_col(BXN, par), scale=p0_col(RLN, par)),
            [t_bx, t_x] + g_xnf.wr())
        g_xnf.did_write(t_xnf)
        g_x[par].did_read(t_xnf)
        t_xn = r.op("ve", lambda e, par=par: e.tensor_copy(xn_t[par], xnf32),
                    [t_xnf] + g_xn[par].wr())
        g_xnf.did_read(t_xn)
        g_xn[par].did_write(t_xn)
        if P0_MIN:
            g_stats[par].did_write(t_xn)
            tps_a = []
            for j in range(16):
                tps_a.append(tp(ps_tpA[:, j * 128:(j + 1) * 128],
                                xn_t[par][:, j * 128:(j + 1) * 128],
                                [t_xn, tok_c["ident"]] +
                                (pgw(ps0, 0, 1024) if j == 0 else [])))
                g_xn[par].did_read(tps_a[-1])
            for b in pg(ps0, 0, 1024):
                b.did_write(tps_a[-1])
            ev_a = r.op("ve", lambda e, tti=tti: e.tensor_copy(
                AR_A[:].rearrange("p (dc t) -> p dc t", t=T)[:, :, tti * 128:tti * 128 + 128],
                ps_tpA[:].rearrange("p (dc j) -> p dc j", j=128)),
                tps_a + g_A.wr())
            for b in pg(ps0, 0, 1024):
                b.did_read(ev_a)
            g_A.did_write(ev_a)
            xq1_store[tti] = t_xn
            continue
        t_a = tt(p0_col(T1c, par), p0_col(RLN, par), p0_col(RLN, par), ALU.mult, [t_rl])
        t_b = tt(p0_col(T1c, par), p0_col(MV + 1, par), p0_col(T1c, par), ALU.mult, [t_a])
        t_c = ts(p0_col(T1c, par), p0_col(T1c, par), 1.0, EPS, ALU.mult, ALU.add, [t_b])
        t_d = act(p0_col(T1c, par), p0_col(T1c, par), AF.Ln, deps=[t_c])
        t_r2 = act(p0_col(BR2, par), p0_col(T1c, par), AF.Exp, scale=-0.5, deps=[t_d])
        t_e = tt(p0_col(T0, par), p0_col(MX, par), p0_col(MV, par), ALU.subtract,
                 [t_mx, t_ag, t_rl])
        t_f = tt(p0_col(T2c, par), p0_col(MV, par), p0_col(MN, par), ALU.subtract,
                 [t_mn, t_ag])
        t_g = tt(p0_col(T0, par), p0_col(T0, par), p0_col(T2c, par), ALU.max, [t_e, t_f])
        t_h = tt(p0_col(T0, par), p0_col(T0, par), p0_col(RLN, par), ALU.mult, [t_g])
        t_i = tt(p0_col(T0, par), p0_col(T0, par), p0_col(BR2, par), ALU.mult,
                 [t_h, t_r2])
        t_j = ts(p0_col(T0, par), p0_col(T0, par), 1e-5, None, ALU.max, ALU.bypass, [t_i])
        t_k = act(p0_col(T1c, par), p0_col(T0, par), AF.Ln, deps=[t_j, t_r2])
        t_l = act(p0_col(T1c, par), p0_col(T1c, par), AF.Exp, scale=-1.0,
                  bias=4.844187086458591, deps=[t_k])
        t_m = tt(p0_col(CQ, par), p0_col(BR2, par), p0_col(T1c, par), ALU.mult, [t_l])
        t_n = ts(p0_col(INVQ, par), p0_col(T0, par), 1.0 / 127.0, None, ALU.mult,
                 ALU.bypass, [t_j])
        t_p1 = r.op("sc", lambda e, par=par: e.activation(
            tmp_rd, xnf32, AF.Identity, bias=MAGIC, scale=p0_col(CQ, par)),
            [t_m, t_xnf, t_magic] + g_tmp.wr())
        g_tmp.did_write(t_p1)
        g_xnf.did_read(t_p1)
        t_p2 = ts(xq1_t[par], tmp_rd, -MAGIC, None, ALU.add, ALU.bypass,
                  [t_p1] + g_xq1t[par].wr())
        g_tmp.did_read(t_p2)
        g_xq1t[par].did_write(t_p2)
        t_iv = ts(invq1[:, tti:tti + 1], p0_col(INVQ, par), 1.0, None,
                  ALU.mult, ALU.bypass, [t_n] + g_invq1.wr())
        g_invq1.writer.append(t_iv)
        g_stats[par].did_write(t_p2)
        g_stats[par].readers["iv"] = t_iv
        if p0st_dbg is not None:
            t_dbg = dma("sd", p0st_dbg[tti][:, 0:16], stp[:, par * 16:par * 16 + 16],
                        deps=[t_p2, t_n, t_m, t_bx, t_xn], lane="Ldbg")
            g_stats[par].readers[("dbg", tti)] = t_dbg
        tps_a = []
        for j in range(16):
            tps_a.append(tp(ps_tpA[:, j * 128:(j + 1) * 128],
                            xn_t[par][:, j * 128:(j + 1) * 128],
                            [t_xn, tok_c["ident"]] +
                            (pgw(ps0, 0, 1024) if j == 0 else [])))
            g_xn[par].did_read(tps_a[-1])
        for b in pg(ps0, 0, 1024):
            b.did_write(tps_a[-1])
        ev_a = r.op("ve", lambda e, tti=tti: e.tensor_copy(
            AR_A[:].rearrange("p (dc t) -> p dc t", t=T)[:, :, tti * 128:tti * 128 + 128],
            ps_tpA[:].rearrange("p (dc j) -> p dc j", j=128)),
            tps_a + g_A.wr(), is_async=True)
        for b in pg(ps0, 0, 1024):
            b.did_read(ev_a)
        g_A.writer.append(ev_a)
        ev_a_toks.append(ev_a)
        tps_b = []
        for j in range(16):
            tps_b.append(tp(ps_tpB[:, j * 128:(j + 1) * 128],
                            xq1_t[par][:, j * 128:(j + 1) * 128],
                            [t_p2] + (pgw(ps0, 1024, 2048) if j == 0 else [])))
            g_xq1t[par].did_read(tps_b[-1])
        for b in pg(ps0, 1024, 2048):
            b.did_write(tps_b[-1])
        ev_b = r.op("ve", lambda e, par=par: e.tensor_copy(
            xq1_sp[par], ps_tpB[:]), tps_b + g_xq1sp[par].wr(), is_async=True)
        for b in pg(ps0, 1024, 2048):
            b.did_read(ev_b)
        g_xq1sp[par].did_write(ev_b)
        t_spill = dma("sd", xq1_d[tti], xq1_sp[par], deps=[ev_b], lane="Lq1%d" % par)
        g_xq1sp[par].did_read(t_spill)
        xq1_store[tti] = t_spill

    if xnfm_dbg is not None:
        for dc in range(ND):
            g_A.did_read(dma("sd", xnfm_dbg[dc],
                             AR_A[:, dc * T:(dc + 1) * T], deps=g_A.rd(),
                             lane="Ldbg"))

    if PHASE_LIMIT == "p0":
        r.op("ve", lambda e: e.engine_nop(),
             [xq1_store[NT - 1]] + g_A.rd())
        import contextlib
        stack = contextlib.ExitStack()
        with nc.Block() as block:
            r.emit(nc, block, stack)
        return nc, stack

    mean_stage = [Bf[:, 12288:14336], Bf[:, 14336:16384]]
    CWM = 2048
    g_mst = [G(), G()]
    t_z0 = r.op("gc", lambda e: e.memset(wacc[:], 0.0), [])
    acc_last = {0: t_z0, 1: t_z0}
    mhat_tok = {}
    nmhat_tok = {}
    mean_jobs = []
    job_ix = [0]
    for wi, wname in enumerate(("w1t", "w2t")):
        wd = din[wname]
        nch = wd.shape[1] // CWM
        nrb = wd.shape[0] // 128
        mcol = W_M1 if wi == 0 else W_M2
        for dc2 in range(nrb):
            for hh in range(nch):
              def _mean_job(wi=wi, wd=wd, dc2=dc2, hh=hh, mcol=mcol):
                  p = job_ix[0] % 2
                  job_ix[0] += 1
                  mst = mean_stage[p]
                  t_ld = dma("sy", mst,
                             wd[dc2 * 128:(dc2 + 1) * 128, hh * CWM:(hh + 1) * CWM],
                             deps=g_mst[p].wr(), lane="Lms" + str(p))
                  g_mst[p].did_write(t_ld)
                  rs = wacc[:, 4 + p:5 + p]
                  t_rd = red(rs, mst, ALU.add, [t_ld, acc_last.get(2 + p, t_z0)],
                             absv=True)
                  g_mst[p].did_read(t_rd)
                  acc_last[2 + p] = t_rd
                  if dc2 == 0 and hh == 0:
                      t_m0 = mm(ps1[0:1, 0:1], ones_col_f[:], rs,
                                True, True,
                                [t_rd, tok_c["ones_col_f"]] + pgw(ps1, 0, 512))
                      for b in pg(ps1, 0, 512):
                          b.did_write(t_m0)
                      t_m1 = r.op("ve", lambda e: e.tensor_scalar(
                          rowbuf[0:1, 0:1], ps1[0:1, 0:1], 1.0 / (128 * CWM), None,
                          ALU.mult, ALU.bypass), [t_m0])
                      for b in pg(ps1, 0, 512):
                          b.did_read(t_m1)
                      t_m2 = mm(ps1[:, 0:1], onesr_f[:], rowbuf[0:1, 0:1], True, True,
                                [t_m1, tok_c["onesr_f"]] + pgw(ps1, 0, 512))
                      for b in pg(ps1, 0, 512):
                          b.did_write(t_m2)
                      t_mh = r.op("ve", lambda e, mcol=mcol: e.tensor_copy(
                          col(mcol), ps1[:, 0:1]), [t_m2], is_async=True)
                      for b in pg(ps1, 0, 512):
                          b.did_read(t_mh)
                      mhat_tok[wi] = t_mh
                      nmhat_tok[wi] = ts(col(mcol + 6), col(mcol), -float(CWM), None,
                                         ALU.mult, ALU.bypass, [t_mh])
                  t_ct = ts(wacc[:, 6 + p:7 + p], rs, col(mcol + 6), None,
                            ALU.add, ALU.bypass, [t_rd, nmhat_tok[wi]])
                  acc_last[2 + p] = t_ct
                  t_ac = r.op("ve", lambda e, wi=wi, p=p: e.tensor_tensor(
                      wacc[:, wi:wi + 1], wacc[:, wi:wi + 1], wacc[:, 6 + p:7 + p],
                      ALU.add), [t_ct, acc_last[wi]])
                  acc_last[wi] = t_ac
              mean_jobs.append(_mean_job)

    def finish_wmean(wi, s_c, c_c, nc_c, i_c):
        mcol = W_M1 if wi == 0 else W_M2
        deps = [acc_last[wi]] + pgw(ps1, 0, 512)
        t_mm = mm(ps1[0:1, 0:1], ones_col_f[:], wacc[:, wi:wi + 1], True, True,
                  deps + [tok_c["ones_col_f"]])
        for b in pg(ps1, 0, 512):
            b.did_write(t_mm)
        t_c0 = r.op("ve", lambda e: e.tensor_scalar(
            rowbuf[0:1, 0:1], ps1[0:1, 0:1], 1.0 / (D * INNER), None,
            ALU.mult, ALU.bypass), [t_mm])
        for b in pg(ps1, 0, 512):
            b.did_read(t_c0)
        t_c1 = r.op("ve", lambda e, mcol=mcol: e.tensor_scalar(
            rowbuf[0:1, 0:1], rowbuf[0:1, 0:1], col(mcol)[0:1, :], 1e-5,
            ALU.add, ALU.max), [t_c0, mhat_tok[wi]])
        t_sp = mm(ps1[:, 0:1], onesr_f[:], rowbuf[0:1, 0:1], True, True,
                  [t_c1, tok_c["onesr_f"]] + pgw(ps1, 0, 512))
        for b in pg(ps1, 0, 512):
            b.did_write(t_sp)
        t_mc = r.op("ve", lambda e: e.tensor_copy(col(i_c), ps1[:, 0:1]), [t_sp],
                    is_async=True)
        for b in pg(ps1, 0, 512):
            b.did_read(t_mc)
        t_s0 = act(col(s_c), col(i_c), AF.Ln, deps=[t_mc])
        t_s1 = act(col(s_c), col(s_c), AF.Exp, scale=-1.0, deps=[t_s0])
        t_u = tt(stp[:, 44 + wi:45 + wi], col(i_c), col(s_c), ALU.mult, [t_s1, t_mc])
        t_v = ts(stp[:, 44 + wi:45 + wi], stp[:, 44 + wi:45 + wi], -1.0, 2.0,
                 ALU.mult, ALU.add, [t_u])
        t_s = tt(col(s_c), col(s_c), stp[:, 44 + wi:45 + wi], ALU.mult, [t_v])
        t_cc = ts(col(c_c), col(i_c), CLIP_W, None, ALU.mult, ALU.bypass, [t_mc])
        t_nc = ts(col(nc_c), col(c_c), -1.0, None, ALU.mult, ALU.bypass, [t_cc])
        return [t_s, t_cc, t_nc, t_mc]

    qh_b = AR_B[:, 4096:6144]
    kh_b = AR_B[:, 6144:8192]
    vh_b = AR_B[:, 8192:10240]
    vtm_b = AR_B[:, 10240:12288]
    sqr_b = AR_B[:, 12288:14336]
    ah_b = AR_B[:, 14336:16384]
    pt_b = [AR_B[:, 16384:17408], AR_B[:, 17408:18432]]
    nrm_pool = AR_B[:, 18432:18434]
    rown = Bf[0:1, 9220:9228]
    bcast_sb = Bf[:, 9472:10496]
    rsum_row = rowbuf[0:1, 0:1024]

    g_qh, g_kh, g_vh, g_vtm, g_sqr, g_ah = G(), G(), G(), G(), G(), G()
    g_pt = [G(), G()]
    g_nrmp, g_rown, g_rsum, g_bcsb = G(), G(), G(), G()
    g_bias_h = G()

    stage_c = AR_C[:, 0:4096]
    stage_s = AR_C[:, 4096:8192]
    wslot = [AR_C[:, 8192 + i * 4096:8192 + (i + 1) * 4096] for i in range(6)]
    seedC = g_xn[0].wr() + g_xn[1].wr() + g_xq1t[0].wr() + g_xq1t[1].wr() + \
        g_x[0].wr() + g_x[1].wr() + g_tmp.wr()
    g_stage = G(*seedC)
    g_wslot = [G(*seedC) for _ in range(6)]

    CW = {"q": ("cq", "sq"), "k": ("ck", "sk"), "v": ("cv", "sv")}
    slice_tok = {}
    sidx_box = [0]

    def prep_group(g2):
        for wn in ("q", "k", "v"):
            sidx = sidx_box[0]
            slot = sidx % 6
            cdn, sdn = CW[wn]
            t_c = dma("gd", stage_c[:].rearrange("p (dc f) -> p dc f", f=256),
                      din[cdn][:, g2 * 256:(g2 + 1) * 256]
                      .rearrange("(dc p) f -> p dc f", p=128),
                      deps=g_stage.wr(), lane="Lstc")
            t_s = dma("gd", stage_s[:].rearrange("p (dc f) -> p dc f", f=256),
                      din[sdn][:, g2 * 256:(g2 + 1) * 256]
                      .rearrange("(dc p) f -> p dc f", p=128),
                      deps=g_stage.wr(), lane="Lsts")
            g_stage.writer = [t_c, t_s]
            g_stage.readers = {}
            t_q = tt(wslot[slot], stage_c, stage_s, ALU.mult,
                     [t_c, t_s] + g_wslot[slot].wr())
            g_stage.did_read(t_q)
            g_wslot[slot].did_write(t_q)
            slice_tok[(g2, wn)] = (slot, t_q)
            sidx_box[0] += 1

    prep_group(0)
    prep_group(1)

    ps_sc = [ps0[:, 0:1024], ps0[:, 1024:2048]]
    ps_av = ps1[:, 0:1024]
    ps_sum = ps1[0:1, 1024:2048]
    ps_nrm = ps1[:, 0:16]
    ps_t128 = ps1[0:1, 32:96].bitcast(BF16)
    ps_tiny = ps1[:, 16:17]
    ps_vt = ps0[:, 0:1024].bitcast(BF16)

    attn_store_toks = {}
    mj = 0
    for h in range(H):
        g2, hl = h // 2, h % 2
        if hl == 0 and 1 <= g2 < 7:
            prep_group(g2 + 1)
        for wn, dstb, gdst, bcol in (("q", qh_b, g_qh, 0), ("k", kh_b, g_kh, 16)):
            slot, t_wq = slice_tok[(g2, wn)]
            wv = wslot[slot][:].rearrange("p (dc f) -> p dc f", f=256)
            for tq in range(4):
                par = tq % 2
                bank = pg(ps0, par * 1024, par * 1024 + 512)
                mms = []
                for dc in range(ND):
                    mms.append(mm(ps_sc[par][:, 0:512],
                                  wv[:, dc, hl * 128:(hl + 1) * 128],
                                  AR_A[:, dc * T + tq * 512:dc * T + tq * 512 + 512],
                                  dc == 0, dc == ND - 1,
                                  ([t_wq, ev_a_toks[4 * tq + 3]] +
                                   [bb for b in bank for bb in b.wr()]) if dc == 0 else []))
                    g_A.did_read(mms[-1])
                g_wslot[slot].did_read(mms[-1])
                for b in bank:
                    b.did_write(mms[-1])
                t_ev = r.op("sc", lambda e, dstb=dstb, tq=tq, par=par, h=h, bcol=bcol:
                            e.activation(dstb[:, tq * 512:(tq + 1) * 512],
                                         ps_sc[par][:, 0:512], AF.Identity,
                                         bias=bqkv[:, bcol + h:bcol + h + 1]),
                            [mms[-1]] + tok_bqkv + (gdst.wr() if tq == 0 else []))
                for b in bank:
                    b.did_read(t_ev)
                if tq == 0:
                    gdst.did_write(t_ev)
                else:
                    gdst.writer.append(t_ev)
        def pnorm(src_g, src_b, out_i):
            t_sq = tt(sqr_b, src_b, src_b, ALU.mult, src_g.rd() + g_sqr.wr())
            src_g.did_read(t_sq)
            g_sqr.did_write(t_sq)
            nm = []
            for tti in range(NT):
                nm.append(mm(ps_nrm[:, tti:tti + 1],
                             sqr_b[:, tti * 128:(tti + 1) * 128], ones_col[:],
                             True, True,
                             [t_sq, tok_c["ones_col"]] +
                             (pgw(ps1, 0, 512) if tti == 0 else [])))
                g_sqr.did_read(nm[-1])
            for b in pg(ps1, 0, 512):
                b.did_write(nm[-1])
            t_red = red(nrm_pool[:, out_i:out_i + 1], ps_nrm[:], ALU.max,
                        nm + g_nrmp.wr())
            for b in pg(ps1, 0, 512):
                b.did_read(t_red)
            g_nrmp.did_write(t_red)
            t_t = r.op("te", lambda e, out_i=out_i: e.transpose(
                ps_t128, nrm_pool[:, out_i:out_i + 1], ident[:]),
                [t_red] + pgw(ps1, 0, 512))
            g_nrmp.did_read(t_t)
            for b in pg(ps1, 0, 512):
                b.did_write(t_t)
            t_mx = red(rown[0:1, out_i:out_i + 1], ps_t128, ALU.max,
                       [t_t] + g_rown.wr())
            for b in pg(ps1, 0, 512):
                b.did_read(t_mx)
            g_rown.did_write(t_mx)
            return t_mx

        t_mq = pnorm(g_qh, qh_b, 0)
        t_mk = pnorm(g_kh, kh_b, 1)
        t_pm = tt(rown[0:1, 2:3], rown[0:1, 0:1], rown[0:1, 1:2], ALU.mult, [t_mq, t_mk])
        t_l1 = act(rown[0:1, 2:3], rown[0:1, 2:3], AF.Ln, deps=[t_pm])
        t_l2 = act(rown[0:1, 2:3], rown[0:1, 2:3], AF.Exp, scale=0.5, deps=[t_l1])
        g_rown.did_write(t_l2)
        t_sp = mm(ps_tiny, onesr_f[:], rown[0:1, 2:3], True, True,
                  [t_l2] + pgw(ps1, 0, 512))
        g_rown.did_read(t_sp)
        for b in pg(ps1, 0, 512):
            b.did_write(t_sp)
        t_bh = ts(col(BIAS_H), ps_tiny, -RSQ128, None, ALU.mult, ALU.bypass,
                  [t_sp] + g_bias_h.wr())
        for b in pg(ps1, 0, 512):
            b.did_read(t_bh)
        g_bias_h.did_write(t_bh)

        for wn, dstb, gdst, bcol in (("v", vh_b, g_vh, 32),):
            slot, t_wq = slice_tok[(g2, wn)]
            wv = wslot[slot][:].rearrange("p (dc f) -> p dc f", f=256)
            for tq in range(4):
                par = tq % 2
                bank = pg(ps0, par * 1024, par * 1024 + 512)
                mms = []
                for dc in range(ND):
                    mms.append(mm(ps_sc[par][:, 0:512],
                                  wv[:, dc, hl * 128:(hl + 1) * 128],
                                  AR_A[:, dc * T + tq * 512:dc * T + tq * 512 + 512],
                                  dc == 0, dc == ND - 1,
                                  ([t_wq, ev_a_toks[4 * tq + 3]] +
                                   [bb for b in bank for bb in b.wr()]) if dc == 0 else []))
                    g_A.did_read(mms[-1])
                g_wslot[slot].did_read(mms[-1])
                for b in bank:
                    b.did_write(mms[-1])
                t_ev = r.op("sc", lambda e, dstb=dstb, tq=tq, par=par, h=h, bcol=bcol:
                            e.activation(dstb[:, tq * 512:(tq + 1) * 512],
                                         ps_sc[par][:, 0:512], AF.Identity,
                                         bias=bqkv[:, bcol + h:bcol + h + 1]),
                            [mms[-1]] + tok_bqkv + (gdst.wr() if tq == 0 else []))
                for b in bank:
                    b.did_read(t_ev)
                if tq == 0:
                    gdst.did_write(t_ev)
                else:
                    gdst.writer.append(t_ev)
        tps = []
        for kt in range(NT):
            tps.append(tp(ps_vt[:, kt * 128:(kt + 1) * 128],
                          vh_b[:, kt * 128:(kt + 1) * 128],
                          g_vh.rd() + (pgw(ps0, 0, 1024) if kt == 0 else [])))
            g_vh.did_read(tps[-1])
        for b in pg(ps0, 0, 1024):
            b.did_write(tps[-1])
        t_vtm = r.op("ve", lambda e: e.tensor_copy(vtm_b[:], ps_vt[:]),
                     tps + g_vtm.wr(), is_async=True)
        for b in pg(ps0, 0, 1024):
            b.did_read(t_vtm)
        g_vtm.did_write(t_vtm)

        for _ in range(12):
            if mj < len(mean_jobs):
                mean_jobs[mj]()
                mj += 1
        ah_parts = []
        for qh2 in range(2):
            av_last = []
            sum_last = []
            exp_par0 = None
            for kt in range(NT):
                par = kt % 2
                bank_s = pg(ps0, par * 1024, (par + 1) * 1024)
                m1 = mm(ps_sc[par][:, 0:512], kh_b[:, kt * 128:(kt + 1) * 128],
                        qh_b[:, qh2 * 1024:qh2 * 1024 + 512], True, True,
                        g_kh.rd() + g_qh.rd() + [bb for b in bank_s for bb in b.wr()])
                m2 = mm(ps_sc[par][:, 512:1024], kh_b[:, kt * 128:(kt + 1) * 128],
                        qh_b[:, qh2 * 1024 + 512:qh2 * 1024 + 1024], True, True, [])
                g_kh.did_read(m2)
                g_qh.did_read(m2)
                for b in bank_s:
                    b.did_write(m2)
                t_exp = r.op("sc", lambda e, par=par: e.activation(
                    pt_b[par], ps_sc[par], AF.Exp, bias=col(BIAS_H), scale=RSQ128),
                    [m1, m2] + g_bias_h.rd() + g_pt[par].wr())
                g_bias_h.did_read(t_exp)
                for b in bank_s:
                    b.did_read(t_exp)
                g_pt[par].did_write(t_exp)
                if par == 0:
                    exp_par0 = t_exp
                s1_ = mm(ps_sum[:, 0:512], ones_col[:], pt_b[par][:, 0:512],
                         kt == 0, kt == NT - 1,
                         [t_exp] + (pgw(ps1, 1024, 2048) if kt == 0 else []))
                s2_ = mm(ps_sum[:, 512:1024], ones_col[:], pt_b[par][:, 512:1024],
                         kt == 0, kt == NT - 1, [])
                a1 = mm(ps_av[:, 0:512], vtm_b[:, kt * 128:(kt + 1) * 128],
                        pt_b[par][:, 0:512], kt == 0, kt == NT - 1,
                        [t_exp] + g_vtm.rd() + (pgw(ps1, 0, 1024) if kt == 0 else []))
                a2 = mm(ps_av[:, 512:1024], vtm_b[:, kt * 128:(kt + 1) * 128],
                        pt_b[par][:, 512:1024], kt == 0, kt == NT - 1, [])
                g_vtm.did_read(a2)
                g_pt[par].did_read(a2)
                if kt == NT - 1:
                    sum_last = [s1_, s2_]
                    av_last = [a1, a2]
                    for b in pg(ps1, 1024, 2048):
                        b.did_write(s2_)
                    for b in pg(ps1, 0, 1024):
                        b.did_write(a2)
            t_rc0 = act(rsum_row, ps_sum[:], AF.Ln,
                        deps=sum_last + g_rsum.wr())
            t_rc = act(rsum_row, rsum_row, AF.Exp, scale=-1.0, deps=[t_rc0])
            for b in pg(ps1, 1024, 2048):
                b.did_read(t_rc)
            g_rsum.did_write(t_rc)
            bcs = []
            for c in range(2):
                bcs.append(mm(ps_sc[0][:, c * 512:(c + 1) * 512], onesr_f[:],
                              rsum_row[0:1, c * 512:(c + 1) * 512], True, True,
                              [t_rc] + ([exp_par0] + pgw(ps0, 0, 1024)
                                        if c == 0 else [])))
                g_rsum.did_read(bcs[-1])
            for b in pg(ps0, 0, 1024):
                b.did_write(bcs[-1])
            t_bc2 = r.op("ve", lambda e: e.tensor_copy(bcast_sb, ps_sc[0]),
                         bcs + g_bcsb.wr(), is_async=True)
            for b in pg(ps0, 0, 1024):
                b.did_read(t_bc2)
            g_bcsb.did_write(t_bc2)
            t_ev = tt(ah_b[:, qh2 * 1024:(qh2 + 1) * 1024], ps_av, bcast_sb, ALU.mult,
                      av_last + [t_bc2] + (g_ah.wr() if qh2 == 0 else []))
            r.ops[t_ev]["async"] = True
            g_bcsb.did_read(t_ev)
            for b in pg(ps1, 0, 1024):
                b.did_read(t_ev)
            ah_parts.append(t_ev)
        g_ah.writer = ah_parts
        g_ah.readers = {}
        t_st = dma("sd", attnh_d[h], ah_b[:], deps=ah_parts, lane="Lah")
        g_ah.did_read(t_st)
        attn_store_toks[h] = t_st

    while mj < len(mean_jobs):
        mean_jobs[mj]()
        mj += 1
    w1_ready = finish_wmean(0, W_S1, W_C1, W_NC1, W_I1)
    w2_ready = finish_wmean(1, W_S2, W_C2, W_NC2, W_I2)
    if wq_dbg is not None:
        dma("sd", wq_dbg[:], stp[:, 32:40], deps=w1_ready + w2_ready, lane="Ldbg")

    wo_stage_c = AR_B[:, 4096:6144]
    wo_stage_s = AR_B[:, 6144:8192]
    ap_stage = [AR_B[:, 8192:10240], AR_B[:, 10240:12288]]
    g_wost = G(*(g_qh.wr() + g_kh.wr()))
    g_apst = [G(*(g_vh.wr())), G(*(g_vtm.wr()))]
    g_wo = G(*(g_A.wr()))
    g_arb = G(*(g_stage.wr() + [t for i in range(6) for t in g_wslot[i].wr()]))
    t_bo_hilo = make_hilo(hilo_bo, din["bo"], D, Cf[0:1, 0:2048],
                          AR_C[0:1, 4096:6144], g_arb.wr())
    g_arb.did_read(t_bo_hilo)

    wo_toks = []
    rb_toks = []
    g_wost_rb = g_wost
    for fc in range(ND):
        t_c = dma("gd", wo_stage_c, din["co"][fc * 128:(fc + 1) * 128, :],
                  deps=g_wost.wr(), lane="Lwoc")
        t_s = dma("gd", wo_stage_s, din["so"][fc * 128:(fc + 1) * 128, :],
                  deps=g_wost.wr(), lane="Lwos")
        g_wost.writer = [t_c, t_s]
        g_wost.readers = {}
        t_q = tt(AR_A[:, fc * 2048:(fc + 1) * 2048], wo_stage_c, wo_stage_s,
                 ALU.mult, [t_c, t_s] + (g_wo.wr() if fc == 0 else []))
        g_wost.did_read(t_q)
        if fc == 0:
            g_wo.did_write(t_q)
        else:
            g_wo.writer.append(t_q)
        wo_toks.append(t_q)
        rb_toks.append(dma("sy", AR_C[:, fc * 2048:(fc + 1) * 2048], attnh_d[fc],
                           deps=[attn_store_toks[fc], t_bo_hilo] +
                           (g_arb.wr() if fc == 0 else []), lane="Larb"))
        if fc == 0:
            g_arb.did_write(rb_toks[-1])
        else:
            g_arb.writer.append(rb_toks[-1])

    ps_full = [ps0, ps1]
    aproj_store = {}
    for tti in range(NT):
        par = tti % 2
        psf = ps_full[par]
        banks = pg(psf, 0, 2048)
        mms = []
        for fc in range(ND):
            for c in range(4):
                dep0 = []
                if c == 0:
                    dep0 = [rb_toks[-1], wo_toks[fc]]
                    if fc == 0:
                        dep0 += [bb for b in banks for bb in b.wr()]
                mms.append(mm(psf[:, c * 512:(c + 1) * 512],
                              AR_C[:, fc * 2048 + tti * 128:fc * 2048 + tti * 128 + 128],
                              AR_A[:, fc * 2048 + c * 512:fc * 2048 + c * 512 + 512],
                              fc == 0, False, dep0))
            g_arb.did_read(mms[-1])
            g_wo.did_read(mms[-1])
        for c in range(4):
            mms.append(mm(psf[:, c * 512:(c + 1) * 512], ones2[:],
                          hilo_bo[:, c * 512:(c + 1) * 512], False, True,
                          [t_bo_hilo, tok_c["ones2"]] if c == 0 else []))
        for b in banks:
            b.did_write(mms[-1])
        t_ev = r.op("sc", lambda e, par=par, psf=psf: e.activation(
            ap_stage[par], psf[:], AF.Identity), [mms[-1]] + g_apst[par].wr())
        for b in banks:
            b.did_read(t_ev)
        g_apst[par].did_write(t_ev)
        t_st2 = dma("sd", aproj_d[tti], ap_stage[par], deps=[t_ev],
                    lane="Lap%d" % par)
        g_apst[par].did_read(t_st2)
        aproj_store[tti] = t_st2

    w1_stage = [Cf[:, 0:8192], Cf[:, 8192:16384]]
    w1q = [AR_A[:, 0:8192], AR_A[:, 8192:16384]]
    Af = AR_A[:].bitcast(F32)
    s_stage = [Af[:, 8192 + i * 512:8192 + (i + 1) * 512] for i in range(4)]
    t1_stage = [Af[:, 10240 + i * 512:10240 + (i + 1) * 512] for i in range(2)]
    b1bc = [Af[:, 11264:11776], Af[:, 11776:12288]]
    g_w1st = [G(*(g_arb.wr())), G(*(g_arb.wr()))]
    g_w1q = [G(*(g_wo.wr())), G(*(g_wo.wr()))]
    g_sst = [G(*(g_wo.wr())) for _ in range(4)]
    g_t1 = [G(*(g_wo.wr())), G(*(g_wo.wr()))]
    g_b1bc = [G(*(g_wo.wr())), G(*(g_wo.wr()))]
    g_scv1 = G()
    g_Brl = G(*(g_qh.wr() + g_kh.wr() + g_vh.wr() + g_vtm.wr() + g_sqr.wr() +
                g_ah.wr() + g_pt[0].wr() + g_pt[1].wr() + g_wost.wr() +
                g_apst[0].wr() + g_apst[1].wr() + g_mst[0].wr() + g_mst[1].wr() + g_rown.wr() +
                g_nrmp.wr()))
    rl_toks = {}
    for tti in range(NT):
        rl_toks[tti] = dma("sy", AR_B[:, tti * 2048:(tti + 1) * 2048], xq1_d[tti],
                           deps=[xq1_store[tti]] + (g_Brl.wr() if tti < 2 else []),
                           lane="Lrl")
        if tti == 0:
            g_Brl.did_write(rl_toks[tti])
        else:
            g_Brl.writer.append(rl_toks[tti])
    scv_toks = []
    for t_ in range(NT):
        scv_toks.append(tt(scv1[:, t_:t_ + 1], invq1[:, t_:t_ + 1], col(W_I1),
                           ALU.mult, g_invq1.wr() + w1_ready + g_scv1.wr()))
        g_scv1.writer.append(scv_toks[-1])
    g_invq1.did_read(scv_toks[-1])

    ps_fp = [ps0[:, 0:512], ps0[:, 512:1024]]
    ps_fb = [pg(ps0, 0, 512), pg(ps0, 512, 1024)]
    s_store = {}
    for sbi in range(16):
        par = sbi % 2
        t_ld = dma("sy", w1_stage[par][:].rearrange("p (dc f) -> p dc f", f=512),
                   din["w1t"][:, sbi * 512:(sbi + 1) * 512]
                   .rearrange("(dc p) f -> p dc f", p=128),
                   deps=g_w1st[par].wr(), lane="Lw1%d" % par)
        g_w1st[par].did_write(t_ld)
        t_q1 = r.op("ve", lambda e, par=par: e.tensor_scalar(
            w1_stage[par], w1_stage[par], col(W_C1), col(W_NC1), ALU.min, ALU.max),
            [t_ld] + w1_ready)
        t_q2 = r.op("ve", lambda e, par=par: e.tensor_scalar(
            w1_stage[par], w1_stage[par], col(W_S1), MAGIC, ALU.mult, ALU.add), [t_q1])
        t_q3 = r.op("ve", lambda e, par=par: e.tensor_scalar(
            w1q[par], w1_stage[par], -MAGIC, None, ALU.add, ALU.bypass),
            [t_q2] + g_w1q[par].wr())
        g_w1st[par].did_write(t_q2)
        g_w1st[par].did_read(t_q3)
        g_w1q[par].did_write(t_q3)
        bpar = sbi % 2
        t_bb = dma("sy", b1bc[bpar],
                   din["b1"][sbi * 512:(sbi + 1) * 512]
                   .rearrange("(a n) -> a n", a=1).partition_broadcast(128).squeeze(1),
                   deps=g_b1bc[bpar].wr(), lane="Lb1%d" % bpar)
        g_b1bc[bpar].did_write(t_bb)
        for tti in range(NT):
            fpar = tti % 2
            mms = []
            for dc in range(ND):
                mms.append(mm(ps_fp[fpar],
                              AR_B[:, tti * 2048 + dc * 128:tti * 2048 + dc * 128 + 128],
                              w1q[par][:, dc * 512:(dc + 1) * 512],
                              dc == 0, dc == ND - 1,
                              ([t_q3, rl_toks[NT - 1]] +
                               [bb for b in ps_fb[fpar] for bb in b.wr()])
                              if dc == 0 else []))
            g_Brl.did_read(mms[-1])
            g_w1q[par].did_read(mms[-1])
            for b in ps_fb[fpar]:
                b.did_write(mms[-1])
            t_sc = r.op("ve", lambda e, fpar=fpar, tti=tti, bpar=bpar:
                        e.scalar_tensor_tensor(t1_stage[fpar], ps_fp[fpar],
                                               scv1[:, tti:tti + 1], b1bc[bpar],
                                               ALU.mult, ALU.add),
                        [mms[-1], t_bb, scv_toks[tti]] + g_t1[fpar].wr(),
                        is_async=True)
            g_b1bc[bpar].did_read(t_sc)
            g_scv1.did_read(t_sc)
            for b in ps_fb[fpar]:
                b.did_read(t_sc)
            g_t1[fpar].did_write(t_sc)
            sslot = tti % 4
            t_ev = r.op("sc", lambda e, fpar=fpar, sslot=sslot: e.activation(
                s_stage[sslot], t1_stage[fpar], AF.Silu), [t_sc] + g_sst[sslot].wr())
            g_t1[fpar].did_read(t_ev)
            g_sst[sslot].did_write(t_ev)
            t_st3 = dma("sd", s_d[tti][:, sbi * 512:(sbi + 1) * 512], s_stage[sslot],
                        deps=[t_ev], lane="Lss%d" % sslot)
            g_sst[sslot].did_read(t_st3)
            s_store[(tti, sbi)] = t_st3

    g_bc_t = Cf[:, 0:8192]
    b_bc_t = Cf[:, 8192:16384]
    s_rbs = [Af[:, 0:8192], Af[:, 8192:16384]]
    xq2_tm = AR_B[:, 0:8192]
    xq2_fm = AR_B[:, 8192:16384]
    zbuf = Bf[:, 8192:16384]
    ps_tp2 = [ps1[:, 0:512].bitcast(BF16), ps1[:, 512:1024].bitcast(BF16)]

    g_gb = G(*(g_w1st[0].wr() + g_w1st[1].wr()))
    seed_srb = (g_w1q[0].wr() + g_w1q[1].wr() + g_sst[0].wr() + g_sst[1].wr() +
                g_sst[2].wr() + g_sst[3].wr() + g_t1[0].wr() + g_t1[1].wr() +
                g_b1bc[0].wr() + g_b1bc[1].wr())
    g_srbs = [G(*seed_srb), G(*seed_srb)]
    g_z = G(*(g_Brl.wr()))
    g_xq2tm = G(*(g_Brl.wr()))
    g_xq2fm = G(*(g_Brl.wr()))
    g_mstats = G()
    g_scv2 = G()

    t_gbc = dma("sy", g_bc_t[:], din["g_ln"][:].rearrange("(a n) -> a n", a=1)
                .partition_broadcast(128).squeeze(1), deps=g_gb.wr(), lane="Lgb")
    t_bbc = dma("sy", b_bc_t[:], din["bt_ln"][:].rearrange("(a n) -> a n", a=1)
                .partition_broadcast(128).squeeze(1), deps=g_gb.wr(), lane="Lgb")
    g_gb.writer = [t_gbc, t_bbc]
    g_gb.readers = {}

    MT = 80
    QC = 96
    HW = INNER // 2
    xq2_store = {}
    zb_h = [zbuf[:, 0:HW], zbuf[:, HW:INNER]]
    gb_h = [g_bc_t[:, 0:HW], g_bc_t[:, HW:INNER]]
    bb_h = [b_bc_t[:, 0:HW], b_bc_t[:, HW:INNER]]
    prev = {}
    for grp in range(4):
        for j in range(4):
            tti = grp * 4 + j
            pb = tti % 2
            s_rb = s_rbs[pb]
            g_srb = g_srbs[pb]
            t_gl, t_sq, t_zv, t_z3, t_zq, t_am, t_p1, t_p2 = (
                [None, None] for _ in range(8))
            t_rbh = [None, None]
            for hf in range(2):
                t_rbh[hf] = dma(
                    "sy", s_rb[:, hf * HW:(hf + 1) * HW],
                    s_d[tti][:, hf * HW:(hf + 1) * HW],
                    deps=(g_srb.wr() if hf == 0 else []) +
                    [s_store[(tti, sbi)] for sbi in range(hf * 8, hf * 8 + 8)],
                    lane="Lsrb%d%d" % (pb, hf))
                if hf == 0:
                    g_srb.writer = [t_rbh[0]]
                    g_srb.readers = {}
                else:
                    g_srb.writer.append(t_rbh[1])
            for hf in range(2):
                t_gl[hf] = r.op("sc", lambda e, s=s_rb, hf=hf: e.activation(
                    s[:, hf * HW:(hf + 1) * HW], s[:, hf * HW:(hf + 1) * HW],
                    AF.Gelu, accum_out=col(QC + hf)),
                    [t_rbh[hf]] + ([prev.get("mu0")]
                                   if prev.get("mu0") else []))
                g_srb.writer.append(t_gl[hf])
            for hf in range(2):
                t_sq[hf] = r.op("sc", lambda e, s=s_rb, hf=hf: e.activation(
                    xq2_tm[:, hf * HW:(hf + 1) * HW], s[:, hf * HW:(hf + 1) * HW],
                    AF.Square, accum_out=col(QC + 2 + hf)),
                    [t_gl[hf]] + ([prev.get("e20")] if prev.get("e20") else []) +
                    prev.get("tpr%d" % hf, []) + g_xq2tm.wr())
                g_srb.did_read(t_sq[hf])
                g_xq2tm.did_write(t_sq[hf])
            t_mu0 = tt(col(MT), col(QC), col(QC + 1), ALU.add, [t_gl[0], t_gl[1]])
            t_e20 = tt(col(MT + 1), col(QC + 2), col(QC + 3), ALU.add,
                       [t_sq[0], t_sq[1]])
            prev["mu0"] = t_mu0
            prev["e20"] = t_e20
            t_mu = ts(col(MT), col(MT), 1.0 / INNER, None, ALU.mult, ALU.bypass,
                      [t_mu0])
            t_e2 = ts(col(MT + 1), col(MT + 1), 1.0 / INNER, None, ALU.mult,
                      ALU.bypass, [t_e20])
            t_mm2 = tt(col(MT + 2), col(MT), col(MT), ALU.mult, [t_mu])
            t_ag = tt(col(MT + 1), col(MT + 1), col(MT + 2), ALU.subtract,
                      [t_e2, t_mm2])
            t_u = ts(col(MT + 4), stp[:, MT + 1:MT + 2], 1.0, EPS, ALU.mult,
                     ALU.add, [t_ag])
            t_l = act(col(MT + 4), col(MT + 4), AF.Ln, deps=[t_u])
            t_r = act(col(MT + 5), col(MT + 4), AF.Exp, scale=-0.5, deps=[t_l])
            t_bx = ts(col(MT + 6), stp[:, MT:MT + 1], col(MT + 5), -1.0,
                      ALU.mult, ALU.mult, [t_r])
            for hf in range(2):
                t_zv[hf] = r.op("sc", lambda e, s=s_rb, hf=hf: e.activation(
                    zb_h[hf], s[:, hf * HW:(hf + 1) * HW], AF.Identity,
                    bias=col(MT + 6), scale=col(MT + 5)),
                    [t_bx, t_gl[hf]] + prev.get("zfree%d" % hf, g_z.wr()))
                g_srb.did_read(t_zv[hf])
                t_z2 = tt(zb_h[hf], zb_h[hf], gb_h[hf], ALU.mult,
                          [t_zv[hf]] + g_gb.rd())
                g_gb.did_read(t_z2)
                t_z3[hf] = tt(zb_h[hf], zb_h[hf], bb_h[hf], ALU.add,
                              [t_z2] + g_gb.rd())
                g_gb.did_read(t_z3[hf])
            for hf in range(2):
                t_zq[hf] = r.op("sc", lambda e, hf=hf: e.activation(
                    xq2_tm[:, hf * HW:(hf + 1) * HW], zb_h[hf], AF.Square,
                    accum_out=col(QC + 4 + hf)),
                    [t_z3[hf]] + ([prev.get("zqs")] if prev.get("zqs") else []))
                t_am[hf] = red(col(QC + 6 + hf), zb_h[hf], ALU.max,
                               [t_z3[hf]] + ([prev.get("am2")]
                                             if prev.get("am2") else []),
                               absv=True)
            t_zqs = tt(col(MT + 8), col(QC + 4), col(QC + 5), ALU.add,
                       [t_zq[0], t_zq[1]])
            t_am2 = tt(col(MT + 7), col(QC + 6), col(QC + 7), ALU.max,
                       [t_am[0], t_am[1]])
            prev["zqs"] = t_zqs
            prev["am2"] = t_am2
            g_mstats.writer = [t_zqs, t_am2]
            g_mstats.readers = {}
            t_u2 = ts(col(MT + 8), col(MT + 8), 1.0 / INNER, EPS, ALU.mult,
                      ALU.add, [t_zqs])
            t_l2 = act(col(MT + 8), col(MT + 8), AF.Ln, deps=[t_u2])
            t_r2 = act(col(MT + 9), col(MT + 8), AF.Exp, scale=-0.5, deps=[t_l2])
            t_mm_ = tt(col(MT + 10), col(MT + 7), col(MT + 9), ALU.mult,
                       [t_am2, t_r2])
            t_cl = ts(col(MT + 10), col(MT + 10), 1e-5, None, ALU.max,
                      ALU.bypass, [t_mm_])
            t_rc = act(col(MT + 11), col(MT + 10), AF.Ln, deps=[t_cl])
            t_s2 = act(col(MT + 11), col(MT + 11), AF.Exp, scale=-1.0,
                       bias=4.844187086458591, deps=[t_rc])
            t_c2 = tt(col(MT + 12), col(MT + 9), col(MT + 11), ALU.mult, [t_s2])
            t_i2 = ts(col(MT + 13), col(MT + 10), 1.0 / 127.0, None, ALU.mult,
                      ALU.bypass, [t_cl])
            if mid_dbg is not None:
                t_mdbg = dma("sd", mid_dbg[tti], stp[:, 80:96],
                             deps=[t_c2, t_i2, t_r, t_bx], lane="Ldbg")
                g_mstats.readers[("mdbg", tti)] = t_mdbg
            for hf in range(2):
                t_p1[hf] = ts(zb_h[hf], zb_h[hf], col(MT + 12), MAGIC, ALU.mult,
                              ALU.add, [t_c2, t_am[hf], t_zq[hf]])
                t_p2[hf] = r.op("sc", lambda e, hf=hf: e.activation(
                    xq2_tm[:, hf * HW:(hf + 1) * HW], zb_h[hf], AF.Identity,
                    bias=-MAGIC), [t_p1[hf]] + g_xq2tm.wr())
                g_xq2tm.did_write(t_p2[hf])
                prev["zfree%d" % hf] = [t_p2[hf]]
            t_iv2 = ts(invq2[:, tti:tti + 1], col(MT + 13), 1.0, None,
                       ALU.mult, ALU.bypass, [t_i2] + g_invq2.wr())
            g_invq2.writer.append(t_iv2)
            g_mstats.readers[("iv2", tti)] = t_iv2
            prev["iv2"] = t_iv2
            ev_toks = []
            tpr = [[], []]
            for b2 in range(8):
                par2 = b2 % 2
                hf = b2 // 4
                bank = pg(ps1, par2 * 512, (par2 + 1) * 512)
                tbs = []
                for u in range(8):
                    ic = b2 * 8 + u
                    tbs.append(tp(ps_tp2[par2][:, u * 128:(u + 1) * 128],
                                  xq2_tm[:, ic * 128:(ic + 1) * 128],
                                  [t_p2[hf]] + ([bb for b in bank for bb in b.wr()]
                                                if u == 0 else [])))
                    g_xq2tm.did_read(tbs[-1])
                for b in bank:
                    b.did_write(tbs[-1])
                tpr[hf].append(tbs[-1])
                t_ev = r.op("ve", lambda e, b2=b2, par2=par2: e.tensor_copy(
                    xq2_fm[:, b2 * 1024:(b2 + 1) * 1024], ps_tp2[par2]),
                    tbs + (g_xq2fm.wr() if b2 == 0 else []), is_async=True)
                for b in bank:
                    b.did_read(t_ev)
                ev_toks.append(t_ev)
            prev["tpr0"] = tpr[0]
            prev["tpr1"] = tpr[1]
            g_xq2fm.writer = ev_toks
            g_xq2fm.readers = {}
            t_st4 = dma("sd", xq2_d[tti], xq2_fm[:], deps=ev_toks, lane="Lq2")
            g_xq2fm.did_read(t_st4)
            xq2_store[tti] = t_st4

    g_z.writer = prev["zfree0"] + prev["zfree1"]
    g_z.readers = {}

    w2q_ar = [AR_A, AR_C]
    xq2_rb = [AR_B[:, 0:8192], AR_B[:, 8192:16384]]
    ap_rb = [AR_B[:, 16384:16896], AR_B[:, 16896:17408]]
    x_rb = [Bf[:, 8704:9216], Bf[:, 9216:9728]]
    t1_st = [Bf[:, 9728:10240], Bf[:, 10240:10752]]
    out_stg = [Bf[:, 10752:11264], Bf[:, 11264:11776]]
    b2bc = Bf[:, 11776:12288]
    w2_stage = Bf[:, 12288:16384]

    g_w2st = G(*(g_z.wr() + g_xq2tm.wr()))
    g_w2q = [G(*(g_srbs[0].wr() + g_srbs[1].wr())),
             G(*(g_gb.wr()))]
    g_xq2rb = [G(*(g_xq2tm.wr())), G(*(g_xq2fm.wr()))]
    g_eio = [G(*(g_z.wr())), G(*(g_z.wr()))]
    g_b2bc = G(*(g_z.wr()))
    scv2_toks = []
    for t_ in range(NT):
        scv2_toks.append(tt(scv2[:, t_:t_ + 1], invq2[:, t_:t_ + 1], col(W_I2),
                            ALU.mult, g_invq2.wr() + w2_ready + g_scv2.wr()))
        g_scv2.writer.append(scv2_toks[-1])
    g_invq2.did_read(scv2_toks[-1])

    w2q_toks = {}

    def stage_w2_slab(oq, bt):
        ar = w2q_ar[oq % 2]
        gq = g_w2q[oq % 2]
        t_ld = dma("sy", w2_stage[:].rearrange("p (ic f) -> p ic f", f=512),
                   din["w2t"][bt * 1024:(bt + 1) * 1024, oq * 512:(oq + 1) * 512]
                   .rearrange("(ic p) f -> p ic f", p=128),
                   deps=g_w2st.wr(), lane="Lw2s")
        g_w2st.did_write(t_ld)
        t_q1 = r.op("ve", lambda e: e.tensor_scalar(
            w2_stage, w2_stage, col(W_C2), col(W_NC2), ALU.min, ALU.max),
            [t_ld] + w2_ready)
        t_q2 = r.op("sc", lambda e: e.activation(
            w2_stage, w2_stage, AF.Identity, scale=col(W_S2), bias=MAGIC), [t_q1])
        t_q3 = r.op("sc", lambda e, bt=bt, ar=ar: e.activation(
            ar[:, bt * 4096:(bt + 1) * 4096], w2_stage, AF.Identity, bias=-MAGIC),
            [t_q2] + (gq.wr() if bt == 0 else []))
        g_w2st.did_write(t_q2)
        g_w2st.did_read(t_q3)
        if bt == 0:
            gq.did_write(t_q3)
        else:
            gq.writer.append(t_q3)
        w2q_toks.setdefault(oq, []).append(t_q3)

    def stage_w2(oq):
        for bt in range(8):
            stage_w2_slab(oq, bt)

    stage_w2(0)
    out_stores = []
    for oq in range(4):
        ar = w2q_ar[oq % 2]
        gq = g_w2q[oq % 2]
        qt = w2q_toks[oq]
        t_b2 = dma("sy", b2bc, din["b2"][oq * 512:(oq + 1) * 512]
                   .rearrange("(a n) -> a n", a=1).partition_broadcast(128).squeeze(1),
                   deps=g_b2bc.wr(), lane="Lb2")
        g_b2bc.did_write(t_b2)
        for tti in range(NT):
            par = tti % 2
            t_x2 = dma("sy", xq2_rb[par], xq2_d[tti],
                       deps=g_xq2rb[par].wr() + [xq2_store[tti]],
                       lane="Lx2%d" % par)
            g_xq2rb[par].did_write(t_x2)
            t_ap = dma("sy", ap_rb[par], aproj_d[tti][:, oq * 512:(oq + 1) * 512],
                       deps=g_eio[par].wr() + [aproj_store[tti]],
                       lane="Lapr%d" % par)
            t_xr = dma("sy", x_rb[par], xdv[tti * 128:(tti + 1) * 128,
                                            oq * 512:(oq + 1) * 512],
                       deps=g_eio[par].wr(), lane="Lxr%d" % par)
            mms = []
            for ic in range(NI):
                mms.append(mm(ps_fp[par], xq2_rb[par][:, ic * 128:(ic + 1) * 128],
                              ar[:, ic * 512:(ic + 1) * 512],
                              ic == 0, ic == NI - 1,
                              ([t_x2, qt[ic // 8]] +
                               [bb for b in ps_fb[par] for bb in b.wr()])
                              if ic == 0 else ([qt[ic // 8]] if ic % 8 == 0 else [])))
            g_xq2rb[par].did_read(mms[-1])
            gq.did_read(mms[-1])
            for b in ps_fb[par]:
                b.did_write(mms[-1])
            t_t1 = r.op("ve", lambda e, par=par, tti=tti:
                        e.scalar_tensor_tensor(t1_st[par], ps_fp[par],
                                               scv2[:, tti:tti + 1], ap_rb[par],
                                               ALU.mult, ALU.add),
                        [mms[-1], t_ap, scv2_toks[tti]], is_async=True)
            g_scv2.did_read(t_t1)
            for b in ps_fb[par]:
                b.did_read(t_t1)
            t_o1 = tt(t1_st[par], t1_st[par], x_rb[par], ALU.add, [t_t1, t_xr])
            t_o = tt(out_stg[par], t1_st[par], b2bc, ALU.add, [t_o1, t_b2])
            g_b2bc.did_read(t_o)
            t_so = dma("sd", out_d[tti * 128:(tti + 1) * 128, oq * 512:(oq + 1) * 512],
                       out_stg[par], deps=[t_o], lane="Lo%d" % par)
            g_eio[par].did_write(t_so)
            out_stores.append(t_so)
            if tti % 2 == 0 and oq < 3:
                stage_w2_slab(oq + 1, tti // 2)

    r.op("ve", lambda e: e.engine_nop(), out_stores[-2:])

    import contextlib
    stack = contextlib.ExitStack()
    with nc.Block() as block:
        r.emit(nc, block, stack)
    return nc, stack


def host_inputs(inputs):
    f32 = lambda a: np.ascontiguousarray(np.asarray(a, dtype=np.float32))
    rep = lambda s: np.ascontiguousarray(np.repeat(f32(s), 32, axis=0))
    shared = {
        "cq": f32(inputs["L0_attn_wq_codes"]), "sq": rep(inputs["L0_attn_wq_scales"]),
        "ck": f32(inputs["L0_attn_wk_codes"]), "sk": rep(inputs["L0_attn_wk_scales"]),
        "cv": f32(inputs["L0_attn_wv_codes"]), "sv": rep(inputs["L0_attn_wv_scales"]),
        "co": f32(inputs["L0_attn_wo_codes"]), "so": rep(inputs["L0_attn_wo_scales"]),
        "bq": f32(inputs["L0_attn_bq"]), "bk": f32(inputs["L0_attn_bk"]),
        "bv": f32(inputs["L0_attn_bv"]), "bo": f32(inputs["L0_attn_bo"]),
        "w1t": np.ascontiguousarray(f32(inputs["L0_ffn_w1"]).T),
        "b1": f32(inputs["L0_ffn_b1"]),
        "g_ln": f32(inputs["L0_ffn_ln_g"]), "bt_ln": f32(inputs["L0_ffn_ln_b"]),
        "w2t": np.ascontiguousarray(f32(inputs["L0_ffn_w2"]).T),
        "b2": f32(inputs["L0_ffn_b2"]),
        "ident": np.eye(128).astype(ml_dtypes.bfloat16),
        "ones_col": np.ones((128, 1), dtype=ml_dtypes.bfloat16),
        "ones2": np.ones((2, 128), dtype=ml_dtypes.bfloat16),
        "onesr_f": np.ones((1, 128), dtype=np.float32),
        "magicv": np.full((128, 1), MAGIC, dtype=np.float32),
        "ln127v": np.full((128, 1), 4.844187086458591, dtype=np.float32),
        "nmagicv": np.full((128, 1), -MAGIC, dtype=np.float32),
        "ones_col_f": np.ones((128, 1), dtype=np.float32),
    }
    x = f32(inputs["L0_attn_x"])
    return x, shared


def kernel(**inputs):
    x, shared = host_inputs(inputs)
    nc, stack = build_program()
    in_maps = [dict(shared, x=np.ascontiguousarray(x[c])) for c in range(8)]
    res = run_bass_kernel_spmd(nc, in_maps, core_ids=list(range(8)))
    out = np.stack([np.asarray(res.results[c]["out"]) for c in range(8)], axis=0)
    return out.astype(np.float32)
